# version 01
# baseline (speedup 1.0000x reference)
import contextlib
from concourse.bass import IndirectOffsetOnAxis
from concourse.bass_utils import run_bass_kernel_spmd
"""Bass kit: hazard-tracked raw-Bass program builder.

Discipline:
  - one semaphore per compute engine (inc 1, only by its own engine's ops)
  - one semaphore per DMA queue (inc 16 on completion)
  - every op declares reads=[Buf...] / writes=[Buf...]; the kit emits the
    minimal wait_ge set (same-engine program order is trusted; cross-engine
    RAW/WAR/WAW edges become semaphore waits, deduped by watermark)
  - dependency graph is acyclic by construction (deps only on earlier tokens)
"""
import numpy as np
import concourse.bass as bass
import concourse.mybir as mybir

F32 = mybir.dt.float32
BF16 = mybir.dt.bfloat16
I32 = mybir.dt.int32
U32 = mybir.dt.uint32
AF = mybir.ActivationFunctionType
ALU = mybir.AluOpType

PE, ACT, DVE, POOL, SP = "pe", "act", "dve", "pool", "sp"
COMPUTE_ENGINES = (PE, ACT, DVE, POOL, SP)
DMA_QUEUES = {SP: "dsp", POOL: "dpool", ACT: "dact"}


class Buf:
    __slots__ = ("name", "writer", "readers")

    def __init__(self, name=""):
        self.name = name
        self.writer = None
        self.readers = []


class Kit:
    def __init__(self, nc, ctx):
        self.nc = nc
        self.ctx = ctx
        self.eng = {PE: nc.tensor, ACT: nc.scalar, DVE: nc.vector,
                    POOL: nc.gpsimd, SP: nc.sync}
        self.sems = {}
        self.cnt = {}
        for name in [*COMPUTE_ENGINES, *DMA_QUEUES.values()]:
            self.sems[name] = ctx.enter_context(nc.semaphore("S_" + name))
            self.cnt[name] = 0
        self.water = {}
        self._const_tile = None
        self._const_cols = {}
        self._const_n = 0
        self._ident = None
        self._ind_dirty = []
        self._fence = None
        for s in self.sems.values():
            nc.gpsimd.sem_clear(s)
        nc.all_engine_barrier()

    def _wait(self, eng, token, force=False):
        if token is None:
            return
        semname, val = token
        if semname in ("dsp", "dpool", "dact"):
            val = self.cnt[semname]
        if semname == eng and eng != PE:
            force = True
        if semname == eng and not force:
            return
        key = (eng, semname)
        if self.water.get(key, 0) >= val:
            if not (semname == eng and force):
                return
        self.eng[eng].wait_ge(self.sems[semname], val)
        self.water[key] = val

    def _deps_for(self, reads, writes):
        deps = []
        for b in reads:
            if b.writer is not None:
                deps.append(b.writer)
        for b in writes:
            if b.writer is not None:
                deps.append(b.writer)
            deps.extend(b.readers)
        return deps

    def _commit(self, token, reads, writes):
        for b in reads:
            b.readers.append(token)
        for b in writes:
            b.writer = token
            b.readers = []

    def op(self, eng, fn, reads=(), writes=(), scalar_reads=(), deps=()):
        assert eng in COMPUTE_ENGINES
        for b in scalar_reads:
            if b.writer is not None:
                self._wait(eng, b.writer, force=True)
        for t in self._deps_for(reads, writes):
            self._wait(eng, t)
        for t in deps:
            self._wait(eng, t)
        inst = fn(self.eng[eng])
        self.cnt[eng] += 1
        inst.then_inc(self.sems[eng], 1)
        token = (eng, self.cnt[eng])
        self._commit(token, list(reads) + list(scalar_reads), writes)
        return token

    def dma(self, queue, out, in_, reads=(), writes=(), deps=(), **kw):
        dq = DMA_QUEUES[queue]
        for t in self._deps_for(reads, writes):
            self._wait(queue, t)
        for t in deps:
            self._wait(queue, t)
        inst = self.eng[queue].dma_start(out=out, in_=in_, **kw)
        self.cnt[dq] += 16
        inst.then_inc(self.sems[dq], 16)
        token = (dq, self.cnt[dq])
        self._commit(token, reads, writes)
        return token

    def indirect(self, out, out_offset, in_, in_offset, reads=(), writes=(),
                 deps=(), **kw):
        for t in self._deps_for(reads, writes):
            self._wait(POOL, t)
        for t in deps:
            self._wait(POOL, t)
        inst = self.nc.gpsimd.indirect_dma_start(out, out_offset, in_, in_offset, **kw)
        self.cnt["dpool"] += 16
        inst.then_inc(self.sems["dpool"], 16)
        token = ("dpool", self.cnt["dpool"])
        self._commit(token, reads, writes)
        self._ind_dirty.extend(writes)
        return token

    def fence_indirect(self):
        if not self._ind_dirty:
            return None
        if self._fence is None:
            ft = self.ctx.enter_context(
                self.nc.sbuf_tensor("kit_fence", [1, 16], F32))
            fd = self.nc.dram_tensor("kit_fence_d", [1, 16], F32).ap()
            self._fence = (ft, fd)
        ft, fd = self._fence
        tok = self.dma(POOL, ft[:], fd)
        for b in self._ind_dirty:
            if b.writer is not None and b.writer[0] == "dpool":
                b.writer = tok
        self._ind_dirty = []
        return tok

    def collective(self, kind, op_, ins_aps, outs_aps, reads=(), writes=(), deps=()):
        for b in reads:
            if b.writer is not None:
                self._wait(POOL, b.writer)
        for t in self._deps_for(reads, writes):
            self._wait(POOL, t)
        for t in deps:
            self._wait(POOL, t)
        inst = self.nc.gpsimd.collective_compute(
            kind, op_, replica_groups=[[0, 1, 2, 3, 4, 5, 6, 7]],
            ins=[a.opt() for a in ins_aps], outs=[a.opt() for a in outs_aps])
        self.cnt["dpool"] += 16
        inst.then_inc(self.sems["dpool"], 16)
        token = ("dpool", self.cnt["dpool"])
        self._commit(token, reads, writes)
        return token

    def barrier(self):
        for eng in COMPUTE_ENGINES:
            for semname, val in self.cnt.items():
                if val > 0 and semname != eng:
                    key = (eng, semname)
                    if self.water.get(key, 0) < val:
                        self.eng[eng].wait_ge(self.sems[semname], val)
                        self.water[key] = val

    def wait_token(self, eng, token):
        self._wait(eng, token)

    def c(self, value):
        if self._const_tile is None:
            self._const_tile = self.ctx.enter_context(
                self.nc.sbuf_tensor("kit_consts", [128, 64], F32))
        v = float(value)
        if v not in self._const_cols:
            i = self._const_n
            assert i < 64
            self._const_n += 1
            b = Buf(f"const_{v}")
            ap = self._const_tile[:, i:i + 1]
            self.op(DVE, lambda e: e.memset(ap, v), writes=[b])
            self._const_cols[v] = (ap, b)
        return self._const_cols[v]

    def c_rows(self, value, rows):
        ap, b = self.c(value)
        i = None
        for vv, (a2, b2) in self._const_cols.items():
            if b2 is b:
                i = list(self._const_cols.keys()).index(vv)
        v = float(value)
        idx = list(self._const_cols.keys()).index(v)
        return self._const_tile[0:rows, idx:idx + 1], b

    def ident(self):
        if self._ident is None:
            t = self.ctx.enter_context(
                self.nc.sbuf_tensor("kit_ident", [128, 128], BF16))
            b = Buf("ident")
            self.op(DVE, lambda e: e.memset(t[:], 1.0), writes=[b])
            self.op(POOL, lambda e: e.affine_select(
                t[:], t[:], [[-1, 128]], ALU.is_equal, 0.0,
                base=0, channel_multiplier=1), reads=[b], writes=[b])
            self._ident = (t, b)
        return self._ident

    def identf(self):
        if getattr(self, "_identf", None) is None:
            t = self.ctx.enter_context(
                self.nc.sbuf_tensor("kit_identf", [128, 128], F32))
            b = Buf("identf")
            self.op(DVE, lambda e: e.memset(t[:], 1.0), writes=[b])
            self.op(POOL, lambda e: e.affine_select(
                t[:], t[:], [[-1, 128]], ALU.is_equal, 0.0,
                base=0, channel_multiplier=1), reads=[b], writes=[b])
            self._identf = (t, b)
        return self._identf

    def transpose(self, out_ps, in_sb, reads, writes):
        if in_sb.dtype == F32:
            iap, ib = self.identf()
        else:
            iap, ib = self.ident()
        p = in_sb.shape[0]
        return self.op(PE, lambda e: e.transpose(out_ps, in_sb, iap[0:p, 0:p]),
                       reads=list(reads) + [ib], writes=writes)

    def sb(self, name, shape, dtype):
        t = self.ctx.enter_context(self.nc.sbuf_tensor(name, shape, dtype))
        return t, Buf(name)

    def ps(self, name, shape, dtype=F32):
        t = self.ctx.enter_context(self.nc.psum_tensor(name, shape, dtype))
        return t, Buf(name)


"""Kernel emitters for mk_macaron_d2. Development module; final kernel.py inlines this.

Layout conventions:
  - residual x per core: HBM token-major [4096, 1024] f32
  - "tm" tile: [128 tokens, C] (tokens on partitions)
  - cm data: channels on partitions, tokens on free
  - weights W (in x out) in SBUF bf16 as [128, nkc*out]: K-chunk kc at cols
    [kc*out, (kc+1)*out)
"""

AX = mybir.AxisListType.X
T = 4096
D = 1024
NT = T // 128
KC = D // 128
DI = 2730
DH2 = 2 * DI
NIC = (DI + 127) // 128
DBG = None


def ceil_div(a, b):
    return (a + b - 1) // b


class Scratch:
    pass



def load_weight_cm(k, name, w_dram, ins, outs):
    nkc = ceil_div(ins, 128)
    t, b = k.sb(name, [128, nkc * outs], BF16)
    for kc in range(nkc):
        lo = kc * 128
        hi = min(ins, lo + 128)
        k.dma(POOL, t[0:hi - lo, kc * outs:(kc + 1) * outs], w_dram[lo:hi, :],
              writes=[b])
    return t, b


def wslice(t, outs, kc, mo, mw, rows=128):
    return t[0:rows, kc * outs + mo: kc * outs + mo + mw]


def load_col_chunks(k, name, v_dram, n, dtype=F32):
    nch = ceil_div(n, 128)
    t, b = k.sb(name, [128, nch], dtype)
    if n % 128 != 0:
        k.op(DVE, lambda e: e.memset(t[:], 0.0), writes=[b])
    v2 = v_dram.rearrange("(c p) -> p c", p=128) if n % 128 == 0 else None
    if v2 is not None:
        k.dma(POOL, t[:, 0:nch], v2, writes=[b], allow_slow_non_contiguous=True)
    else:
        for c in range(nch):
            lo = c * 128
            hi = min(n, lo + 128)
            k.dma(POOL, t[0:hi - lo, c:c + 1], v_dram[lo:hi].rearrange("(p o) -> p o", o=1),
                  writes=[b], allow_slow_non_contiguous=True)
    return t, b


def bcast_row_tile(k, sc, name, v_dram, n, dtype=F32):
    row, rowb = k.sb(name + "_r", [1, n], F32)
    k.dma(SP, row[:], v_dram.rearrange("(o n) -> o n", o=1), writes=[rowb])
    t, b = k.sb(name, [128, n], dtype)
    pb, pbb = sc.pbc
    ones, onesb = sc.ones1f
    for c in range(ceil_div(n, 512)):
        lo = c * 512
        hi = min(n, lo + 512)
        k.op(PE, lambda e, lo=lo, hi=hi: e.matmul(
            pb[:, 0:hi - lo], ones[0:1, 0:128], row[0:1, lo:hi],
            start=True, stop=True), reads=[rowb, onesb], writes=[pbb])
        k.op(DVE, lambda e, lo=lo, hi=hi: e.tensor_copy(t[:, lo:hi],
                                                        pb[:, 0:hi - lo]),
             reads=[pbb], writes=[b])
    return t, b


def alloc_common(k, sc):
    sc.stat = k.sb("sc_stat", [128, 4], F32)
    sc.sq = k.sb("sc_sq", [128, D], F32)
    sc.pst = k.ps("sc_pst", [128, 128], BF16)
    sc.pbc = k.ps("sc_pbc", [128, 512], F32)
    o, ob = k.sb("sc_ones1f", [1, 128], F32)
    k.op(DVE, lambda e: e.memset(o[:], 1.0), writes=[ob])
    sc.ones1f = (o, ob)
    o2, o2b = k.sb("sc_ones1bf", [1, 128], BF16)
    k.op(DVE, lambda e: e.memset(o2[:], 1.0), writes=[o2b])
    sc.ones1bf = (o2, o2b)


def emit_ln_tm(k, sc, x_ap, x_buf, out_ap, out_buf, eps=1e-5, nch=D):
    st, stb = sc.stat
    sq, sqb = sc.sq
    k.op(ACT, lambda e: e.activation(sq[:, 0:nch], x_ap, AF.Square,
                                     accum_out=st[:, 1:2]),
         reads=[x_buf], writes=[sqb, stb])
    k.op(DVE, lambda e: e.reduce_sum(st[:, 0:1], x_ap, axis=AX),
         reads=[x_buf], writes=[stb])
    k.op(DVE, lambda e: e.tensor_scalar(st[:, 0:1], st[:, 0:1], 1.0 / nch, None,
                                        ALU.mult), scalar_reads=[stb], writes=[stb])
    k.op(DVE, lambda e: e.scalar_tensor_tensor(st[:, 2:3], st[:, 0:1], -1.0,
                                               st[:, 0:1], ALU.mult, ALU.mult),
         scalar_reads=[stb], writes=[stb])
    k.op(DVE, lambda e: e.tensor_scalar(st[:, 3:4], st[:, 1:2], 1.0 / nch,
                                        None, ALU.mult), scalar_reads=[stb], writes=[stb])
    k.op(DVE, lambda e: e.tensor_tensor(st[:, 3:4], st[:, 3:4], st[:, 2:3],
                                        ALU.add), scalar_reads=[stb], writes=[stb])
    c_eps, c_epsb = k.c(eps)
    k.op(ACT, lambda e: e.activation(st[:, 3:4], st[:, 3:4], AF.Ln, bias=c_eps),
         reads=[stb, c_epsb], scalar_reads=[stb], writes=[stb])
    k.op(ACT, lambda e: e.activation(st[:, 3:4], st[:, 3:4], AF.Exp, scale=-0.5),
         reads=[stb], writes=[stb])
    k.op(DVE, lambda e: e.tensor_scalar(out_ap, x_ap, st[:, 0:1], st[:, 3:4],
                                        ALU.subtract, ALU.mult),
         reads=[x_buf], scalar_reads=[stb], writes=[out_buf])


def emit_transpose_128(k, sc, src_ap, src_bufs, dst_ap, dst_bufs):
    pst, pstb = sc.pst
    p, n = src_ap.shape
    k.transpose(pst[0:n, 0:p], src_ap, reads=src_bufs, writes=[pstb])
    k.op(DVE, lambda e: e.tensor_copy(dst_ap, pst[0:n, 0:p]),
         reads=[pstb], writes=dst_bufs)



def alloc_lat_ffn(k, sc):
    sc.xs = k.sb("lf_xs", [128, D], F32)
    sc.xn = k.sb("lf_xn", [128, D], F32)
    sc.xnbf = k.sb("lf_xnbf", [128, D], BF16)
    sc.xncm = k.sb("lf_xncm", [128, KC * 128], BF16)
    sc.hg = k.ps("lf_hg", [128, 128], F32)
    sc.acm = k.sb("lf_acm", [128, NIC * 128], BF16)
    sc.gcm = k.sb("lf_gcm", [128, NIC * 128], BF16)
    sc.po = k.ps("lf_po", [128, 1024], F32)
    sc.xt = k.sb("lf_xt", [128, D], F32)


def emit_lat_ffn(k, sc, x_in, x_out, W):
    xs, xsb = sc.xs
    xn, xnb = sc.xn
    xnbf, xnbfb = sc.xnbf
    xncm, xncmb = sc.xncm
    hg, hgb = sc.hg
    acm, acmb = sc.acm
    gcm, gcmb = sc.gcm
    po, pob = sc.po
    xt, xtb = sc.xt
    w1t, w1b = W["w1"]
    b1t, b1bb = W["b1c"]
    w2t, w2b = W["w2"]
    b2t, b2bb = W["b2r"]
    gB, gBb = W["gammaB"]
    bB, bBb = W["betaB"]
    ones1, ones1b = sc.ones1bf

    for it in range(NT):
        r0 = it * 128
        k.dma(SP, xt[:], x_in[r0:r0 + 128, :], writes=[xtb])
        k.dma(SP, xs[:, 0:D // 2], x_in[r0:r0 + 128, 0:D // 2], writes=[xsb])
        if it == 0:
            k.op(DVE, lambda e: e.memset(xs[0:1, D // 2:D], 0.0), writes=[xsb])
            k.dma(SP, xs[1:128, D // 2:D], x_in[0:127, D // 2:D], writes=[xsb])
        else:
            k.dma(SP, xs[:, D // 2:D], x_in[r0 - 1:r0 + 127, D // 2:D],
                  writes=[xsb])
        emit_ln_tm(k, sc, xs[:], xsb, xn[:], xnb)
        k.op(DVE, lambda e: e.tensor_tensor(xn[:], xn[:], gB[:], ALU.mult),
             reads=[xnb, gBb], writes=[xnb])
        k.op(DVE, lambda e: e.tensor_tensor(xn[:], xn[:], bB[:], ALU.add),
             reads=[xnb, bBb], writes=[xnb])
        k.op(DVE, lambda e: e.tensor_copy(xnbf[:], xn[:]), reads=[xnb],
             writes=[xnbfb])
        if DBG is not None and it == 0:
            DBG[0].dma(SP, DBG[1], xn[:], reads=[xnb])
            DBG[0].dma(SP, DBG[2], xs[:], reads=[xsb])
        for kc in range(KC):
            emit_transpose_128(k, sc, xnbf[:, kc * 128:(kc + 1) * 128],
                               [xnbfb], xncm[:, kc * 128:(kc + 1) * 128],
                               [xncmb])
        for half, base in ((0, 0), (1, DI)):
            for i in range(NIC):
                m0 = base + i * 128
                mw = min(128, base + DI - m0)
                def mm(e, m0=m0, mw=mw):
                    last = None
                    for kc in range(KC):
                        last = e.matmul(hg[0:mw, 0:128],
                                        wslice(w1t, DH2, kc, m0, mw),
                                        xncm[:, kc * 128:(kc + 1) * 128],
                                        start=(kc == 0), stop=(kc == KC - 1))
                    return last
                k.op(PE, mm, reads=[w1b, xncmb], writes=[hgb])
                colb = half * NIC + i
                if half == 0:
                    k.op(DVE, lambda e, i=i, mw=mw, colb=colb: e.tensor_scalar(
                        acm[0:mw, i * 128:i * 128 + 128], hg[0:mw, 0:128],
                        b1t[0:mw, colb:colb + 1], None, ALU.add),
                        reads=[hgb], scalar_reads=[b1bb], writes=[acmb])
                else:
                    k.op(ACT, lambda e, i=i, mw=mw, colb=colb: e.activation(
                        gcm[0:mw, i * 128:i * 128 + 128], hg[0:mw, 0:128],
                        AF.Gelu, bias=b1t[0:mw, colb:colb + 1]),
                        reads=[hgb], scalar_reads=[b1bb], writes=[gcmb])
        k.op(DVE, lambda e: e.tensor_tensor(gcm[:], acm[:], gcm[:], ALU.mult),
             reads=[acmb, gcmb], writes=[gcmb])
        def mm2(e):
            last = None
            for no in (0, 512):
                for i in range(NIC):
                    mw = min(128, DI - i * 128)
                    last = e.matmul(po[:, no:no + 512],
                                    gcm[0:mw, i * 128:i * 128 + 128],
                                    wslice(w2t, 1024, i, no, 512, rows=mw),
                                    start=(i == 0), stop=False)
                last = e.matmul(po[:, no:no + 512], ones1[0:1, 0:128],
                                b2t[0:1, no:no + 512], start=False, stop=True)
            return last
        k.op(PE, mm2, reads=[gcmb, w2b, b2bb, ones1b], writes=[pob])
        k.op(DVE, lambda e: e.tensor_add(xt[:], xt[:], po[:, 0:1024]),
             reads=[pob], writes=[xtb])
        k.dma(SP, x_out[r0:r0 + 128, :], xt[:], reads=[xtb])


def load_lat_ffn_weights(k, sc, layer_tag, w1_d, b1_d, w2_d, b2_d, g_d, b_d):
    W = {}
    W["w1"] = load_weight_cm(k, f"w1_{layer_tag}", w1_d, D, DH2)
    W["w2"] = load_weight_cm(k, f"w2_{layer_tag}", w2_d, DI, 1024)
    b1t, b1b = k.sb(f"b1c_{layer_tag}", [128, 2 * NIC], F32)
    k.op(DVE, lambda e: e.memset(b1t[:], 0.0), writes=[b1b])
    for half in range(2):
        for i in range(NIC):
            lo = half * DI + i * 128
            hi = min(half * DI + DI, lo + 128)
            c = half * NIC + i
            k.dma(POOL, b1t[0:hi - lo, c:c + 1],
                  b1_d[lo:hi].rearrange("(p o) -> p o", o=1), writes=[b1b],
                  allow_slow_non_contiguous=True)
    W["b1c"] = (b1t, b1b)
    b2t, b2b = k.sb(f"b2r_{layer_tag}", [1, 1024], BF16)
    k.dma(POOL, b2t[:], b2_d.rearrange("(o n) -> o n", o=1), writes=[b2b])
    W["b2r"] = (b2t, b2b)
    W["gammaB"] = bcast_row_tile(k, sc, f"gB_{layer_tag}", g_d, D)
    W["betaB"] = bcast_row_tile(k, sc, f"bB_{layer_tag}", b_d, D)
    return W


NH = 16
HD = 64
KEYS = 384


def alloc_lat_attn(k, sc, ctx=None):
    sc.a_xs = k.sb("la_xs", [128, D], F32)
    sc.a_xsbf = k.sb("la_xsbf", [128, D], BF16)
    sc.a_xscm = k.sb("la_xscm", [128, KC * 128], BF16)
    sc.a_pqkv = k.ps("la_pqkv", [128, 512], F32)
    sc.a_qt = k.sb("la_qt", [128, D], F32)
    sc.a_kt = k.sb("la_kt", [128, D], F32)
    sc.a_sq = k.sb("la_sq", [128, D], F32)
    sc.a_ss = k.sb("la_ss", [128, NH], F32)
    sc.a_t1 = k.sb("la_t1", [128, D], F32)
    sc.a_t2 = k.sb("la_t2", [128, D], F32)
    sc.a_qbf = k.sb("la_qbf", [128, D], BF16)
    sc.a_kbf = k.sb("la_kbf", [128, D], BF16)
    sc.a_qcm = k.sb("la_qcm", [128, KC * 128], BF16)
    sc.a_kring = [k.sb(f"la_kring{j}", [128, KC * 128], BF16) for j in range(3)]
    sc.a_vring = [k.sb(f"la_vring{j}", [128, D], BF16) for j in range(3)]
    sc.a_psim = k.ps("la_psim", [128, KEYS], F32)
    sc.a_smask = k.sb("la_smask", [128, KEYS], F32)
    sc.a_nmax = k.sb("la_nmax", [128, 2], F32)
    sc.a_ebf = k.sb("la_ebf", [128, KEYS], BF16)
    sc.a_et = k.sb("la_et", [128, 3 * 128], BF16)
    sc.a_pav = k.ps("la_pav", [64, 128], F32)
    sc.a_acm = k.sb("la_acm", [128, KC * 128], BF16)
    sc.a_po = k.ps("la_po", [128, 512], F32)
    sc.a_xt = k.sb("la_xt", [128, D], F32)
    sc.a_tabs = k.sb("la_tabs", [128, 4 * HD], F32)


def emit_lat_attn(k, sc, x_in, x_out, W, masks):
    xs, xsb = sc.a_xs
    xsbf, xsbfb = sc.a_xsbf
    xscm, xscmb = sc.a_xscm
    pqkv, pqkvb = sc.a_pqkv
    qt, qtb = sc.a_qt
    kt, ktb = sc.a_kt
    sq, sqb = sc.a_sq
    ss, ssb = sc.a_ss
    t1, t1b = sc.a_t1
    t2, t2b = sc.a_t2
    qbf, qbfb = sc.a_qbf
    kbf, kbfb = sc.a_kbf
    qcm, qcmb = sc.a_qcm
    psim, psimb = sc.a_psim
    smask, smaskb = sc.a_smask
    nmax, nmaxb = sc.a_nmax
    ebf, ebfb = sc.a_ebf
    et, etb = sc.a_et
    pav, pavb = sc.a_pav
    acm, acmb = sc.a_acm
    po, pob = sc.a_po
    xt, xtb = sc.a_xt
    tabs, tabsb = sc.a_tabs
    wqkv, wqkvb = W["wqkv"]
    wo, wob = W["wo"]
    qsb_t, qsb_b = W["qsb"]
    ksb_t, ksb_b = W["ksb"]
    m_t, m_b = masks
    tabs_d = W["tabs_d"]

    for j in range(3):
        k.op(DVE, lambda e, j=j: e.memset(sc.a_kring[j][0][:], 0.0),
             writes=[sc.a_kring[j][1]])
        k.op(DVE, lambda e, j=j: e.memset(sc.a_vring[j][0][:], 0.0),
             writes=[sc.a_vring[j][1]])

    def bh(ap_):
        return ap_.rearrange("p (h d) -> p h d", h=NH)

    for it in range(NT):
        r0 = it * 128
        vt, vtb = sc.a_vring[it % 3]
        kcm, kcmb = sc.a_kring[it % 3]
        k.dma(SP, xt[:], x_in[r0:r0 + 128, :], writes=[xtb])
        k.dma(SP, xs[:, 0:D // 2], x_in[r0:r0 + 128, 0:D // 2], writes=[xsb])
        if it == 0:
            k.op(DVE, lambda e: e.memset(xs[0:1, D // 2:D], 0.0), writes=[xsb])
            k.dma(SP, xs[1:128, D // 2:D], x_in[0:127, D // 2:D], writes=[xsb])
        else:
            k.dma(SP, xs[:, D // 2:D], x_in[r0 - 1:r0 + 127, D // 2:D],
                  writes=[xsb])
        k.dma(SP, tabs[:], tabs_d[r0:r0 + 128, :], writes=[tabsb])
        k.op(DVE, lambda e: e.tensor_copy(xsbf[:], xs[:]), reads=[xsb],
             writes=[xsbfb])
        for kc in range(KC):
            emit_transpose_128(k, sc, xsbf[:, kc * 128:(kc + 1) * 128],
                               [xsbfb], xscm[:, kc * 128:(kc + 1) * 128],
                               [xscmb])
        for nc6 in range(6):
            def mmq(e, nc6=nc6):
                last = None
                for kc in range(KC):
                    last = e.matmul(pqkv[:, 0:512],
                                    xscm[:, kc * 128:(kc + 1) * 128],
                                    wslice(wqkv, 3072, kc, nc6 * 512, 512),
                                    start=(kc == 0), stop=(kc == KC - 1))
                return last
            k.op(PE, mmq, reads=[xscmb, wqkvb], writes=[pqkvb])
            if nc6 < 2:
                k.op(DVE, lambda e, nc6=nc6: e.tensor_copy(
                    qt[:, nc6 * 512:(nc6 + 1) * 512], pqkv[:, 0:512]),
                    reads=[pqkvb], writes=[qtb])
            elif nc6 < 4:
                k.op(DVE, lambda e, nc6=nc6: e.tensor_copy(
                    kt[:, (nc6 - 2) * 512:(nc6 - 1) * 512], pqkv[:, 0:512]),
                    reads=[pqkvb], writes=[ktb])
            else:
                k.op(DVE, lambda e, nc6=nc6: e.tensor_copy(
                    vt[:, (nc6 - 4) * 512:(nc6 - 3) * 512], pqkv[:, 0:512]),
                    reads=[pqkvb], writes=[vtb])
        for (src_t, src_b, dst_t, dst_b, scl, sclb, atab, btab) in (
                (qt, qtb, qbf, qbfb, qsb_t, qsb_b, 0, 1),
                (kt, ktb, kbf, kbfb, ksb_t, ksb_b, 2, 3)):
            k.op(ACT, lambda e, s=src_t: e.activation(sq[:], s[:], AF.Square),
                 reads=[src_b], writes=[sqb])
            k.op(DVE, lambda e: e.reduce_sum(ss[:], bh(sq[:]), axis=AX),
                 reads=[sqb], writes=[ssb])
            k.op(ACT, lambda e: e.activation(ss[:], ss[:], AF.Sqrt),
                 reads=[ssb], writes=[ssb])
            c12, c12b = k.c(1e-12)
            k.op(DVE, lambda e: e.tensor_scalar(ss[:], ss[:], c12, None,
                                                ALU.max),
                 reads=[ssb], scalar_reads=[c12b], writes=[ssb])
            k.op(DVE, lambda e: e.reciprocal(ss[:], ss[:]), reads=[ssb],
                 writes=[ssb])
            k.op(DVE, lambda e, s=src_t: e.tensor_tensor(
                bh(t1[:]), bh(s[:]),
                ss[:].rearrange("p (h o) -> p h o", o=1).broadcast_to((128, NH, HD)),
                ALU.mult), reads=[src_b, ssb], writes=[t1b])
            k.op(DVE, lambda e, sc_=scl: e.tensor_tensor(
                bh(t1[:]), bh(t1[:]),
                sc_[:].rearrange("p (o d) -> p o d", o=1).broadcast_to((128, NH, HD)),
                ALU.mult), reads=[t1b, sclb], writes=[t1b])
            atap = tabs[:, atab * HD:(atab + 1) * HD]
            btap = tabs[:, btab * HD:(btab + 1) * HD]
            k.op(DVE, lambda e, at=atap: e.tensor_tensor(
                bh(sq[:]), bh(t1[:]),
                at.rearrange("p (o d) -> p o d", o=1).broadcast_to((128, NH, HD)),
                ALU.mult), reads=[t1b, tabsb], writes=[sqb])
            swap = bass.AP(tensor=t1[:].tensor, offset=HD // 2,
                           ap=[list(t1[:].ap[0]), [HD, NH], [-(HD // 2), 2],
                               [1, HD // 2]])
            k.op(DVE, lambda e, bt=btap, sw=swap: e.tensor_tensor(
                t2[:].rearrange("p (h s j) -> p h s j", h=NH, s=2), sw,
                bt.rearrange("p (o s j) -> p o s j", o=1, s=2)
                  .broadcast_to((128, NH, 2, HD // 2)),
                ALU.mult), reads=[t1b, tabsb], writes=[t2b])
            k.op(DVE, lambda e, d=dst_t: e.tensor_tensor(d[:], sq[:], t2[:],
                                                         ALU.add),
                 reads=[sqb, t2b], writes=[dst_b])
        for h in range(NH):
            pr = (h % 2) * 64
            cc = (h // 2) * 128
            emit_transpose_128(k, sc, qbf[:, h * HD:(h + 1) * HD], [qbfb],
                               qcm[pr:pr + 64, cc:cc + 128], [qcmb])
            emit_transpose_128(k, sc, kbf[:, h * HD:(h + 1) * HD], [kbfb],
                               kcm[pr:pr + 64, cc:cc + 128], [kcmb])
        mvar = 2 if it >= 2 else it
        for h in range(NH):
            pr = (h % 2) * 64
            cc = (h // 2) * 128
            def mms(e, pr=pr, cc=cc):
                last = None
                for j in range(3):
                    jt = it - 2 + j
                    kr, krb = sc.a_kring[jt % 3]
                    last = e.matmul(psim[:, j * 128:(j + 1) * 128],
                                    qcm[pr:pr + 64, cc:cc + 128],
                                    kr[pr:pr + 64, cc:cc + 128],
                                    start=True, stop=True)
                return last
            k.op(PE, mms, reads=[qcmb] + [sc.a_kring[j][1] for j in range(3)],
                 writes=[psimb])
            k.op(DVE, lambda e: e.scalar_tensor_tensor(
                smask[:], psim[:, 0:KEYS], 8.0,
                m_t[:, mvar * KEYS:(mvar + 1) * KEYS], ALU.mult, ALU.add),
                reads=[psimb, m_b], writes=[smaskb])
            k.op(DVE, lambda e: e.reduce_max(nmax[:, 0:1], smask[:], axis=AX,
                                             negate=True),
                 reads=[smaskb], writes=[nmaxb])
            k.op(ACT, lambda e: e.activation(ebf[:], smask[:], AF.Exp,
                                             bias=nmax[:, 0:1],
                                             accum_out=nmax[:, 1:2]),
                 reads=[smaskb], scalar_reads=[nmaxb], writes=[ebfb, nmaxb])
            k.op(DVE, lambda e: e.reciprocal(nmax[:, 1:2], nmax[:, 1:2]),
                 reads=[nmaxb], writes=[nmaxb])
            k.op(DVE, lambda e: e.tensor_scalar(ebf[:], ebf[:], nmax[:, 1:2],
                                                None, ALU.mult),
                 reads=[ebfb], scalar_reads=[nmaxb], writes=[ebfb])
            for j in range(3):
                emit_transpose_128(k, sc, ebf[:, j * 128:(j + 1) * 128],
                                   [ebfb], et[:, j * 128:(j + 1) * 128], [etb])
            def mmv(e, h=h):
                last = None
                for j in range(3):
                    jt = it - 2 + j
                    vr, vrb = sc.a_vring[jt % 3]
                    last = e.matmul(pav[:, 0:128], vr[:, h * HD:(h + 1) * HD],
                                    et[:, j * 128:(j + 1) * 128],
                                    start=(j == 0), stop=(j == 2))
                return last
            k.op(PE, mmv, reads=[etb] + [sc.a_vring[j][1] for j in range(3)],
                 writes=[pavb])
            k.op(DVE, lambda e, pr=pr, cc=cc: e.tensor_copy(
                acm[pr:pr + 64, cc:cc + 128], pav[:, 0:128]),
                reads=[pavb], writes=[acmb])
        for no in range(2):
            def mmo(e, no=no):
                last = None
                for kc in range(KC):
                    last = e.matmul(po[:, 0:512],
                                    acm[:, kc * 128:(kc + 1) * 128],
                                    wslice(wo, 1024, kc, no * 512, 512),
                                    start=(kc == 0), stop=(kc == KC - 1))
                return last
            k.op(PE, mmo, reads=[acmb, wob], writes=[pob])
            k.op(DVE, lambda e, no=no: e.tensor_add(
                xt[:, no * 512:(no + 1) * 512], xt[:, no * 512:(no + 1) * 512],
                po[:, 0:512]), reads=[pob], writes=[xtb])
        k.dma(SP, x_out[r0:r0 + 128, :], xt[:], reads=[xtb])


def load_lat_attn_weights(k, sc, tag, wqkv_d, wo_d, qs_rep_d, ks_rep_d, tabs_d):
    W = {}
    W["wqkv"] = load_weight_cm(k, f"wqkv_{tag}", wqkv_d, D, 3072)
    W["wo"] = load_weight_cm(k, f"wo_{tag}", wo_d, D, 1024)
    W["qsb"] = bcast_row_tile(k, sc, f"qsb_{tag}", qs_rep_d, HD)
    W["ksb"] = bcast_row_tile(k, sc, f"ksb_{tag}", ks_rep_d, HD)
    W["tabs_d"] = tabs_d
    return W


P7 = 256
NP7 = T // P7


def alloc_rwkv7(k, sc):
    sc.r_xnp = k.sb("r7_xnp", [128, KC * P7], BF16)
    sc.r_xnsh = k.sb("r7_xnsh", [128, KC * P7], BF16)
    sc.r_stage = k.sb("r7_stage", [128, KC * 128], BF16)
    sc.r_xt = k.sb("r7_xt", [128, D], F32)
    sc.r_xn = k.sb("r7_xn", [128, D], F32)
    sc.r_xnbf = k.sb("r7_xnbf", [128, D], BF16)
    sc.r_mix = k.sb("r7_mix", [128, KC * P7], BF16)
    sc.r_sx = k.sb("r7_sx", [128, KC * P7], F32)
    sc.r_pw = k.ps("r7_pw", [128, P7], F32)
    sc.r_pw2 = k.ps("r7_pw2", [128, P7], F32)
    sc.r_plr = k.ps("r7_plr", [64, P7], F32)
    sc.r_ph = k.ps("r7_ph", [16, P7], F32)
    sc.r_r = k.sb("r7_r", [128, KC * P7], F32)
    sc.r_k = k.sb("r7_k", [128, KC * P7], F32)
    sc.r_v = k.sb("r7_v", [128, KC * P7], F32)
    sc.r_g = k.sb("r7_g", [128, KC * P7], BF16)
    sc.r_kv = k.sb("r7_kv", [128, P7], F32)
    sc.r_st = k.sb("r7_st", [128, KC * P7], F32)
    sc.r_carry = k.sb("r7_carry", [128, KC], F32)
    sc.r_lr = k.sb("r7_lr", [64, P7], BF16)
    sc.r_hh = k.sb("r7_hh", [16, 8 * P7], F32)
    sc.r_wf = k.sb("r7_wf", [128, P7], F32)
    sc.r_xgn = k.sb("r7_xgn", [128, KC * P7], BF16)
    sc.r_po = k.ps("r7_po", [128, 512], F32)
    sc.r_sqp = k.sb("r7_sqp", [128, P7], F32)


def emit_rwkv7(k, sc, x_in, x_out, W, xn_hbm, xn_hbm_buf):
    xnp, xnpb = sc.r_xnp
    xnsh, xnshb = sc.r_xnsh
    stage, stageb = sc.r_stage
    xt, xtb = sc.r_xt
    xn, xnb = sc.r_xn
    xnbf, xnbfb = sc.r_xnbf
    mix, mixb = sc.r_mix
    sx, sxb = sc.r_sx
    pw, pwb = sc.r_pw
    pw2, pw2b = sc.r_pw2
    plr, plrb = sc.r_plr
    ph, phb = sc.r_ph
    r_, rb = sc.r_r
    kx, kxb = sc.r_k
    vx, vxb = sc.r_v
    gx, gxb = sc.r_g
    kv, kvb = sc.r_kv
    stt, sttb = sc.r_st
    carry, carryb = sc.r_carry
    lr, lrb = sc.r_lr
    hh, hhb = sc.r_hh
    wf, wfb = sc.r_wf
    xgn, xgnb = sc.r_xgn
    po, pob = sc.r_po
    sqp, sqpb = sc.r_sqp
    Wr, Wrb = W["Wr"]; Wk, Wkb = W["Wk"]; Wv, Wvb = W["Wv"]; Wo, Wob = W["Wo"]
    w1, w1b = W["w1"]; a1, a1b = W["a1"]; g1, g1b = W["g1"]
    w2, w2b = W["w2"]; a2, a2b = W["a2"]; g2, g2b = W["g2"]
    w0a0, w0a0b = W["w0a0"]
    mixc, mixcb = W["mixcols"]
    cols, colsb = W["cols"]
    bm, bmb = W["bm"]
    rkm, rkmb = W["rkm"]

    def HS(i):
        return hh[:, i * P7:(i + 1) * P7]
    HW, HA, HRS, HRA, HBC, HM, HR2, HT = range(8)

    for it in range(NT):
        r0 = it * 128
        k.dma(SP, xt[:], x_in[r0:r0 + 128, :], writes=[xtb])
        emit_ln_tm(k, sc, xt[:], xtb, xn[:], xnb)
        k.op(DVE, lambda e: e.tensor_copy(xnbf[:], xn[:]), reads=[xnb],
             writes=[xnbfb])
        for kc in range(KC):
            emit_transpose_128(k, sc, xnbf[:, kc * 128:(kc + 1) * 128],
                               [xnbfb], stage[:, kc * 128:(kc + 1) * 128],
                               [stageb])
        for kc in range(KC):
            k.dma(SP, xn_hbm[kc * 128:(kc + 1) * 128, r0:r0 + 128],
                  stage[:, kc * 128:(kc + 1) * 128], reads=[stageb],
                  writes=[xn_hbm_buf])
    for rr in range(KC):
        k.op(DVE, lambda e, rr=rr: e.tensor_scalar(
            rkm[:, rr * 16:(rr + 1) * 16], bm[:, rr * 16:(rr + 1) * 16],
            cols[:, 8 + rr:8 + rr + 1], None, ALU.mult),
            reads=[bmb], scalar_reads=[colsb], writes=[rkmb])
    k.op(DVE, lambda e: e.memset(carry[:], 0.0), writes=[carryb])

    def mixpanel(p0, mcol):
        for rr in range(KC):
            k.op(DVE, lambda e, rr=rr, mcol=mcol: e.scalar_tensor_tensor(
                mix[:, rr * P7:(rr + 1) * P7], sx[:, rr * P7:(rr + 1) * P7],
                mixc[:, mcol * 8 + rr:mcol * 8 + rr + 1],
                xnp[:, rr * P7:(rr + 1) * P7], ALU.mult, ALU.add),
                reads=[sxb, xnpb], scalar_reads=[mixcb], writes=[mixb])

    def lowrank1(wt, wtb):
        def mm(e):
            last = None
            for kc in range(KC):
                last = e.matmul(plr[:, 0:P7], wslice(wt, 64, kc, 0, 64),
                                mix[:, kc * P7:(kc + 1) * P7],
                                start=(kc == 0), stop=(kc == KC - 1))
            return last
        k.op(PE, mm, reads=[wtb, mixb], writes=[plrb])
        k.op(ACT, lambda e: e.activation(lr[:], plr[:, 0:P7], AF.Tanh),
             reads=[plrb], writes=[lrb])

    def bcast16(dst_psum, dst_psum_b, src_row_ap, rr):
        bmT, bmTb = W["bmT"]
        k.op(PE, lambda e: e.matmul(dst_psum[:, 0:P7],
                                    bmT[0:16, rr * 128:(rr + 1) * 128],
                                    src_row_ap, start=True, stop=True),
             reads=[bmTb, hhb], writes=[dst_psum_b])

    for ip in range(NP7):
        p0 = ip * P7
        for rr in range(KC):
            k.dma(SP, xnp[:, rr * P7:(rr + 1) * P7],
                  xn_hbm[rr * 128:(rr + 1) * 128, p0:p0 + P7],
                  reads=[xn_hbm_buf], writes=[xnpb])
            if ip == 0:
                k.op(DVE, lambda e, rr=rr: e.memset(
                    xnsh[:, rr * P7:rr * P7 + 1], 0.0), writes=[xnshb])
                k.dma(SP, xnsh[:, rr * P7 + 1:(rr + 1) * P7],
                      xn_hbm[rr * 128:(rr + 1) * 128, 0:P7 - 1],
                      reads=[xn_hbm_buf], writes=[xnshb])
            else:
                k.dma(SP, xnsh[:, rr * P7:(rr + 1) * P7],
                      xn_hbm[rr * 128:(rr + 1) * 128, p0 - 1:p0 + P7 - 1],
                      reads=[xn_hbm_buf], writes=[xnshb])
            k.op(DVE, lambda e, rr=rr: e.tensor_sub(
                sx[:, rr * P7:(rr + 1) * P7], xnp[:, rr * P7:(rr + 1) * P7],
                xnsh[:, rr * P7:(rr + 1) * P7]), reads=[xnpb, xnshb],
                writes=[sxb])
        for (mcol, w_t, w_b, dst, dstb) in (
                (0, Wr, Wrb, r_, rb), (2, Wk, Wkb, kx, kxb),
                (3, Wv, Wvb, vx, vxb)):
            mixpanel(p0, mcol)
            for rr in range(KC):
                def mm(e, rr=rr, w_t=w_t):
                    last = None
                    for kc in range(KC):
                        last = e.matmul(pw[:, 0:P7],
                                        wslice(w_t, 1024, kc, rr * 128, 128),
                                        mix[:, kc * P7:(kc + 1) * P7],
                                        start=(kc == 0), stop=(kc == KC - 1))
                    return last
                k.op(PE, mm, reads=[w_b, mixb], writes=[pwb])
                k.op(DVE, lambda e, rr=rr, dst=dst: e.tensor_copy(
                    dst[:, rr * P7:(rr + 1) * P7], pw[:, 0:P7]),
                    reads=[pwb], writes=[dstb])
        mixpanel(p0, 1)
        lowrank1(w1, w1b)
        k.op(PE, lambda e: e.matmul(ph[:, 0:P7], w2[0:64, 0:16], lr[:],
                                    start=True, stop=True),
             reads=[w2b, lrb], writes=[phb])
        k.op(DVE, lambda e: e.tensor_scalar(HS(HW), ph[:, 0:P7],
                                            w0a0[0:16, 0:1], None, ALU.add),
             reads=[phb], scalar_reads=[w0a0b], writes=[hhb])
        zs = HS(HW)
        ts_ = HS(HT)
        k.op(DVE, lambda e: e.tensor_scalar(ts_, zs, -1.0, None, ALU.mult),
             reads=[hhb], writes=[hhb])
        k.op(DVE, lambda e: e.tensor_max(ts_, ts_, zs), reads=[hhb],
             writes=[hhb])
        k.op(ACT, lambda e: e.activation(ts_, ts_, AF.Exp, scale=-1.0),
             reads=[hhb], writes=[hhb])
        c1, c1b = k.c_rows(1.0, 16)
        k.op(ACT, lambda e: e.activation(ts_, ts_, AF.Ln, bias=c1),
             reads=[hhb, c1b], writes=[hhb])
        k.op(DVE, lambda e: e.tensor_scalar(zs, zs, -1.0, 0.0, ALU.mult,
                                            ALU.max), reads=[hhb], writes=[hhb])
        k.op(DVE, lambda e: e.tensor_add(zs, zs, ts_), reads=[hhb], writes=[hhb])
        cm05, cm05b = k.c_rows(-0.5, 16)
        k.op(ACT, lambda e: e.activation(zs, zs, AF.Exp, scale=-1.0, bias=cm05),
             reads=[hhb, cm05b], writes=[hhb])
        mixpanel(p0, 4)
        lowrank1(a1, a1b)
        k.op(PE, lambda e: e.matmul(ph[:, 0:P7], a2[0:64, 0:16], lr[:],
                                    start=True, stop=True),
             reads=[a2b, lrb], writes=[phb])
        k.op(ACT, lambda e: e.activation(HS(HA), ph[:, 0:P7], AF.Sigmoid,
                                         bias=w0a0[0:16, 1:2]),
             reads=[phb], scalar_reads=[w0a0b], writes=[hhb])
        mixpanel(p0, 5)
        lowrank1(g1, g1b)
        for rr in range(KC):
            k.op(PE, lambda e, rr=rr: e.matmul(
                pw[:, 0:P7], wslice(g2, 1024, 0, rr * 128, 128, rows=64),
                lr[:], start=True, stop=True), reads=[g2b, lrb], writes=[pwb])
            k.op(ACT, lambda e, rr=rr: e.activation(
                gx[:, rr * P7:(rr + 1) * P7], pw[:, 0:P7], AF.Sigmoid),
                reads=[pwb], writes=[gxb])
        kkx, kkxb = kx, kxb
        for rr in range(KC):
            k.op(DVE, lambda e, rr=rr: e.tensor_scalar(
                kkx[:, rr * P7:(rr + 1) * P7], kx[:, rr * P7:(rr + 1) * P7],
                cols[:, rr:rr + 1], None, ALU.mult),
                reads=[kxb], scalar_reads=[colsb], writes=[kkxb])
        for rr in range(KC):
            k.op(ACT, lambda e, rr=rr: e.activation(
                sqp[:, 0:P7], kkx[:, rr * P7:(rr + 1) * P7], AF.Square),
                reads=[kkxb], writes=[sqpb])
            k.op(PE, lambda e, rr=rr: e.matmul(
                ph[:, 0:P7], bm[:, rr * 16:(rr + 1) * 16], sqp[:, 0:P7],
                start=(rr == 0), stop=(rr == KC - 1)),
                reads=[bmb, sqpb], writes=[phb])
        rs = HS(HRS)
        k.op(ACT, lambda e: e.activation(rs, ph[:, 0:P7], AF.Sqrt),
             reads=[phb], writes=[hhb])
        c12, c12b = k.c_rows(1e-12, 16)
        k.op(DVE, lambda e: e.tensor_scalar(rs, rs, c12, None, ALU.max),
             reads=[hhb], scalar_reads=[c12b], writes=[hhb])
        k.op(DVE, lambda e: e.reciprocal(rs, rs), reads=[hhb], writes=[hhb])
        k.op(DVE, lambda e: e.tensor_tensor(HS(HRA), rs, HS(HA), ALU.mult),
             reads=[hhb], writes=[hhb])
        for rr in range(KC):
            k.op(DVE, lambda e, rr=rr: e.tensor_tensor(
                sqp[:, 0:P7], r_[:, rr * P7:(rr + 1) * P7],
                kkx[:, rr * P7:(rr + 1) * P7], ALU.mult),
                reads=[rb, kkxb], writes=[sqpb])
            k.op(PE, lambda e, rr=rr: e.matmul(
                ph[:, 0:P7], rkm[:, rr * 16:(rr + 1) * 16], sqp[:, 0:P7],
                start=(rr == 0), stop=(rr == KC - 1)),
                reads=[rkmb, sqpb], writes=[phb])
        k.op(DVE, lambda e: e.tensor_tensor(HS(HBC), ph[:, 0:P7], rs, ALU.mult),
             reads=[phb, hhb], writes=[hhb])
        for rr in range(KC):
            bcast16(pw, pwb, HS(HRA), rr)
            k.op(DVE, lambda e, rr=rr: e.tensor_tensor(
                kv[:, 0:P7], kkx[:, rr * P7:(rr + 1) * P7],
                vx[:, rr * P7:(rr + 1) * P7], ALU.mult),
                reads=[kkxb, vxb], writes=[kvb])
            k.op(DVE, lambda e: e.tensor_tensor(kv[:, 0:P7], kv[:, 0:P7],
                                                pw[:, 0:P7], ALU.mult),
                 reads=[kvb, pwb], writes=[kvb])
            bcast16(pw2, pw2b, HS(HW), rr)
            k.op(DVE, lambda e: e.tensor_copy(wf[:, 0:P7], pw2[:, 0:P7]),
                 reads=[pw2b], writes=[wfb])
            k.op(DVE, lambda e, rr=rr: e.tensor_tensor_scan(
                stt[:, rr * P7:(rr + 1) * P7], wf[:, 0:P7], kv[:, 0:P7],
                carry[:, rr:rr + 1], ALU.mult, ALU.add),
                reads=[wfb, kvb], scalar_reads=[carryb], writes=[sttb])
            k.op(DVE, lambda e, rr=rr: e.tensor_copy(
                carry[:, rr:rr + 1], stt[:, (rr + 1) * P7 - 1:(rr + 1) * P7]),
                reads=[sttb], writes=[carryb])
        for rr in range(KC):
            bcast16(pw, pwb, HS(HBC), rr)
            k.op(DVE, lambda e, rr=rr: e.tensor_tensor(
                stt[:, rr * P7:(rr + 1) * P7], stt[:, rr * P7:(rr + 1) * P7],
                r_[:, rr * P7:(rr + 1) * P7], ALU.mult),
                reads=[sttb, rb], writes=[sttb])
            k.op(DVE, lambda e, rr=rr: e.tensor_tensor(
                sqp[:, 0:P7], vx[:, rr * P7:(rr + 1) * P7], pw[:, 0:P7],
                ALU.mult), reads=[vxb, pwb], writes=[sqpb])
            k.op(DVE, lambda e, rr=rr: e.tensor_add(
                stt[:, rr * P7:(rr + 1) * P7], stt[:, rr * P7:(rr + 1) * P7],
                sqp[:, 0:P7]), reads=[sttb, sqpb], writes=[sttb])
        for rr in range(KC):
            k.op(PE, lambda e, rr=rr: e.matmul(
                ph[:, 0:P7], bm[:, rr * 16:(rr + 1) * 16],
                stt[:, rr * P7:(rr + 1) * P7],
                start=(rr == 0), stop=(rr == KC - 1)),
                reads=[bmb, sttb], writes=[phb])
        k.op(DVE, lambda e: e.tensor_scalar(HS(HM), ph[:, 0:P7], 1.0 / 64, None,
                                            ALU.mult), reads=[phb], writes=[hhb])
        for rr in range(KC):
            k.op(ACT, lambda e, rr=rr: e.activation(
                sqp[:, 0:P7], stt[:, rr * P7:(rr + 1) * P7], AF.Square),
                reads=[sttb], writes=[sqpb])
            k.op(PE, lambda e, rr=rr: e.matmul(
                ph[:, 0:P7], bm[:, rr * 16:(rr + 1) * 16], sqp[:, 0:P7],
                start=(rr == 0), stop=(rr == KC - 1)),
                reads=[bmb, sqpb], writes=[phb])
        k.op(DVE, lambda e: e.scalar_tensor_tensor(HS(HT), HS(HM), -1.0,
                                                   HS(HM), ALU.mult, ALU.mult),
             reads=[hhb], writes=[hhb])
        k.op(DVE, lambda e: e.tensor_scalar(HS(HR2), ph[:, 0:P7], 1.0 / 64,
                                            None, ALU.mult),
             reads=[phb], writes=[hhb])
        k.op(DVE, lambda e: e.tensor_add(HS(HR2), HS(HR2), HS(HT)),
             reads=[hhb], writes=[hhb])
        ceps, cepsb = k.c_rows(1e-5, 16)
        k.op(ACT, lambda e: e.activation(HS(HR2), HS(HR2), AF.Ln, bias=ceps),
             reads=[hhb, cepsb], writes=[hhb])
        k.op(ACT, lambda e: e.activation(HS(HR2), HS(HR2), AF.Exp, scale=-0.5),
             reads=[hhb], writes=[hhb])
        k.op(DVE, lambda e: e.scalar_tensor_tensor(HS(HT), HS(HM), -1.0,
                                                   HS(HR2), ALU.mult, ALU.mult),
             reads=[hhb], writes=[hhb])
        for rr in range(KC):
            bcast16(pw, pwb, HS(HR2), rr)
            bcast16(pw2, pw2b, HS(HT), rr)
            k.op(DVE, lambda e, rr=rr: e.tensor_tensor(
                sqp[:, 0:P7], stt[:, rr * P7:(rr + 1) * P7], pw[:, 0:P7],
                ALU.mult), reads=[sttb, pwb], writes=[sqpb])
            k.op(DVE, lambda e, rr=rr: e.tensor_add(
                sqp[:, 0:P7], sqp[:, 0:P7], pw2[:, 0:P7]),
                reads=[sqpb, pw2b], writes=[sqpb])
            k.op(DVE, lambda e, rr=rr: e.tensor_scalar(
                sqp[:, 0:P7], sqp[:, 0:P7], cols[:, 16 + rr:16 + rr + 1],
                cols[:, 24 + rr:24 + rr + 1], ALU.mult, ALU.add),
                reads=[sqpb], scalar_reads=[colsb], writes=[sqpb])
            k.op(DVE, lambda e, rr=rr: e.tensor_tensor(
                xgn[:, rr * P7:(rr + 1) * P7], sqp[:, 0:P7],
                gx[:, rr * P7:(rr + 1) * P7], ALU.mult),
                reads=[sqpb, gxb], writes=[xgnb])
        for tt in range(P7 // 128):
            t0 = p0 + tt * 128
            k.dma(SP, xt[:], x_in[t0:t0 + 128, :], writes=[xtb])
            for no in range(2):
                def mmo(e, tt=tt, no=no):
                    last = None
                    for kc in range(KC):
                        last = e.matmul(po[:, 0:512],
                                        xgn[:, kc * P7 + tt * 128:kc * P7 + tt * 128 + 128],
                                        wslice(Wo, 1024, kc, no * 512, 512),
                                        start=(kc == 0), stop=(kc == KC - 1))
                    return last
                k.op(PE, mmo, reads=[xgnb, Wob], writes=[pob])
                k.op(DVE, lambda e, no=no: e.tensor_add(
                    xt[:, no * 512:(no + 1) * 512],
                    xt[:, no * 512:(no + 1) * 512], po[:, 0:512]),
                    reads=[pob], writes=[xtb])
            k.dma(SP, x_out[t0:t0 + 128, :], xt[:], reads=[xtb])


def load_rwkv7_weights(k, sc, tag, d):
    W = {}
    for nm in ("Wr", "Wk", "Wv", "Wo"):
        W[nm] = load_weight_cm(k, f"{nm}_{tag}", d[nm], D, 1024)
    for nm in ("w1", "a1", "g1"):
        W[nm] = load_weight_cm(k, f"{nm}_{tag}", d[nm], D, 64)
    for nm, (ri, ci) in (("w2", (64, 16)), ("a2", (64, 16)), ("g2", (64, 1024))):
        t, b = k.sb(f"{nm}_{tag}", [ri, ci], BF16)
        k.dma(POOL, t[:], d[nm], writes=[b])
        W[nm] = (t, b)
    for nm, (ri, ci) in (("w0a0", (16, 2)), ("mixcols", (128, 48)),
                         ("cols", (128, 32)), ("bm", (128, 128)),
                         ("bmT", (16, 1024))):
        t, b = k.sb(f"{nm}_{tag}", [ri, ci], F32)
        k.dma(SP, t[:], d[nm], writes=[b])
        W[nm] = (t, b)
    t, b = k.sb(f"rkm_{tag}", [128, 128], F32)
    W["rkm"] = (t, b)
    return W


P4 = 256
NP4 = T // P4


def alloc_rwkv4(k, sc):
    sc.q_xnp = k.sb("r4_xnp", [128, KC * P4], BF16)
    sc.q_xnsh = k.sb("r4_xnsh", [128, KC * P4], BF16)
    sc.q_stage = k.sb("r4_stage", [128, KC * 128], BF16)
    sc.q_xt = k.sb("r4_xt", [128, D], F32)
    sc.q_xn = k.sb("r4_xn", [128, D], F32)
    sc.q_xnbf = k.sb("r4_xnbf", [128, D], BF16)
    sc.q_mix = k.sb("r4_mix", [128, KC * P4], BF16)
    sc.q_pw = k.ps("r4_pw", [128, P4], F32)
    sc.q_r = k.sb("r4_r", [128, KC * P4], F32)
    sc.q_eK = k.sb("r4_eK", [128, KC * P4], F32)
    sc.q_v = k.sb("r4_v", [128, KC * P4], F32)
    sc.q_bb = k.sb("r4_bb", [128, KC * P4], F32)
    sc.q_bbp = k.sb("r4_bbp", [128, P4], F32)
    sc.q_aa = k.sb("r4_aa", [128, KC * P4], F32)
    sc.q_aap = k.sb("r4_aap", [128, P4], F32)
    sc.q_t1 = k.sb("r4_t1", [128, P4], F32)
    sc.q_t2 = k.sb("r4_t2", [128, P4], F32)
    sc.q_t3 = k.sb("r4_t3", [128, P4], F32)
    sc.q_ewf = k.sb("r4_ewf", [128, KC * P4], F32)
    sc.q_carB = k.sb("r4_carB", [128, KC], F32)
    sc.q_carA = k.sb("r4_carA", [128, KC], F32)
    sc.q_carB2 = k.sb("r4_carB2", [128, KC], F32)
    sc.q_carA2 = k.sb("r4_carA2", [128, KC], F32)
    sc.q_ph = k.ps("r4_ph", [16, P4], F32)
    sc.q_acc = k.sb("r4_acc", [16, 4], F32)
    sc.q_dcol = k.sb("r4_dcol", [128, 16], F32)
    sc.q_xgn = k.sb("r4_xgn", [128, KC * P4], BF16)
    sc.q_po = k.ps("r4_po", [128, 512], F32)
    sc.q_colAB = k.sb("r4_colAB", [128, 2], F32)
    sc.q_pc = k.ps("r4_pc", [128, 1], F32)


def emit_rwkv4(k, sc, x_in, x_out, W, xn_hbm, xn_hbm_buf, rw_hbm, rw_hbm_buf):
    xnp, xnpb = sc.q_xnp
    xnsh, xnshb = sc.q_xnsh
    stage, stageb = sc.q_stage
    xt, xtb = sc.q_xt
    xn, xnb = sc.q_xn
    xnbf, xnbfb = sc.q_xnbf
    mix, mixb = sc.q_mix
    pw, pwb = sc.q_pw
    r_, rb = sc.q_r
    eK, eKb = sc.q_eK
    vx, vxb = sc.q_v
    bb, bbb = sc.q_bb
    bbp, bbpb = sc.q_bbp
    aa, aab = sc.q_aa
    aap, aapb = sc.q_aap
    t1, t1b = sc.q_t1
    t2, t2b = sc.q_t2
    t3, t3b = sc.q_t3
    ewf, ewfb = sc.q_ewf
    carB, carBb = sc.q_carB
    carA, carAb = sc.q_carA
    carB2, carB2b = sc.q_carB2
    carA2, carA2b = sc.q_carA2
    ph, phb = sc.q_ph
    acc, accb = sc.q_acc
    dcol, dcolb = sc.q_dcol
    xgn, xgnb = sc.q_xgn
    po, pob = sc.q_po
    colAB, colABb = sc.q_colAB
    pc, pcb = sc.q_pc
    WrT, WrTb = W["WrT"]; WkT, WkTb = W["WkT"]
    WvT, WvTb = W["WvT"]; WoT, WoTb = W["WoT"]
    cols, colsb = W["cols4"]
    bm, bmb = W["bm"]
    bmT, bmTb = W["bmT"]

    CT_TMR, CT_TMK, CT_TMV, CT_TD, CT_TF, CT_G, CT_B = range(7)

    def ccol(which, rr):
        return cols[:, which * 8 + rr:which * 8 + rr + 1]

    for it in range(NT):
        r0 = it * 128
        k.dma(SP, xt[:], x_in[r0:r0 + 128, :], writes=[xtb])
        emit_ln_tm(k, sc, xt[:], xtb, xn[:], xnb)
        k.op(DVE, lambda e: e.tensor_copy(xnbf[:], xn[:]), reads=[xnb],
             writes=[xnbfb])
        for kc in range(KC):
            emit_transpose_128(k, sc, xnbf[:, kc * 128:(kc + 1) * 128],
                               [xnbfb], stage[:, kc * 128:(kc + 1) * 128],
                               [stageb])
        for kc in range(KC):
            k.dma(SP, xn_hbm[kc * 128:(kc + 1) * 128, r0:r0 + 128],
                  stage[:, kc * 128:(kc + 1) * 128], reads=[stageb],
                  writes=[xn_hbm_buf])
    for rr in range(KC):
        k.op(ACT, lambda e, rr=rr: e.activation(dcol[:, rr:rr + 1],
                                                ccol(CT_TD, rr), AF.Exp),
             reads=[colsb], writes=[dcolb])
        k.op(ACT, lambda e, rr=rr: e.activation(dcol[:, rr:rr + 1],
                                                dcol[:, rr:rr + 1], AF.Exp,
                                                scale=-1.0),
             reads=[dcolb], writes=[dcolb])
        k.op(ACT, lambda e, rr=rr: e.activation(dcol[:, 8 + rr:8 + rr + 1],
                                                ccol(CT_TF, rr), AF.Exp),
             reads=[colsb], writes=[dcolb])
        k.op(DVE, lambda e, rr=rr: e.memset(ewf[:, rr * P4:(rr + 1) * P4], 1.0),
             writes=[ewfb])
        k.op(DVE, lambda e, rr=rr: e.tensor_scalar(
            ewf[:, rr * P4:(rr + 1) * P4], ewf[:, rr * P4:(rr + 1) * P4],
            dcol[:, rr:rr + 1], None, ALU.mult),
            reads=[ewfb], scalar_reads=[dcolb], writes=[ewfb])
    k.op(DVE, lambda e: e.memset(carB[:], 0.0), writes=[carBb])
    k.op(DVE, lambda e: e.memset(carA[:], 0.0), writes=[carAb])
    k.op(DVE, lambda e: e.memset(carB2[:], 0.0), writes=[carB2b])
    k.op(DVE, lambda e: e.memset(carA2[:], 0.0), writes=[carA2b])
    k.op(DVE, lambda e: e.memset(acc[:], 0.0), writes=[accb])

    def mixpanel(which):
        for rr in range(KC):
            k.op(DVE, lambda e, rr=rr: e.tensor_scalar(
                mix[:, rr * P4:(rr + 1) * P4], xnsh[:, rr * P4:(rr + 1) * P4],
                ccol(which, rr), None, ALU.mult),
                reads=[xnshb], scalar_reads=[colsb], writes=[mixb])
            k.op(DVE, lambda e, rr=rr: e.tensor_sub(
                mix[:, rr * P4:(rr + 1) * P4], xnsh[:, rr * P4:(rr + 1) * P4],
                mix[:, rr * P4:(rr + 1) * P4]), reads=[xnshb, mixb],
                writes=[mixb])
            k.op(DVE, lambda e, rr=rr: e.scalar_tensor_tensor(
                mix[:, rr * P4:(rr + 1) * P4], xnp[:, rr * P4:(rr + 1) * P4],
                ccol(which, rr), mix[:, rr * P4:(rr + 1) * P4],
                ALU.mult, ALU.add),
                reads=[xnpb, mixb], scalar_reads=[colsb], writes=[mixb])

    for ip in range(NP4):
        p0 = ip * P4
        for rr in range(KC):
            k.dma(SP, xnp[:, rr * P4:(rr + 1) * P4],
                  xn_hbm[rr * 128:(rr + 1) * 128, p0:p0 + P4],
                  reads=[xn_hbm_buf], writes=[xnpb])
            if ip == 0:
                k.op(DVE, lambda e, rr=rr: e.memset(
                    xnsh[:, rr * P4:rr * P4 + 1], 0.0), writes=[xnshb])
                k.dma(SP, xnsh[:, rr * P4 + 1:(rr + 1) * P4],
                      xn_hbm[rr * 128:(rr + 1) * 128, 0:P4 - 1],
                      reads=[xn_hbm_buf], writes=[xnshb])
            else:
                k.dma(SP, xnsh[:, rr * P4:(rr + 1) * P4],
                      xn_hbm[rr * 128:(rr + 1) * 128, p0 - 1:p0 + P4 - 1],
                      reads=[xn_hbm_buf], writes=[xnshb])
        for (which, w_t, w_b, dst, dstb, act) in (
                (CT_TMR, WrT, WrTb, r_, rb, None),
                (CT_TMK, WkT, WkTb, eK, eKb, AF.Exp),
                (CT_TMV, WvT, WvTb, vx, vxb, None)):
            mixpanel(which)
            for rr in range(KC):
                def mm(e, rr=rr, w_t=w_t):
                    last = None
                    for kc in range(KC):
                        last = e.matmul(pw[:, 0:P4],
                                        wslice(w_t, 1024, kc, rr * 128, 128),
                                        mix[:, kc * P4:(kc + 1) * P4],
                                        start=(kc == 0), stop=(kc == KC - 1))
                    return last
                k.op(PE, mm, reads=[w_b, mixb], writes=[pwb])
                if act is None:
                    k.op(DVE, lambda e, rr=rr, dst=dst: e.tensor_copy(
                        dst[:, rr * P4:(rr + 1) * P4], pw[:, 0:P4]),
                        reads=[pwb], writes=[dstb])
                else:
                    k.op(ACT, lambda e, rr=rr, dst=dst, act=act: e.activation(
                        dst[:, rr * P4:(rr + 1) * P4], pw[:, 0:P4], act),
                        reads=[pwb], writes=[dstb])
        for rr in range(KC):
            sl = slice(rr * P4, (rr + 1) * P4)
            ew_sl = ewf[:, sl]
            k.op(DVE, lambda e: e.tensor_tensor_scan(
                bb[:, sl], ew_sl, eK[:, sl], carB[:, rr:rr + 1],
                ALU.mult, ALU.add),
                reads=[ewfb, eKb], scalar_reads=[carBb], writes=[bbb])
            k.op(DVE, lambda e: e.tensor_copy(bbp[:, 1:P4], bb[:, rr * P4:(rr + 1) * P4 - 1]),
                 reads=[bbb], writes=[bbpb])
            k.op(DVE, lambda e: e.tensor_copy(bbp[:, 0:1], carB[:, rr:rr + 1]),
                 reads=[carBb], writes=[bbpb])
            k.op(DVE, lambda e: e.tensor_copy(carB[:, rr:rr + 1],
                                              bb[:, (rr + 1) * P4 - 1:(rr + 1) * P4]),
                 reads=[bbb], writes=[carBb])
            k.op(DVE, lambda e: e.tensor_tensor(t1[:], ew_sl, bbp[:], ALU.mult),
                 reads=[ewfb, bbpb], writes=[t1b])
            k.op(DVE, lambda e: e.tensor_max(t2[:], t1[:], eK[:, sl]),
                 reads=[t1b, eKb], writes=[t2b])
            k.op(DVE, lambda e: e.reciprocal(t2[:], t2[:]), reads=[t2b],
                 writes=[t2b])
            k.op(DVE, lambda e: e.tensor_tensor(t1[:], t1[:], t2[:], ALU.mult),
                 reads=[t1b, t2b], writes=[t1b])
            k.op(DVE, lambda e: e.tensor_tensor(t3[:], eK[:, sl], t2[:],
                                                ALU.mult),
                 reads=[eKb, t2b], writes=[t3b])
            k.op(DVE, lambda e: e.tensor_tensor(t3[:], t3[:], vx[:, sl],
                                                ALU.mult),
                 reads=[t3b, vxb], writes=[t3b])
            k.op(DVE, lambda e: e.tensor_tensor_scan(
                aa[:, sl], t1[:], t3[:], carA[:, rr:rr + 1], ALU.mult, ALU.add),
                reads=[t1b, t3b], scalar_reads=[carAb], writes=[aab])
            k.op(DVE, lambda e: e.tensor_copy(aap[:, 1:P4], aa[:, rr * P4:(rr + 1) * P4 - 1]),
                 reads=[aab], writes=[aapb])
            k.op(DVE, lambda e: e.tensor_copy(aap[:, 0:1], carA[:, rr:rr + 1]),
                 reads=[carAb], writes=[aapb])
            k.op(DVE, lambda e: e.tensor_copy(carA[:, rr:rr + 1],
                                              aa[:, (rr + 1) * P4 - 1:(rr + 1) * P4]),
                 reads=[aab], writes=[carAb])
            k.op(DVE, lambda e, rr=rr: e.tensor_scalar(
                t1[:], eK[:, sl], dcol[:, 8 + rr:8 + rr + 1], None, ALU.mult),
                reads=[eKb], scalar_reads=[dcolb], writes=[t1b])
            k.op(DVE, lambda e: e.tensor_tensor(t2[:], aap[:], bbp[:], ALU.mult),
                 reads=[aapb, bbpb], writes=[t2b])
            k.op(DVE, lambda e: e.tensor_tensor(t3[:], t1[:], vx[:, sl],
                                                ALU.mult),
                 reads=[t1b, vxb], writes=[t3b])
            k.op(DVE, lambda e: e.tensor_add(t2[:], t2[:], t3[:]),
                 reads=[t2b, t3b], writes=[t2b])
            k.op(DVE, lambda e: e.tensor_max(t3[:], bbp[:], t1[:]),
                 reads=[bbpb, t1b], writes=[t3b])
            k.op(DVE, lambda e: e.tensor_add(t1[:], t1[:], bbp[:]),
                 reads=[t1b, bbpb], writes=[t1b])
            k.op(DVE, lambda e: e.scalar_tensor_tensor(t1[:], t3[:], 1e-8,
                                                       t1[:], ALU.mult, ALU.add),
                 reads=[t3b, t1b], writes=[t1b])
            k.op(DVE, lambda e: e.reciprocal(t1[:], t1[:]), reads=[t1b],
                 writes=[t1b])
            k.op(DVE, lambda e: e.tensor_tensor(t2[:], t2[:], t1[:], ALU.mult),
                 reads=[t2b, t1b], writes=[t2b])
            k.op(ACT, lambda e: e.activation(t1[:], r_[:, sl], AF.Sigmoid),
                 reads=[rb], writes=[t1b])
            k.op(DVE, lambda e: e.tensor_tensor(t3[:], t1[:], t2[:], ALU.mult),
                 reads=[t1b, t2b], writes=[t3b])
            k.op(DVE, lambda e, rr=rr: e.tensor_copy(xgn[:, sl], t3[:]),
                 reads=[t3b], writes=[xgnb])
            k.dma(SP, rw_hbm[rr * 128:(rr + 1) * 128, p0:p0 + P4], xgn[:, sl],
                  reads=[xgnb], writes=[rw_hbm_buf])
            k.op(PE, lambda e, rr=rr: e.matmul(
                ph[:, 0:P4], bm[:, rr * 16:(rr + 1) * 16], t3[:],
                start=True, stop=True), reads=[bmb, t3b], writes=[phb])
            k.op(DVE, lambda e: e.reduce_sum(acc[:, 2:3], ph[:, 0:P4], axis=AX),
                 reads=[phb], writes=[accb])
            k.op(DVE, lambda e: e.tensor_add(acc[:, 0:1], acc[:, 0:1],
                                             acc[:, 2:3]),
                 reads=[accb], writes=[accb])
            k.op(ACT, lambda e: e.activation(t1[:], t3[:], AF.Square),
                 reads=[t3b], writes=[t1b])
            k.op(PE, lambda e, rr=rr: e.matmul(
                ph[:, 0:P4], bm[:, rr * 16:(rr + 1) * 16], t1[:],
                start=True, stop=True), reads=[bmb, t1b], writes=[phb])
            k.op(DVE, lambda e: e.reduce_sum(acc[:, 2:3], ph[:, 0:P4], axis=AX),
                 reads=[phb], writes=[accb])
            k.op(DVE, lambda e: e.tensor_add(acc[:, 1:2], acc[:, 1:2],
                                             acc[:, 2:3]),
                 reads=[accb], writes=[accb])
    NEL = float(T * 64)
    k.op(DVE, lambda e: e.tensor_scalar(acc[:, 0:1], acc[:, 0:1], 1.0 / NEL,
                                        None, ALU.mult), reads=[accb],
         writes=[accb])
    k.op(DVE, lambda e: e.tensor_scalar(acc[:, 1:2], acc[:, 1:2], 1.0 / NEL,
                                        None, ALU.mult), reads=[accb],
         writes=[accb])
    k.op(DVE, lambda e: e.scalar_tensor_tensor(acc[:, 2:3], acc[:, 0:1], -1.0,
                                               acc[:, 0:1], ALU.mult, ALU.mult),
         reads=[accb], writes=[accb])
    k.op(DVE, lambda e: e.tensor_add(acc[:, 1:2], acc[:, 1:2], acc[:, 2:3]),
         reads=[accb], writes=[accb])
    ce, ceb = k.c_rows(1e-5, 16)
    k.op(ACT, lambda e: e.activation(acc[:, 1:2], acc[:, 1:2], AF.Ln, bias=ce),
         reads=[accb, ceb], writes=[accb])
    k.op(ACT, lambda e: e.activation(acc[:, 1:2], acc[:, 1:2], AF.Exp,
                                     scale=-0.5), reads=[accb], writes=[accb])
    k.op(DVE, lambda e: e.scalar_tensor_tensor(acc[:, 2:3], acc[:, 0:1], -1.0,
                                               acc[:, 1:2], ALU.mult, ALU.mult),
         reads=[accb], writes=[accb])
    for ip in range(NP4):
        p0 = ip * P4
        for rr in range(KC):
            sl = slice(rr * P4, (rr + 1) * P4)
            k.op(PE, lambda e, rr=rr: e.matmul(
                pc[:, 0:1], bmT[0:16, rr * 128:(rr + 1) * 128], acc[0:16, 1:2],
                start=True, stop=True), reads=[bmTb, accb], writes=[pcb])
            k.op(DVE, lambda e: e.tensor_copy(colAB[:, 0:1], pc[:, 0:1]),
                 reads=[pcb], writes=[colABb])
            k.op(PE, lambda e, rr=rr: e.matmul(
                pc[:, 0:1], bmT[0:16, rr * 128:(rr + 1) * 128], acc[0:16, 2:3],
                start=True, stop=True), reads=[bmTb, accb], writes=[pcb])
            k.op(DVE, lambda e: e.tensor_copy(colAB[:, 1:2], pc[:, 0:1]),
                 reads=[pcb], writes=[colABb])
            k.dma(SP, mix[:, sl], rw_hbm[rr * 128:(rr + 1) * 128, p0:p0 + P4],
                  reads=[rw_hbm_buf], writes=[mixb])
            k.op(DVE, lambda e, rr=rr: e.tensor_scalar(
                t1[:], mix[:, sl], colAB[:, 0:1], colAB[:, 1:2],
                ALU.mult, ALU.add),
                reads=[mixb], scalar_reads=[colABb], writes=[t1b])
            k.op(DVE, lambda e, rr=rr: e.tensor_scalar(
                xgn[:, sl], t1[:], ccol(CT_G, rr), ccol(CT_B, rr),
                ALU.mult, ALU.add),
                reads=[t1b], scalar_reads=[colsb], writes=[xgnb])
        for tt in range(P4 // 128):
            t0 = p0 + tt * 128
            k.dma(SP, xt[:], x_in[t0:t0 + 128, :], writes=[xtb])
            for no in range(2):
                def mmo(e, tt=tt, no=no):
                    last = None
                    for kc in range(KC):
                        last = e.matmul(po[:, 0:512],
                                        xgn[:, kc * P4 + tt * 128:kc * P4 + tt * 128 + 128],
                                        wslice(WoT, 1024, kc, no * 512, 512),
                                        start=(kc == 0), stop=(kc == KC - 1))
                    return last
                k.op(PE, mmo, reads=[xgnb, WoTb], writes=[pob])
                k.op(DVE, lambda e, no=no: e.tensor_add(
                    xt[:, no * 512:(no + 1) * 512],
                    xt[:, no * 512:(no + 1) * 512], po[:, 0:512]),
                    reads=[pob], writes=[xtb])
            k.dma(SP, x_out[t0:t0 + 128, :], xt[:], reads=[xtb])


def load_rwkv4_weights(k, sc, tag, d):
    W = {}
    for nm in ("WrT", "WkT", "WvT", "WoT"):
        W[nm] = load_weight_cm(k, f"{nm}_{tag}", d[nm], D, 1024)
    for nm, (ri, ci) in (("cols4", (128, 56)), ("bm", (128, 128)),
                         ("bmT", (16, 1024))):
        t, b = k.sb(f"{nm}_{tag}", [ri, ci], F32)
        k.dma(SP, t[:], d[nm], writes=[b])
        W[nm] = (t, b)
    return W


NE = 8
CAPG = 10240
GT = 48
W1COEF = 512


def alloc_moe_route(k, sc):
    sc.m_xt = k.sb("mr_xt", [128, D], F32)
    sc.m_xn = k.sb("mr_xn", [128, D], F32)
    sc.m_xnbf = k.sb("mr_xnbf", [128, D], BF16)
    sc.m_xncm = k.sb("mr_xncm", [128, KC * 128], BF16)
    sc.m_pl = k.ps("mr_pl", [128, 8], F32)
    sc.m_probs = k.sb("mr_probs", [128, 8], F32)
    sc.m_row = k.sb("mr_row", [128, 16], F32)
    sc.m_top = k.sb("mr_top", [128, 8], F32)
    sc.m_eq = k.sb("mr_eq", [128, 8], F32)
    sc.m_eq2 = k.sb("mr_eq2", [128, 8], F32)
    sc.m_iota = k.sb("mr_iota", [128, 8], F32)
    sc.m_ohtm = k.sb("mr_ohtm", [128, 16], BF16)
    sc.m_poh = k.ps("mr_poh", [8, 128], BF16)
    sc.m_oh0 = k.sb("mr_oh0", [8, T], F32)
    sc.m_incl = k.sb("mr_incl", [8, T], F32)
    sc.m_oh1 = k.sb("mr_oh1", [8, T], F32)
    sc.m_ones16 = k.sb("mr_ones16", [8, T // 8], F32)
    sc.m_pos = k.ps("mr_pos", [1, T // 8], F32)
    sc.m_post0 = k.sb("mr_post0", [1, T], F32)
    sc.m_post1 = k.sb("mr_post1", [1, T], F32)
    sc.m_ppos = k.ps("mr_ppos", [128, 1], F32)
    sc.m_route = k.sb("mr_route", [128, 8], F32)
    sc.m_car16 = k.sb("mr_car16", [8, 1], F32)
    sc.m_cnt16 = k.sb("mr_cnt16", [8, 2], F32)


def emit_moe_route(k, sc, x_in, counts_out, route_out):
    xt, xtb = sc.m_xt
    xn, xnb = sc.m_xn
    xnbf, xnbfb = sc.m_xnbf
    xncm, xncmb = sc.m_xncm
    pl, plb = sc.m_pl
    probs, probsb = sc.m_probs
    row, rowb = sc.m_row
    top, topb = sc.m_top
    eq, eqb = sc.m_eq
    eq2, eq2b = sc.m_eq2
    iota, iotab = sc.m_iota
    ohtm, ohtmb = sc.m_ohtm
    poh, pohb = sc.m_poh
    oh0, oh0b = sc.m_oh0
    oh1, oh1b = sc.m_oh1
    incl, inclb = sc.m_incl
    ones16, ones16b = sc.m_ones16
    pos, posb = sc.m_pos
    post0, post0b = sc.m_post0
    post1, post1b = sc.m_post1
    ppos, pposb = sc.m_ppos
    route, routeb = sc.m_route
    car16, car16b = sc.m_car16
    cnt16, cnt16b = sc.m_cnt16
    ones8c, ones8cb = k.sb("mr_ones8c", [8, 1], F32)
    k.op(DVE, lambda e: e.memset(ones8c[:], 1.0), writes=[ones8cb])
    rT, rTb = sc.m_routerT

    BIG = 1e6
    k.op(POOL, lambda e: e.iota(iota[:], [[1, 8]], base=0, channel_multiplier=0,
                                allow_small_or_imprecise_dtypes=True),
         writes=[iotab])
    k.op(DVE, lambda e: e.memset(ones16[:], 1.0), writes=[ones16b])

    for it in range(NT):
        r0 = it * 128
        k.dma(SP, xt[:], x_in[r0:r0 + 128, :], writes=[xtb])
        emit_ln_tm(k, sc, xt[:], xtb, xn[:], xnb)
        k.op(DVE, lambda e: e.tensor_copy(xnbf[:], xn[:]), reads=[xnb],
             writes=[xnbfb])
        for kc in range(KC):
            emit_transpose_128(k, sc, xnbf[:, kc * 128:(kc + 1) * 128],
                               [xnbfb], xncm[:, kc * 128:(kc + 1) * 128],
                               [xncmb])
        def mml(e):
            last = None
            for kc in range(KC):
                last = e.matmul(pl[:, 0:8], xncm[:, kc * 128:(kc + 1) * 128],
                                rT[:, kc * 8:(kc + 1) * 8],
                                start=(kc == 0), stop=(kc == KC - 1))
            return last
        k.op(PE, mml, reads=[xncmb, rTb], writes=[plb])
        k.op(DVE, lambda e: e.reduce_max(row[:, 7:8], pl[:, 0:8], axis=AX,
                                         negate=True), reads=[plb],
             writes=[rowb])
        k.op(ACT, lambda e: e.activation(probs[:], pl[:, 0:8], AF.Exp,
                                         bias=row[:, 7:8],
                                         accum_out=row[:, 6:7]),
             reads=[plb], scalar_reads=[rowb], writes=[probsb, rowb])
        k.op(DVE, lambda e: e.reciprocal(row[:, 6:7], row[:, 6:7]),
             reads=[rowb], writes=[rowb])
        k.op(DVE, lambda e: e.tensor_scalar(probs[:], probs[:], row[:, 6:7],
                                            None, ALU.mult),
             reads=[probsb], scalar_reads=[rowb], writes=[probsb])
        k.op(DVE, lambda e: e.max(top[:], probs[:]), reads=[probsb],
             writes=[topb])
        k.op(DVE, lambda e: e.tensor_scalar(eq[:], probs[:], top[:, 0:1], None,
                                            ALU.is_equal),
             reads=[probsb], scalar_reads=[topb], writes=[eqb])
        k.op(DVE, lambda e: e.scalar_tensor_tensor(eq2[:], eq[:], -BIG,
                                                   iota[:], ALU.mult, ALU.add),
             reads=[eqb, iotab], writes=[eq2b])
        k.op(DVE, lambda e: e.tensor_reduce(row[:, 2:3], eq2[:], axis=AX,
                                            op=ALU.min),
             reads=[eq2b], writes=[rowb])
        k.op(DVE, lambda e: e.tensor_scalar(row[:, 2:3], row[:, 2:3], BIG,
                                            None, ALU.add),
             reads=[rowb], writes=[rowb])
        k.op(DVE, lambda e: e.tensor_scalar(eq[:], iota[:], row[:, 2:3], None,
                                            ALU.is_equal),
             reads=[iotab], scalar_reads=[rowb], writes=[eqb])
        k.op(DVE, lambda e: e.tensor_scalar(eq2[:], probs[:], top[:, 1:2], None,
                                            ALU.is_equal),
             reads=[probsb], scalar_reads=[topb], writes=[eq2b])
        k.op(DVE, lambda e: e.tensor_sub(eq2[:], eq2[:], eq[:]),
             reads=[eq2b, eqb], writes=[eq2b])
        k.op(DVE, lambda e: e.tensor_scalar(eq2[:], eq2[:], 0.0, None,
                                            ALU.max),
             reads=[eq2b], writes=[eq2b])
        k.op(DVE, lambda e: e.scalar_tensor_tensor(eq2[:], eq2[:], -BIG,
                                                   iota[:], ALU.mult, ALU.add),
             reads=[eq2b, iotab], writes=[eq2b])
        k.op(DVE, lambda e: e.tensor_reduce(row[:, 3:4], eq2[:], axis=AX,
                                            op=ALU.min),
             reads=[eq2b], writes=[rowb])
        k.op(DVE, lambda e: e.tensor_scalar(row[:, 3:4], row[:, 3:4], BIG,
                                            None, ALU.add),
             reads=[rowb], writes=[rowb])
        k.op(DVE, lambda e: e.tensor_scalar(eq2[:], iota[:], row[:, 3:4], None,
                                            ALU.is_equal),
             reads=[iotab], scalar_reads=[rowb], writes=[eq2b])
        k.op(DVE, lambda e: e.tensor_add(row[:, 6:7], top[:, 0:1], top[:, 1:2]),
             reads=[topb], writes=[rowb])
        k.op(DVE, lambda e: e.reciprocal(row[:, 6:7], row[:, 6:7]),
             reads=[rowb], writes=[rowb])
        k.op(DVE, lambda e: e.tensor_scalar(row[:, 4:5], top[:, 0:1],
                                            row[:, 6:7], None, ALU.mult),
             reads=[topb], scalar_reads=[rowb], writes=[rowb])
        k.op(DVE, lambda e: e.tensor_scalar(row[:, 5:6], top[:, 1:2],
                                            row[:, 6:7], None, ALU.mult),
             reads=[topb], scalar_reads=[rowb], writes=[rowb])
        k.op(DVE, lambda e: e.tensor_copy(ohtm[:, 0:8], eq[:]), reads=[eqb],
             writes=[ohtmb])
        k.op(DVE, lambda e: e.tensor_copy(ohtm[:, 8:16], eq2[:]), reads=[eq2b],
             writes=[ohtmb])
        k.transpose(poh[0:8, 0:128], ohtm[:, 0:8], reads=[ohtmb], writes=[pohb])
        k.op(DVE, lambda e: e.tensor_copy(oh0[:, r0:r0 + 128], poh[0:8, 0:128]),
             reads=[pohb], writes=[oh0b])
        k.transpose(poh[0:8, 0:128], ohtm[:, 8:16], reads=[ohtmb], writes=[pohb])
        k.op(DVE, lambda e: e.tensor_copy(oh1[:, r0:r0 + 128], poh[0:8, 0:128]),
             reads=[pohb], writes=[oh1b])
        k.op(DVE, lambda e: e.tensor_copy(route[:, 0:1], row[:, 2:3]),
             reads=[rowb], writes=[routeb])
        k.op(DVE, lambda e: e.tensor_copy(route[:, 1:2], row[:, 3:4]),
             reads=[rowb], writes=[routeb])
        k.op(DVE, lambda e: e.tensor_copy(route[:, 2:3], row[:, 4:5]),
             reads=[rowb], writes=[routeb])
        k.op(DVE, lambda e: e.tensor_copy(route[:, 3:4], row[:, 5:6]),
             reads=[rowb], writes=[routeb])
        k.dma(SP, route_out[r0:r0 + 128, 0:4], route[:, 0:4], reads=[routeb])
    CH = T // 8
    for (oh, ohb, postt, posttb, cidx) in ((oh0, oh0b, post0, post0b, 0),
                                           (oh1, oh1b, post1, post1b, 1)):
        for c in range(8):
            lo = c * CH
            if c == 0:
                k.op(DVE, lambda e, oh=oh: e.tensor_tensor_scan(
                    incl[:, 0:CH], ones16[:], oh[:, 0:CH], 0.0, ALU.mult,
                    ALU.add), reads=[ones16b, ohb], writes=[inclb])
            else:
                k.op(DVE, lambda e, lo=lo: e.tensor_copy(car16[:],
                                                         incl[:, lo - 1:lo]),
                     reads=[inclb], writes=[car16b])
                k.op(DVE, lambda e, lo=lo, oh=oh: e.tensor_tensor_scan(
                    incl[:, lo:lo + CH], ones16[:], oh[:, lo:lo + CH],
                    car16[:], ALU.mult, ALU.add),
                    reads=[ones16b, ohb], scalar_reads=[car16b], writes=[inclb])
        k.op(DVE, lambda e, cidx=cidx: e.tensor_copy(cnt16[:, cidx:cidx + 1],
                                                     incl[:, T - 1:T]),
             reads=[inclb], writes=[cnt16b])
        for c in range(8):
            lo = c * CH
            k.op(DVE, lambda e, lo=lo, oh=oh: e.tensor_tensor(
                incl[:, lo:lo + CH], incl[:, lo:lo + CH], oh[:, lo:lo + CH],
                ALU.mult), reads=[inclb, ohb], writes=[inclb])
            k.op(PE, lambda e, lo=lo: e.matmul(pos[0:1, 0:CH],
                                               ones8c[0:8, 0:1],
                                               incl[:, lo:lo + CH],
                                               start=True, stop=True),
                 reads=[ones8cb, inclb], writes=[posb])
            k.op(DVE, lambda e, lo=lo, postt=postt: e.tensor_scalar(
                postt[:, lo:lo + CH], pos[0:1, 0:CH], -1.0, None, ALU.add),
                reads=[posb], writes=[posttb])
    k.dma(SP, counts_out[0:8, :], cnt16[:, 0:1], reads=[cnt16b])
    k.dma(SP, counts_out[8:16, :], cnt16[:, 1:2], reads=[cnt16b])
    for it in range(NT):
        r0 = it * 128
        k.transpose(ppos[:, 0:1], post0[0:1, r0:r0 + 128], reads=[post0b],
                    writes=[pposb])
        k.op(DVE, lambda e: e.tensor_copy(route[:, 4:5], ppos[:, 0:1]),
             reads=[pposb], writes=[routeb])
        k.transpose(ppos[:, 0:1], post1[0:1, r0:r0 + 128], reads=[post1b],
                    writes=[pposb])
        k.op(DVE, lambda e: e.tensor_copy(route[:, 5:6], ppos[:, 0:1]),
             reads=[pposb], writes=[routeb])
        k.dma(SP, route_out[r0:r0 + 128, 4:6], route[:, 4:6], reads=[routeb])


def load_moe_route_weights(k, sc, routerT_d):
    t, b = k.sb("mr_rT", [128, KC * 8], BF16)
    for kc in range(KC):
        k.dma(POOL, t[:, kc * 8:(kc + 1) * 8],
              routerT_d[kc * 128:(kc + 1) * 128, :], writes=[b])
    sc.m_routerT = (t, b)


def alloc_moe_expert(k, sc):
    sc.e_xt = k.sb("me_xt", [128, D], F32)
    sc.e_xn = k.sb("me_xn", [128, D], F32)
    sc.e_xnbf = k.sb("me_xnbf", [128, D], BF16)
    sc.e_route = k.sb("me_route", [128, 8], F32)
    sc.e_rows = k.sb("me_rows", [128, 16], F32)
    sc.e_slot = k.sb("me_slot", [128, 4], U32)
    sc.e_gk = k.sb("me_gk", [128, NT * 2], F32)
    sc.e_c1r = k.sb("me_c1r", [1, 80], F32)
    sc.e_p1 = k.ps("me_p1", [128, 512], F32)
    sc.e_p2 = k.ps("me_p2", [1, 16], F32)
    sc.e_bc8 = k.sb("me_bc8", [128, 48], F32)
    sc.e_iotaw = k.sb("me_iotaw", [128, 64], U32)
    sc.e_iotaw2 = k.sb("me_iotaw2", [128, 64], U32)
    sc.e_widx = k.sb("me_widx", [128, 64], U32)
    sc.e_w2idx = k.sb("me_w2idx", [128, 64], U32)
    sc.e_w1sb = k.sb("me_w1sb", [128, 8 * 4096], BF16)
    sc.e_w2sb = k.sb("me_w2sb", [128, 32 * 1024], BF16)
    sc.e_gin = k.sb("me_gin", [128, D], BF16)
    sc.e_gcm = k.sb("me_gcm", [128, KC * 128], BF16)
    sc.e_hcm = k.sb("me_hcm", [128, 32 * 128], BF16)
    sc.e_y = k.sb("me_y", [128, D], BF16)
    sc.e_eprev = k.sb("me_eprev", [128, 2], F32)
    sc.e_ecur = k.sb("me_ecur", [128, 2], F32)
    sc.e_py = k.ps("me_py", [128, 512], F32)
    sc.e_y0 = k.sb("me_y0", [128, D], BF16)
    sc.e_y1 = k.sb("me_y1", [128, D], BF16)


def emit_moe_expert(k, sc, x_in, x_out, route_in, countsall_in, coremask_in,
                    w1f, w2f, g0_hbm, g1_hbm, y0_hbm, y1_hbm):
    xt, xtb = sc.e_xt
    xn, xnb = sc.e_xn
    xnbf, xnbfb = sc.e_xnbf
    route, routeb = sc.e_route
    rows, rowsb = sc.e_rows
    slot, slotb = sc.e_slot
    gk, gkb = sc.e_gk
    c1r, c1rb = sc.e_c1r
    p1, p1b = sc.e_p1
    p2, p2b = sc.e_p2
    bc8, bc8b = sc.e_bc8
    iotaw, iotawb = sc.e_iotaw
    iotaw2, iotaw2b = sc.e_iotaw2
    widx, widxb = sc.e_widx
    w2idx, w2idxb = sc.e_w2idx
    w1sb, w1sbb = sc.e_w1sb
    w2sb, w2sbb = sc.e_w2sb
    gin, ginb = sc.e_gin
    gcm, gcmb = sc.e_gcm
    hcm, hcmb = sc.e_hcm
    yy, yyb = sc.e_y
    eprev, eprevb = sc.e_eprev
    ecur, ecurb = sc.e_ecur
    py, pyb = sc.e_py
    y0, y0b = sc.e_y0
    y1, y1b = sc.e_y1
    ones1f, ones1fb = sc.ones1f
    g_hbm = [g0_hbm, g1_hbm]
    y_hbm = [y0_hbm, y1_hbm]
    g_buf = [Buf("g0h"), Buf("g1h")]
    y_buf = [Buf("y0h"), Buf("y1h")]
    w1f_buf = Buf("w1f")
    w2f_buf = Buf("w2f")

    cmsb, cmsbb = k.sb("me_cmsb", [8, 2], F32)
    k.dma(SP, cmsb[:], coremask_in, writes=[cmsbb])
    casb, casbb = k.sb("me_casb", [8, 16], F32)
    k.dma(SP, casb[:], countsall_in, writes=[casbb])
    k.op(PE, lambda e: e.matmul(p2[0:1, 0:16], cmsb[:, 0:1], casb[:],
                                start=True, stop=True),
         reads=[cmsbb, casbb], writes=[p2b])
    k.op(DVE, lambda e: e.tensor_copy(c1r[0:1, 0:16], p2[0:1, 0:16]),
         reads=[p2b], writes=[c1rb])
    k.op(PE, lambda e: e.matmul(p2[0:1, 0:16], cmsb[:, 1:2], casb[:],
                                start=True, stop=True),
         reads=[cmsbb, casbb], writes=[p2b])
    k.op(DVE, lambda e: e.tensor_copy(c1r[0:1, 16:32], p2[0:1, 0:16]),
         reads=[p2b], writes=[c1rb])
    ceilu, ceilub = k.sb("me_ceilu", [1, 16], U32)
    k.op(DVE, lambda e: e.tensor_scalar(ceilu[0:1, 0:16], c1r[0:1, 16:32],
                                        1.0 / 256, 0.498046875, ALU.mult,
                                        ALU.add),
         reads=[c1rb], writes=[ceilub])
    k.op(DVE, lambda e: e.tensor_scalar(c1r[0:1, 32:48], ceilu[0:1, 0:16],
                                        256.0, None, ALU.mult),
         reads=[ceilub], writes=[c1rb])
    k.op(DVE, lambda e: e.memset(c1r[0:1, 64:80], 1.0), writes=[c1rb])
    for kk2 in range(2):
        k.op(DVE, lambda e, kk2=kk2: e.tensor_tensor_scan(
            c1r[0:1, 48 + kk2 * 8:56 + kk2 * 8], c1r[0:1, 64:72],
            c1r[0:1, 32 + kk2 * 8:40 + kk2 * 8], 0.0, ALU.mult, ALU.add),
            reads=[c1rb], writes=[c1rb])
    k.op(DVE, lambda e: e.tensor_sub(c1r[0:1, 48:64], c1r[0:1, 48:64],
                                     c1r[0:1, 32:48]),
         reads=[c1rb], writes=[c1rb])
    for i, (lo_, hi_) in enumerate(((48, 56), (56, 64), (0, 8), (8, 16))):
        k.op(PE, lambda e, lo_=lo_, hi_=hi_: e.matmul(
            p1[:, 0:8], ones1f[0:1, 0:128], c1r[0:1, lo_:hi_],
            start=True, stop=True),
            reads=[c1rb, ones1fb], writes=[p1b])
        k.op(DVE, lambda e, i=i: e.tensor_copy(bc8[:, i * 8:(i + 1) * 8],
                                               p1[:, 0:8]),
             reads=[p1b], writes=[bc8b])
    k.op(POOL, lambda e: e.iota(bc8[:, 32:40], [[1, 8]], base=0,
                                channel_multiplier=0,
                                allow_small_or_imprecise_dtypes=True),
         writes=[bc8b])
    for kc in range(8):
        for ncx in range(8):
            k.op(POOL, lambda e, kc=kc, ncx=ncx: e.iota(
                iotaw[:, kc * 8 + ncx:kc * 8 + ncx + 1], [[1, 1]],
                base=kc * 1024 + ncx, channel_multiplier=8), writes=[iotawb])
    for kc2 in range(32):
        for ncx in range(2):
            k.op(POOL, lambda e, kc2=kc2, ncx=ncx: e.iota(
                iotaw2[:, kc2 * 2 + ncx:kc2 * 2 + ncx + 1], [[1, 1]],
                base=kc2 * 256 + ncx, channel_multiplier=2), writes=[iotaw2b])

    for it in range(NT):
        r0 = it * 128
        k.dma(SP, xt[:], x_in[r0:r0 + 128, :], writes=[xtb])
        k.dma(SP, route[:], route_in[r0:r0 + 128, :], writes=[routeb])
        emit_ln_tm(k, sc, xt[:], xtb, xn[:], xnb)
        k.op(DVE, lambda e: e.tensor_copy(xnbf[:], xn[:]), reads=[xnb],
             writes=[xnbfb])
        for kk2 in range(2):
            ecol = route[:, kk2:kk2 + 1]
            gcol = route[:, 2 + kk2:3 + kk2]
            pcol = route[:, 4 + kk2:5 + kk2]
            k.op(DVE, lambda e, kk2=kk2, ecol=ecol: e.tensor_scalar(
                rows[:, 0:8], bc8[:, 32:40], ecol, None, ALU.is_equal),
                reads=[bc8b], scalar_reads=[routeb], writes=[rowsb])
            k.op(DVE, lambda e, kk2=kk2: e.tensor_tensor(
                rows[:, 8:16], rows[:, 0:8], bc8[:, kk2 * 8:(kk2 + 1) * 8],
                ALU.mult), reads=[rowsb, bc8b], writes=[rowsb])
            k.op(DVE, lambda e: e.reduce_sum(rows[:, 8:9], rows[:, 8:16],
                                             axis=AX),
                 reads=[rowsb], writes=[rowsb])
            k.op(DVE, lambda e, kk2=kk2: e.tensor_tensor(
                rows[:, 0:8], rows[:, 0:8],
                bc8[:, 16 + kk2 * 8:24 + kk2 * 8], ALU.mult),
                reads=[rowsb, bc8b], writes=[rowsb])
            k.op(DVE, lambda e: e.reduce_sum(rows[:, 9:10], rows[:, 0:8],
                                             axis=AX),
                 reads=[rowsb], writes=[rowsb])
            k.op(DVE, lambda e, kk2=kk2, pcol=pcol: e.tensor_scalar(
                slot[:, kk2:kk2 + 1], rows[:, 8:9], pcol, None, ALU.add),
                reads=[rowsb], scalar_reads=[routeb], writes=[slotb])
            k.op(DVE, lambda e, pcol=pcol: e.tensor_scalar(
                rows[:, 10:11], rows[:, 9:10], pcol, None, ALU.add),
                reads=[rowsb], scalar_reads=[routeb], writes=[rowsb])
            k.op(DVE, lambda e: e.tensor_scalar(rows[:, 10:11], rows[:, 10:11],
                                                float(CAPG), None, ALU.is_lt),
                 reads=[rowsb], writes=[rowsb])
            k.op(DVE, lambda e, kk2=kk2, it=it, gcol=gcol: e.tensor_scalar(
                gk[:, it * 2 + kk2:it * 2 + kk2 + 1], rows[:, 10:11], gcol,
                0.5, ALU.mult, ALU.mult),
                reads=[rowsb], scalar_reads=[routeb], writes=[gkb])
            k.indirect(g_hbm[kk2], IndirectOffsetOnAxis(slot[:, kk2:kk2 + 1], 0),
                       xnbf[:], None, reads=[slotb, xnbfb], writes=[g_buf[kk2]])
    k.fence_indirect()
    for kk2 in range(2):
        k.op(DVE, lambda e: e.memset(eprev[:, 0:1], -1.0), writes=[eprevb])
        for gt in range(GT):
            k.op(DVE, lambda e, gt=gt, kk2=kk2: e.tensor_scalar(
                rows[:, 0:8], bc8[:, kk2 * 8:(kk2 + 1) * 8], float(gt * 128),
                None, ALU.is_le), reads=[bc8b], writes=[rowsb])
            k.op(DVE, lambda e: e.reduce_sum(rows[:, 8:9], rows[:, 0:8],
                                             axis=AX),
                 reads=[rowsb], writes=[rowsb])
            k.op(DVE, lambda e: e.tensor_scalar(ecur[:, 0:1], rows[:, 8:9],
                                                -1.0, None, ALU.add),
                 reads=[rowsb], writes=[ecurb])
            k.op(DVE, lambda e: e.tensor_scalar(rows[:, 12:13], ecur[:, 0:1],
                                                8192.0, None, ALU.mult),
                 reads=[ecurb], writes=[rowsb])
            k.op(DVE, lambda e: e.tensor_scalar(widx[:], iotaw[:],
                                                rows[:, 12:13], None, ALU.add),
                 reads=[iotawb], scalar_reads=[rowsb], writes=[widxb])
            k.op(DVE, lambda e: e.tensor_scalar(rows[:, 13:14], ecur[:, 0:1],
                                                8192.0, None, ALU.mult),
                 reads=[ecurb], writes=[rowsb])
            k.op(DVE, lambda e: e.tensor_scalar(w2idx[:], iotaw2[:],
                                                rows[:, 13:14], None, ALU.add),
                 reads=[iotaw2b], scalar_reads=[rowsb], writes=[w2idxb])
            for kc in range(8):
                for ncx in range(8):
                    cix = kc * 8 + ncx
                    k.indirect(w1sb[:, cix * 512:(cix + 1) * 512], None, w1f,
                               IndirectOffsetOnAxis(widx[:, cix:cix + 1], 0),
                               reads=[widxb, w1f_buf], writes=[w1sbb])
            for kc2 in range(32):
                for ncx in range(2):
                    cix = kc2 * 2 + ncx
                    k.indirect(w2sb[:, cix * 512:(cix + 1) * 512], None, w2f,
                               IndirectOffsetOnAxis(w2idx[:, cix:cix + 1], 0),
                               reads=[w2idxb, w2f_buf], writes=[w2sbb])
            k.fence_indirect()
            k.dma(SP, gin[:], g_hbm[kk2][gt * 128:(gt + 1) * 128, :],
                  reads=[g_buf[kk2]], writes=[ginb])
            for kc in range(KC):
                emit_transpose_128(k, sc, gin[:, kc * 128:(kc + 1) * 128],
                                   [ginb], gcm[:, kc * 128:(kc + 1) * 128],
                                   [gcmb])
            for mo in range(32):
                def mmh(e, mo=mo):
                    last = None
                    for kc in range(KC):
                        last = e.matmul(p1[:, 0:128],
                                        w1sb[:, kc * 4096 + mo * 128:kc * 4096 + mo * 128 + 128],
                                        gcm[:, kc * 128:(kc + 1) * 128],
                                        start=(kc == 0), stop=(kc == KC - 1))
                    return last
                k.op(PE, mmh, reads=[w1sbb, gcmb], writes=[p1b])
                k.op(ACT, lambda e, mo=mo: e.activation(
                    hcm[:, mo * 128:(mo + 1) * 128], p1[:, 0:128], AF.Gelu),
                    reads=[p1b], writes=[hcmb])
            for no in range(2):
                def mmy(e, no=no):
                    last = None
                    for kc2 in range(32):
                        last = e.matmul(py[:, 0:512],
                                        hcm[:, kc2 * 128:(kc2 + 1) * 128],
                                        w2sb[:, kc2 * 1024 + no * 512:kc2 * 1024 + no * 512 + 512],
                                        start=(kc2 == 0), stop=(kc2 == 31))
                    return last
                k.op(PE, mmy, reads=[hcmb, w2sbb], writes=[pyb])
                k.op(DVE, lambda e, no=no: e.tensor_copy(
                    yy[:, no * 512:(no + 1) * 512], py[:, 0:512]),
                    reads=[pyb], writes=[yyb])
            k.dma(SP, y_hbm[kk2][gt * 128:(gt + 1) * 128, :], yy[:],
                  reads=[yyb], writes=[y_buf[kk2]])
    for it in range(NT):
        r0 = it * 128
        k.dma(SP, xt[:], x_in[r0:r0 + 128, :], writes=[xtb])
        k.dma(SP, route[:], route_in[r0:r0 + 128, :], writes=[routeb])
        for kk2 in range(2):
            ecol = route[:, kk2:kk2 + 1]
            pcol = route[:, 4 + kk2:5 + kk2]
            k.op(DVE, lambda e, ecol=ecol: e.tensor_scalar(
                rows[:, 0:8], bc8[:, 32:40], ecol, None, ALU.is_equal),
                reads=[bc8b], scalar_reads=[routeb], writes=[rowsb])
            k.op(DVE, lambda e, kk2=kk2: e.tensor_tensor(
                rows[:, 8:16], rows[:, 0:8], bc8[:, kk2 * 8:(kk2 + 1) * 8],
                ALU.mult), reads=[rowsb, bc8b], writes=[rowsb])
            k.op(DVE, lambda e: e.reduce_sum(rows[:, 8:9], rows[:, 8:16],
                                             axis=AX),
                 reads=[rowsb], writes=[rowsb])
            k.op(DVE, lambda e, kk2=kk2, pcol=pcol: e.tensor_scalar(
                slot[:, kk2:kk2 + 1], rows[:, 8:9], pcol, None, ALU.add),
                reads=[rowsb], scalar_reads=[routeb], writes=[slotb])
        k.indirect(y0[:], None, y_hbm[0],
                   IndirectOffsetOnAxis(slot[:, 0:1], 0),
                   reads=[slotb, y_buf[0]], writes=[y0b])
        k.indirect(y1[:], None, y_hbm[1],
                   IndirectOffsetOnAxis(slot[:, 1:2], 0),
                   reads=[slotb, y_buf[1]], writes=[y1b])
        k.fence_indirect()
        k.op(DVE, lambda e, it=it: e.tensor_scalar(
            xn[:], y0[:], gk[:, it * 2:it * 2 + 1], None, ALU.mult),
            reads=[y0b], scalar_reads=[gkb], writes=[xnb])
        k.op(DVE, lambda e: e.tensor_add(xt[:], xt[:], xn[:]),
             reads=[xnb], writes=[xtb])
        k.op(DVE, lambda e, it=it: e.tensor_scalar(
            xn[:], y1[:], gk[:, it * 2 + 1:it * 2 + 2], None, ALU.mult),
            reads=[y1b], scalar_reads=[gkb], writes=[xnb])
        k.op(DVE, lambda e: e.tensor_add(xt[:], xt[:], xn[:]),
             reads=[xnb], writes=[xtb])
        k.dma(SP, x_out[r0:r0 + 128, :], xt[:], reads=[xtb])



def alloc_tiny(k, sc):
    sc.t_a = k.sb("tp_a", [128, D], F32)
    sc.t_b = k.sb("tp_b", [128, D], F32)


def emit_lnpass(k, sc, x_in, y_out):
    a, ab = sc.t_a
    b, bb = sc.t_b
    for it in range(NT):
        r0 = it * 128
        k.dma(SP, a[:], x_in[r0:r0 + 128, :], writes=[ab])
        emit_ln_tm(k, sc, a[:], ab, b[:], bb)
        k.dma(SP, y_out[r0:r0 + 128, :], b[:], reads=[bb])


def emit_axpy(k, sc, x_in, y_in, out, scale=0.5):
    a, ab = sc.t_a
    b, bb = sc.t_b
    for it in range(NT):
        r0 = it * 128
        k.dma(SP, a[:], x_in[r0:r0 + 128, :], writes=[ab])
        k.dma(SP, b[:], y_in[r0:r0 + 128, :], writes=[bb])
        k.op(DVE, lambda e: e.scalar_tensor_tensor(a[:], b[:], scale, a[:],
                                                   ALU.mult, ALU.add),
             reads=[ab, bb], writes=[ab])
        k.dma(SP, out[r0:r0 + 128, :], a[:], reads=[ab])


"""Driver: builds the 8 per-sublayer programs once, runs the unfused launch
chain. kernel.py is generated by concatenating bk.py + kern.py + this file
(with sibling imports stripped)."""

_PROGS = {}


def _mk(name, build_fn):
    if name not in _PROGS:
        _PROGS[name] = build_fn()
    return _PROGS[name]


def _new_nc():
    return bass.Bass("TRN2", target_bir_lowering=False)


def _build_route():
    nc = _new_nc()
    x_d = nc.dram_tensor("x", [T, D], F32, kind="ExternalInput").ap()
    rT_d = nc.dram_tensor("rT", [D, 8], F32, kind="ExternalInput").ap()
    cnt_d = nc.dram_tensor("counts", [16, 1], F32, kind="ExternalOutput").ap()
    rt_d = nc.dram_tensor("route", [T, 8], F32, kind="ExternalOutput").ap()
    ctx = contextlib.ExitStack()
    with ctx:
        k = Kit(nc, ctx)
        sc = Scratch()
        alloc_common(k, sc)
        alloc_moe_route(k, sc)
        load_moe_route_weights(k, sc, rT_d)
        emit_moe_route(k, sc, x_d, cnt_d, rt_d)
        k.barrier()
    return nc


def _build_moeB():
    nc = _new_nc()
    x_d = nc.dram_tensor("x", [T, D], F32, kind="ExternalInput").ap()
    rt_d = nc.dram_tensor("route", [T, 8], F32, kind="ExternalInput").ap()
    ca_d = nc.dram_tensor("countsall", [8, 16], F32, kind="ExternalInput").ap()
    cm_d = nc.dram_tensor("coremask", [8, 2], F32, kind="ExternalInput").ap()
    w1_d = nc.dram_tensor("w1f", [65536, 512], F32, kind="ExternalInput").ap()
    w2_d = nc.dram_tensor("w2f", [65536, 512], F32, kind="ExternalInput").ap()
    y_d = nc.dram_tensor("y", [T, D], F32, kind="ExternalOutput").ap()
    g0 = nc.dram_tensor("g0", [GT * 128, D], BF16).ap()
    g1 = nc.dram_tensor("g1", [GT * 128, D], BF16).ap()
    yg0 = nc.dram_tensor("yg0", [GT * 128, D], BF16).ap()
    yg1 = nc.dram_tensor("yg1", [GT * 128, D], BF16).ap()
    ctx = contextlib.ExitStack()
    with ctx:
        k = Kit(nc, ctx)
        sc = Scratch()
        alloc_common(k, sc)
        alloc_moe_expert(k, sc)
        emit_moe_expert(k, sc, x_d, y_d, rt_d, ca_d, cm_d, w1_d, w2_d,
                        g0, g1, yg0, yg1)
        k.barrier()
    return nc


def _build_rwkv7():
    nc = _new_nc()
    shapes = {"Wr": (D, D), "Wk": (D, D), "Wv": (D, D), "Wo": (D, D),
              "w1": (D, 64), "a1": (D, 64), "g1": (D, 64),
              "w2": (64, 16), "a2": (64, 16), "g2": (64, D),
              "w0a0": (16, 2), "mixcols": (128, 48), "cols": (128, 32),
              "bm": (128, 128), "bmT": (16, 1024)}
    dd = {nm: nc.dram_tensor(nm, list(sh), F32, kind="ExternalInput").ap()
          for nm, sh in shapes.items()}
    x_d = nc.dram_tensor("x", [T, D], F32, kind="ExternalInput").ap()
    y_d = nc.dram_tensor("y", [T, D], F32, kind="ExternalOutput").ap()
    xn_hbm = nc.dram_tensor("xn_scr", [D, T], BF16).ap()
    ctx = contextlib.ExitStack()
    with ctx:
        k = Kit(nc, ctx)
        sc = Scratch()
        alloc_common(k, sc)
        alloc_rwkv7(k, sc)
        W = load_rwkv7_weights(k, sc, "l0", dd)
        emit_rwkv7(k, sc, x_d, y_d, W, xn_hbm, Buf("xnh"))
        k.barrier()
    return nc


def _build_rwkv4():
    nc = _new_nc()
    shapes = {"WrT": (D, D), "WkT": (D, D), "WvT": (D, D), "WoT": (D, D),
              "cols4": (128, 56), "bm": (128, 128), "bmT": (16, 1024)}
    dd = {nm: nc.dram_tensor(nm, list(sh), F32, kind="ExternalInput").ap()
          for nm, sh in shapes.items()}
    x_d = nc.dram_tensor("x", [T, D], F32, kind="ExternalInput").ap()
    y_d = nc.dram_tensor("y", [T, D], F32, kind="ExternalOutput").ap()
    xn_hbm = nc.dram_tensor("xn_scr", [D, T], BF16).ap()
    rw_hbm = nc.dram_tensor("rw_scr", [D, T], BF16).ap()
    ctx = contextlib.ExitStack()
    with ctx:
        k = Kit(nc, ctx)
        sc = Scratch()
        alloc_common(k, sc)
        alloc_rwkv4(k, sc)
        W = load_rwkv4_weights(k, sc, "l1", dd)
        emit_rwkv4(k, sc, x_d, y_d, W, xn_hbm, Buf("xnh"), rw_hbm, Buf("rwh"))
        k.barrier()
    return nc


def _build_latattn():
    nc = _new_nc()
    x_d = nc.dram_tensor("x", [T, D], F32, kind="ExternalInput").ap()
    wqkv_d = nc.dram_tensor("wqkv", [D, 3072], F32, kind="ExternalInput").ap()
    qs_d = nc.dram_tensor("qs", [64], F32, kind="ExternalInput").ap()
    ks_d = nc.dram_tensor("ks", [64], F32, kind="ExternalInput").ap()
    wo_d = nc.dram_tensor("wo", [D, 1024], F32, kind="ExternalInput").ap()
    tabs_d = nc.dram_tensor("tabs", [T, 256], F32, kind="ExternalInput").ap()
    mask_d = nc.dram_tensor("masks", [128, 3 * 384], F32,
                            kind="ExternalInput").ap()
    y_d = nc.dram_tensor("y", [T, D], F32, kind="ExternalOutput").ap()
    ctx = contextlib.ExitStack()
    with ctx:
        k = Kit(nc, ctx)
        sc = Scratch()
        alloc_common(k, sc)
        alloc_lat_attn(k, sc)
        mt, mb = k.sb("masks_sb", [128, 3 * 384], F32)
        k.dma(SP, mt[:], mask_d, writes=[mb])
        W = load_lat_attn_weights(k, sc, "l", wqkv_d, wo_d, qs_d, ks_d, tabs_d)
        emit_lat_attn(k, sc, x_d, y_d, W, (mt, mb))
        k.barrier()
    return nc


def _build_latffn():
    nc = _new_nc()
    x_d = nc.dram_tensor("x", [T, D], F32, kind="ExternalInput").ap()
    w1_d = nc.dram_tensor("w1", [D, DH2], F32, kind="ExternalInput").ap()
    b1_d = nc.dram_tensor("b1", [DH2], F32, kind="ExternalInput").ap()
    w2_d = nc.dram_tensor("w2", [DI, 1024], F32, kind="ExternalInput").ap()
    b2_d = nc.dram_tensor("b2", [1024], F32, kind="ExternalInput").ap()
    g_d = nc.dram_tensor("g", [D], F32, kind="ExternalInput").ap()
    b_d = nc.dram_tensor("b", [D], F32, kind="ExternalInput").ap()
    y_d = nc.dram_tensor("y", [T, D], F32, kind="ExternalOutput").ap()
    ctx = contextlib.ExitStack()
    with ctx:
        k = Kit(nc, ctx)
        sc = Scratch()
        alloc_common(k, sc)
        alloc_lat_ffn(k, sc)
        W = load_lat_ffn_weights(k, sc, "l", w1_d, b1_d, w2_d, b2_d, g_d, b_d)
        emit_lat_ffn(k, sc, x_d, y_d, W)
        k.barrier()
    return nc


def _build_ln():
    nc = _new_nc()
    x_d = nc.dram_tensor("x", [T, D], F32, kind="ExternalInput").ap()
    y_d = nc.dram_tensor("y", [T, D], F32, kind="ExternalOutput").ap()
    ctx = contextlib.ExitStack()
    with ctx:
        k = Kit(nc, ctx)
        sc = Scratch()
        alloc_common(k, sc)
        alloc_tiny(k, sc)
        emit_lnpass(k, sc, x_d, y_d)
        k.barrier()
    return nc


def _build_axpy():
    nc = _new_nc()
    x_d = nc.dram_tensor("x", [T, D], F32, kind="ExternalInput").ap()
    yy_d = nc.dram_tensor("yy", [T, D], F32, kind="ExternalInput").ap()
    o_d = nc.dram_tensor("o", [T, D], F32, kind="ExternalOutput").ap()
    ctx = contextlib.ExitStack()
    with ctx:
        k = Kit(nc, ctx)
        sc = Scratch()
        alloc_common(k, sc)
        alloc_tiny(k, sc)
        emit_axpy(k, sc, x_d, yy_d, o_d)
        k.barrier()
    return nc



def _host_rotary_tables():
    HDL = 64
    half = np.arange(0, HDL, 2, dtype=np.float64)
    inv_freq = 1.0 / 10000.0 ** (half / HDL)
    t = np.arange(T, dtype=np.float64)
    freqs = np.concatenate([t[:, None] * inv_freq[None, :]] * 2, -1)
    base_scale = (half + 0.4 * HDL) / (1.4 * HDL)
    power = (t - T // 2) / 128.0
    scale = base_scale[None, :] ** power[:, None]
    scale = np.concatenate([scale, scale], -1)
    cos = np.cos(freqs)
    sin = np.sin(freqs)
    sgn = np.concatenate([-np.ones(HDL // 2), np.ones(HDL // 2)])
    return np.concatenate([cos * scale, sgn * sin * scale,
                           cos / scale, sgn * sin / scale], -1).astype(np.float32)


def _host_masks():
    KEYS_ = 384
    p = np.arange(128)[:, None]
    jl = np.arange(KEYS_)[None, :]
    base = (p <= jl) & (jl <= p + 256)
    out = np.zeros((128, 3 * KEYS_), np.float32)
    for v, m in enumerate((base & (jl >= 256), base & (jl >= 128), base)):
        out[:, v * KEYS_:(v + 1) * KEYS_] = np.where(m, 0.0, -1e30)
    return out


def _host_bm():
    bm = np.zeros((128, 128), np.float32)
    bmT = np.zeros((16, 1024), np.float32)
    for rr in range(8):
        for p in range(128):
            h = 2 * rr + (1 if p >= 64 else 0)
            bm[p, rr * 16 + h] = 1.0
            bmT[h, rr * 128 + p] = 1.0
    return bm, bmT


def _cols8(vec1024):
    out = np.zeros((128, 8), np.float32)
    v = np.asarray(vec1024).reshape(-1)
    for rr in range(8):
        out[:, rr] = v[rr * 128:(rr + 1) * 128]
    return out


def _run(nc, in_maps):
    res = run_bass_kernel_spmd(nc, in_maps, core_ids=list(range(8)))
    return res.results


def kernel(**inp):
    xs = [np.ascontiguousarray(inp["L0_attn_x"][c]).astype(np.float32,
                                                           copy=False)
          for c in range(8)]
    bm, bmT = _host_bm()
    tabs = _host_rotary_tables()
    masks = _host_masks()

    rT = np.ascontiguousarray(inp["L0_ffn_router_w"].T)
    w1f = np.ascontiguousarray(inp["L0_ffn_w1"].reshape(65536, 512))
    w2f = np.ascontiguousarray(inp["L0_ffn_w2"].reshape(65536, 512))
    coremasks = []
    for c in range(8):
        cm = np.zeros((8, 2), np.float32)
        cm[:c, 0] = 1.0
        cm[c, 1] = 1.0
        coremasks.append(cm)

    mixcols = np.zeros((128, 48), np.float32)
    for m, nm in enumerate(("x_r", "x_w", "x_k", "x_v", "x_a", "x_g")):
        mixcols[:, m * 8:(m + 1) * 8] = _cols8(inp["L0_attn_" + nm])
    cols7 = np.zeros((128, 32), np.float32)
    cols7[:, 0:8] = _cols8(inp["L0_attn_k_k"])
    cols7[:, 8:16] = _cols8(inp["L0_attn_r_k"])
    cols7[:, 16:24] = _cols8(inp["L0_attn_ln_w"])
    cols7[:, 24:32] = _cols8(inp["L0_attn_ln_b"])
    r7w = {"Wr": inp["L0_attn_Wr"], "Wk": inp["L0_attn_Wk"],
           "Wv": inp["L0_attn_Wv"], "Wo": inp["L0_attn_Wo"],
           "w1": inp["L0_attn_w1"], "a1": inp["L0_attn_a1"],
           "g1": inp["L0_attn_g1"], "w2": inp["L0_attn_w2"],
           "a2": inp["L0_attn_a2"], "g2": inp["L0_attn_g2"],
           "w0a0": np.stack([inp["L0_attn_w0"].reshape(16),
                             inp["L0_attn_a0"].reshape(16)], 1),
           "mixcols": mixcols, "cols": cols7, "bm": bm, "bmT": bmT}
    r7w = {kk: np.ascontiguousarray(vv, dtype=np.float32)
           for kk, vv in r7w.items()}

    cols4 = np.zeros((128, 56), np.float32)
    cols4[:, 0:8] = _cols8(inp["L1_attn_time_mix_r"])
    cols4[:, 8:16] = _cols8(inp["L1_attn_time_mix_k"])
    cols4[:, 16:24] = _cols8(inp["L1_attn_time_mix_v"])
    cols4[:, 24:32] = _cols8(inp["L1_attn_time_decay"])
    cols4[:, 32:40] = _cols8(inp["L1_attn_time_first"])
    cols4[:, 40:48] = _cols8(inp["L1_attn_ln_gamma"])
    cols4[:, 48:56] = _cols8(inp["L1_attn_ln_beta"])
    r4w = {"WrT": np.ascontiguousarray(inp["L1_attn_W_r"].T),
           "WkT": np.ascontiguousarray(inp["L1_attn_W_k"].T),
           "WvT": np.ascontiguousarray(inp["L1_attn_W_v"].T),
           "WoT": np.ascontiguousarray(inp["L1_attn_W_o"].T),
           "cols4": cols4, "bm": bm, "bmT": bmT}
    r4w = {kk: np.ascontiguousarray(vv, dtype=np.float32)
           for kk, vv in r4w.items()}

    def moe_call(xs):
        nc = _mk("route", _build_route)
        resA = _run(nc, [{"x": xs[c], "rT": rT} for c in range(8)])
        countsall = np.concatenate([r["counts"].reshape(1, 16) for r in resA],
                                   0).astype(np.float32)
        nc = _mk("moeB", _build_moeB)
        resB = _run(nc, [{"x": xs[c], "route": resA[c]["route"],
                          "countsall": countsall, "coremask": coremasks[c],
                          "w1f": w1f, "w2f": w2f} for c in range(8)])
        return [r["y"] for r in resB]

    def rwkv7_call(xs):
        nc = _mk("rwkv7", _build_rwkv7)
        res = _run(nc, [dict(r7w, x=xs[c]) for c in range(8)])
        return [r["y"] for r in res]

    def rwkv4_call(xs):
        nc = _mk("rwkv4", _build_rwkv4)
        res = _run(nc, [dict(r4w, x=xs[c]) for c in range(8)])
        return [r["y"] for r in res]

    def lat6_call(xs):
        nc = _mk("ln", _build_ln)
        res = _run(nc, [{"x": xs[c]} for c in range(8)])
        ys = [r["y"] for r in res]
        for i in range(6):
            nc = _mk("latattn", _build_latattn)
            ins = {"wqkv": np.ascontiguousarray(inp["L1_ffn_Wqkv"][i]),
                   "qs": np.ascontiguousarray(inp["L1_ffn_q_scale"][i]),
                   "ks": np.ascontiguousarray(inp["L1_ffn_k_scale"][i]),
                   "wo": np.ascontiguousarray(inp["L1_ffn_Wo"][i]),
                   "tabs": tabs, "masks": masks}
            res = _run(nc, [dict(ins, x=ys[c]) for c in range(8)])
            ys = [r["y"] for r in res]
            nc = _mk("latffn", _build_latffn)
            ins = {"w1": np.ascontiguousarray(inp["L1_ffn_W1"][i]),
                   "b1": np.ascontiguousarray(inp["L1_ffn_b1"][i]),
                   "w2": np.ascontiguousarray(inp["L1_ffn_W2"][i]),
                   "b2": np.ascontiguousarray(inp["L1_ffn_b2"][i]),
                   "g": np.ascontiguousarray(inp["L1_ffn_ln_g"][i]),
                   "b": np.ascontiguousarray(inp["L1_ffn_ln_b"][i])}
            res = _run(nc, [dict(ins, x=ys[c]) for c in range(8)])
            ys = [r["y"] for r in res]
        nc = _mk("axpy", _build_axpy)
        res = _run(nc, [{"x": xs[c], "yy": ys[c]} for c in range(8)])
        return [r["o"] for r in res]

    xs = moe_call(xs)
    xs = rwkv7_call(xs)
    xs = moe_call(xs)
    xs = lat6_call(xs)
    xs = rwkv4_call(xs)
    xs = lat6_call(xs)
    return np.stack(xs, 0)



# version 04
# speedup vs baseline: 1.1519x; 1.1519x over previous
import contextlib
from concourse.bass import IndirectOffsetOnAxis
from concourse.bass_utils import run_bass_kernel_spmd
"""Bass kit: hazard-tracked raw-Bass program builder.

Discipline:
  - one semaphore per compute engine (inc 1, only by its own engine's ops)
  - one semaphore per DMA queue (inc 16 on completion)
  - every op declares reads=[Buf...] / writes=[Buf...]; the kit emits the
    minimal wait_ge set (same-engine program order is trusted; cross-engine
    RAW/WAR/WAW edges become semaphore waits, deduped by watermark)
  - dependency graph is acyclic by construction (deps only on earlier tokens)
"""
import numpy as np
import concourse.bass as bass
import concourse.mybir as mybir

F32 = mybir.dt.float32
BF16 = mybir.dt.bfloat16
I32 = mybir.dt.int32
U32 = mybir.dt.uint32
AF = mybir.ActivationFunctionType
ALU = mybir.AluOpType

PE, ACT, DVE, POOL, SP = "pe", "act", "dve", "pool", "sp"
COMPUTE_ENGINES = (PE, ACT, DVE, POOL, SP)
DMA_QUEUES = {SP: "dsp", POOL: "dpool", ACT: "dact"}


class Buf:
    __slots__ = ("name", "writer", "readers")

    def __init__(self, name=""):
        self.name = name
        self.writer = None
        self.readers = []


class Kit:
    def __init__(self, nc, ctx):
        self.nc = nc
        self.ctx = ctx
        self.eng = {PE: nc.tensor, ACT: nc.scalar, DVE: nc.vector,
                    POOL: nc.gpsimd, SP: nc.sync}
        self.sems = {}
        self.cnt = {}
        for name in [*COMPUTE_ENGINES, *DMA_QUEUES.values()]:
            self.sems[name] = ctx.enter_context(nc.semaphore("S_" + name))
            self.cnt[name] = 0
        self.water = {}
        self._const_tile = None
        self._const_cols = {}
        self._const_n = 0
        self._ident = None
        self._ind_dirty = []
        self._fence = None
        for s in self.sems.values():
            nc.gpsimd.sem_clear(s)
        nc.all_engine_barrier()

    def _wait(self, eng, token, force=False):
        if token is None:
            return
        semname, val = token
        if semname in ("dsp", "dpool", "dact"):
            val = self.cnt[semname]
        if semname == eng and eng != PE:
            force = True
        if semname == eng and not force:
            return
        key = (eng, semname)
        if self.water.get(key, 0) >= val:
            if not (semname == eng and force):
                return
        self.eng[eng].wait_ge(self.sems[semname], val)
        self.water[key] = val

    def _deps_for(self, reads, writes):
        deps = []
        for b in reads:
            if b.writer is not None:
                deps.append(b.writer)
        for b in writes:
            if b.writer is not None:
                deps.append(b.writer)
            deps.extend(b.readers)
        return deps

    def _commit(self, token, reads, writes):
        for b in reads:
            b.readers.append(token)
        for b in writes:
            b.writer = token
            b.readers = []

    def op(self, eng, fn, reads=(), writes=(), scalar_reads=(), deps=()):
        assert eng in COMPUTE_ENGINES
        for b in scalar_reads:
            if b.writer is not None:
                self._wait(eng, b.writer, force=True)
        for t in self._deps_for(reads, writes):
            self._wait(eng, t)
        for t in deps:
            self._wait(eng, t)
        inst = fn(self.eng[eng])
        self.cnt[eng] += 1
        inst.then_inc(self.sems[eng], 1)
        token = (eng, self.cnt[eng])
        self._commit(token, list(reads) + list(scalar_reads), writes)
        return token

    def dma(self, queue, out, in_, reads=(), writes=(), deps=(), **kw):
        dq = DMA_QUEUES[queue]
        for t in self._deps_for(reads, writes):
            self._wait(queue, t)
        for t in deps:
            self._wait(queue, t)
        inst = self.eng[queue].dma_start(out=out, in_=in_, **kw)
        self.cnt[dq] += 16
        inst.then_inc(self.sems[dq], 16)
        token = (dq, self.cnt[dq])
        self._commit(token, reads, writes)
        return token

    def indirect(self, out, out_offset, in_, in_offset, reads=(), writes=(),
                 deps=(), **kw):
        for t in self._deps_for(reads, writes):
            self._wait(POOL, t)
        for t in deps:
            self._wait(POOL, t)
        inst = self.nc.gpsimd.indirect_dma_start(out, out_offset, in_, in_offset, **kw)
        self.cnt["dpool"] += 16
        inst.then_inc(self.sems["dpool"], 16)
        token = ("dpool", self.cnt["dpool"])
        self._commit(token, reads, writes)
        self._ind_dirty.extend(writes)
        return token

    def fence_indirect(self):
        if not self._ind_dirty:
            return None
        if self._fence is None:
            ft = self.ctx.enter_context(
                self.nc.sbuf_tensor("kit_fence", [1, 16], F32))
            fd = self.nc.dram_tensor("kit_fence_d", [1, 16], F32).ap()
            self._fence = (ft, fd)
        ft, fd = self._fence
        tok = self.dma(POOL, ft[:], fd)
        for b in self._ind_dirty:
            if b.writer is not None and b.writer[0] == "dpool":
                b.writer = tok
        self._ind_dirty = []
        return tok

    def collective(self, kind, op_, ins_aps, outs_aps, reads=(), writes=(), deps=()):
        for b in reads:
            if b.writer is not None:
                self._wait(POOL, b.writer)
        for t in self._deps_for(reads, writes):
            self._wait(POOL, t)
        for t in deps:
            self._wait(POOL, t)
        inst = self.nc.gpsimd.collective_compute(
            kind, op_, replica_groups=[[0, 1, 2, 3, 4, 5, 6, 7]],
            ins=[a.opt() for a in ins_aps], outs=[a.opt() for a in outs_aps])
        self.cnt["dpool"] += 16
        inst.then_inc(self.sems["dpool"], 16)
        token = ("dpool", self.cnt["dpool"])
        self._commit(token, reads, writes)
        return token

    def barrier(self):
        for eng in COMPUTE_ENGINES:
            for semname, val in self.cnt.items():
                if val > 0 and semname != eng:
                    key = (eng, semname)
                    if self.water.get(key, 0) < val:
                        self.eng[eng].wait_ge(self.sems[semname], val)
                        self.water[key] = val

    def wait_token(self, eng, token):
        self._wait(eng, token)

    def c(self, value):
        if self._const_tile is None:
            self._const_tile = self.ctx.enter_context(
                self.nc.sbuf_tensor("kit_consts", [128, 64], F32))
        v = float(value)
        if v not in self._const_cols:
            i = self._const_n
            assert i < 64
            self._const_n += 1
            b = Buf(f"const_{v}")
            ap = self._const_tile[:, i:i + 1]
            self.op(DVE, lambda e: e.memset(ap, v), writes=[b])
            self._const_cols[v] = (ap, b)
        return self._const_cols[v]

    def c_rows(self, value, rows):
        ap, b = self.c(value)
        i = None
        for vv, (a2, b2) in self._const_cols.items():
            if b2 is b:
                i = list(self._const_cols.keys()).index(vv)
        v = float(value)
        idx = list(self._const_cols.keys()).index(v)
        return self._const_tile[0:rows, idx:idx + 1], b

    def ident(self):
        if self._ident is None:
            t = self.ctx.enter_context(
                self.nc.sbuf_tensor("kit_ident", [128, 128], BF16))
            b = Buf("ident")
            self.op(DVE, lambda e: e.memset(t[:], 1.0), writes=[b])
            self.op(POOL, lambda e: e.affine_select(
                t[:], t[:], [[-1, 128]], ALU.is_equal, 0.0,
                base=0, channel_multiplier=1), reads=[b], writes=[b])
            self._ident = (t, b)
        return self._ident

    def identf(self):
        if getattr(self, "_identf", None) is None:
            t = self.ctx.enter_context(
                self.nc.sbuf_tensor("kit_identf", [128, 128], F32))
            b = Buf("identf")
            self.op(DVE, lambda e: e.memset(t[:], 1.0), writes=[b])
            self.op(POOL, lambda e: e.affine_select(
                t[:], t[:], [[-1, 128]], ALU.is_equal, 0.0,
                base=0, channel_multiplier=1), reads=[b], writes=[b])
            self._identf = (t, b)
        return self._identf

    def transpose(self, out_ps, in_sb, reads, writes):
        if in_sb.dtype == F32:
            iap, ib = self.identf()
        else:
            iap, ib = self.ident()
        p = in_sb.shape[0]
        return self.op(PE, lambda e: e.transpose(out_ps, in_sb, iap[0:p, 0:p]),
                       reads=list(reads) + [ib], writes=writes)

    def sb(self, name, shape, dtype):
        ctx = self._scopes[-1] if getattr(self, "_scopes", None) else self.ctx
        self._nalloc = getattr(self, "_nalloc", 0) + 1
        t = ctx.enter_context(
            self.nc.sbuf_tensor(f"{name}_u{self._nalloc}", shape, dtype))
        return t, Buf(name)

    def ps(self, name, shape, dtype=F32):
        ctx = self._scopes[-1] if getattr(self, "_scopes", None) else self.ctx
        self._nalloc = getattr(self, "_nalloc", 0) + 1
        t = ctx.enter_context(
            self.nc.psum_tensor(f"{name}_u{self._nalloc}", shape, dtype))
        return t, Buf(name)

    def prealloc(self):
        self.c(0.0)
        self.ident()
        self.identf()
        if self._fence is None:
            ft = self.ctx.enter_context(
                self.nc.sbuf_tensor("kit_fence", [1, 16], F32))
            fd = self.nc.dram_tensor("kit_fence_d", [1, 16], F32).ap()
            self._fence = (ft, fd)

    def push_scope(self):
        import contextlib as _cl
        if not hasattr(self, "_scopes"):
            self._scopes = []
        self._scopes.append(_cl.ExitStack())

    def pop_scope(self):
        self.barrier()
        self._scopes.pop().close()


"""Kernel emitters for mk_macaron_d2. Development module; final kernel.py inlines this.

Layout conventions:
  - residual x per core: HBM token-major [4096, 1024] f32
  - "tm" tile: [128 tokens, C] (tokens on partitions)
  - cm data: channels on partitions, tokens on free
  - weights W (in x out) in SBUF bf16 as [128, nkc*out]: K-chunk kc at cols
    [kc*out, (kc+1)*out)
"""

AX = mybir.AxisListType.X
T = 4096
D = 1024
NT = T // 128
KC = D // 128
DI = 2730
DH2 = 2 * DI
NIC = (DI + 127) // 128
DBG = None


def ceil_div(a, b):
    return (a + b - 1) // b


class Scratch:
    pass



def load_weight_cm(k, name, w_dram, ins, outs):
    nkc = ceil_div(ins, 128)
    t, b = k.sb(name, [128, nkc * outs], BF16)
    for kc in range(nkc):
        lo = kc * 128
        hi = min(ins, lo + 128)
        k.dma(POOL, t[0:hi - lo, kc * outs:(kc + 1) * outs], w_dram[lo:hi, :],
              writes=[b])
    return t, b


def wslice(t, outs, kc, mo, mw, rows=128):
    return t[0:rows, kc * outs + mo: kc * outs + mo + mw]


def load_col_chunks(k, name, v_dram, n, dtype=F32):
    nch = ceil_div(n, 128)
    t, b = k.sb(name, [128, nch], dtype)
    if n % 128 != 0:
        k.op(DVE, lambda e: e.memset(t[:], 0.0), writes=[b])
    v2 = v_dram.rearrange("(c p) -> p c", p=128) if n % 128 == 0 else None
    if v2 is not None:
        k.dma(POOL, t[:, 0:nch], v2, writes=[b], allow_slow_non_contiguous=True)
    else:
        for c in range(nch):
            lo = c * 128
            hi = min(n, lo + 128)
            k.dma(POOL, t[0:hi - lo, c:c + 1], v_dram[lo:hi].rearrange("(p o) -> p o", o=1),
                  writes=[b], allow_slow_non_contiguous=True)
    return t, b


def bcast_row_tile(k, sc, name, v_dram, n, dtype=F32):
    row, rowb = k.sb(name + "_r", [1, n], F32)
    k.dma(SP, row[:], v_dram.rearrange("(o n) -> o n", o=1), writes=[rowb])
    t, b = k.sb(name, [128, n], dtype)
    pb, pbb = sc.pbc
    ones, onesb = sc.ones1f
    for c in range(ceil_div(n, 512)):
        lo = c * 512
        hi = min(n, lo + 512)
        k.op(PE, lambda e, lo=lo, hi=hi: e.matmul(
            pb[:, 0:hi - lo], ones[0:1, 0:128], row[0:1, lo:hi],
            start=True, stop=True), reads=[rowb, onesb], writes=[pbb])
        k.op(DVE, lambda e, lo=lo, hi=hi: e.tensor_copy(t[:, lo:hi],
                                                        pb[:, 0:hi - lo]),
             reads=[pbb], writes=[b])
    return t, b


def alloc_common(k, sc):
    k.prealloc()
    sc.stat = k.sb("sc_stat", [128, 4], F32)
    sc.sq = k.sb("sc_sq", [128, D], F32)
    sc.pst = k.ps("sc_pst", [128, 128], BF16)
    sc.pbc = k.ps("sc_pbc", [128, 512], F32)
    o, ob = k.sb("sc_ones1f", [1, 128], F32)
    k.op(DVE, lambda e: e.memset(o[:], 1.0), writes=[ob])
    sc.ones1f = (o, ob)
    o2, o2b = k.sb("sc_ones1bf", [1, 128], BF16)
    k.op(DVE, lambda e: e.memset(o2[:], 1.0), writes=[o2b])
    sc.ones1bf = (o2, o2b)


def emit_ln_tm(k, sc, x_ap, x_buf, out_ap, out_buf, eps=1e-5, nch=D):
    st, stb = sc.stat
    sq, sqb = sc.sq
    k.op(ACT, lambda e: e.activation(sq[:, 0:nch], x_ap, AF.Square,
                                     accum_out=st[:, 1:2]),
         reads=[x_buf], writes=[sqb, stb])
    k.op(DVE, lambda e: e.reduce_sum(st[:, 0:1], x_ap, axis=AX),
         reads=[x_buf], writes=[stb])
    k.op(DVE, lambda e: e.tensor_scalar(st[:, 0:1], st[:, 0:1], 1.0 / nch, None,
                                        ALU.mult), scalar_reads=[stb], writes=[stb])
    k.op(DVE, lambda e: e.scalar_tensor_tensor(st[:, 2:3], st[:, 0:1], -1.0,
                                               st[:, 0:1], ALU.mult, ALU.mult),
         scalar_reads=[stb], writes=[stb])
    k.op(DVE, lambda e: e.tensor_scalar(st[:, 3:4], st[:, 1:2], 1.0 / nch,
                                        None, ALU.mult), scalar_reads=[stb], writes=[stb])
    k.op(DVE, lambda e: e.tensor_tensor(st[:, 3:4], st[:, 3:4], st[:, 2:3],
                                        ALU.add), scalar_reads=[stb], writes=[stb])
    c_eps, c_epsb = k.c(eps)
    k.op(ACT, lambda e: e.activation(st[:, 3:4], st[:, 3:4], AF.Ln, bias=c_eps),
         reads=[stb, c_epsb], scalar_reads=[stb], writes=[stb])
    k.op(ACT, lambda e: e.activation(st[:, 3:4], st[:, 3:4], AF.Exp, scale=-0.5),
         reads=[stb], writes=[stb])
    k.op(DVE, lambda e: e.tensor_scalar(out_ap, x_ap, st[:, 0:1], st[:, 3:4],
                                        ALU.subtract, ALU.mult),
         reads=[x_buf], scalar_reads=[stb], writes=[out_buf])


def emit_transpose_128(k, sc, src_ap, src_bufs, dst_ap, dst_bufs):
    pst, pstb = sc.pst
    p, n = src_ap.shape
    k.transpose(pst[0:n, 0:p], src_ap, reads=src_bufs, writes=[pstb])
    k.op(DVE, lambda e: e.tensor_copy(dst_ap, pst[0:n, 0:p]),
         reads=[pstb], writes=dst_bufs)



def alloc_lat_ffn(k, sc):
    sc.xs = k.sb("lf_xs", [128, D], F32)
    sc.xn = k.sb("lf_xn", [128, D], F32)
    sc.xnbf = k.sb("lf_xnbf", [128, D], BF16)
    sc.xncm = k.sb("lf_xncm", [128, KC * 128], BF16)
    sc.hg = k.ps("lf_hg", [128, 128], F32)
    sc.acm = k.sb("lf_acm", [128, NIC * 128], BF16)
    sc.gcm = k.sb("lf_gcm", [128, NIC * 128], BF16)
    sc.po = k.ps("lf_po", [128, 1024], F32)
    sc.xt = k.sb("lf_xt", [128, D], F32)


def emit_lat_ffn(k, sc, x_in, x_out, W):
    xs, xsb = sc.xs
    xn, xnb = sc.xn
    xnbf, xnbfb = sc.xnbf
    xncm, xncmb = sc.xncm
    hg, hgb = sc.hg
    acm, acmb = sc.acm
    gcm, gcmb = sc.gcm
    po, pob = sc.po
    xt, xtb = sc.xt
    w1t, w1b = W["w1"]
    b1t, b1bb = W["b1c"]
    w2t, w2b = W["w2"]
    b2t, b2bb = W["b2r"]
    gB, gBb = W["gammaB"]
    bB, bBb = W["betaB"]
    ones1, ones1b = sc.ones1bf

    for it in range(NT):
        r0 = it * 128
        k.dma(SP, xt[:], x_in[r0:r0 + 128, :], writes=[xtb])
        k.dma(SP, xs[:, 0:D // 2], x_in[r0:r0 + 128, 0:D // 2], writes=[xsb])
        if it == 0:
            k.op(DVE, lambda e: e.memset(xs[0:1, D // 2:D], 0.0), writes=[xsb])
            k.dma(SP, xs[1:128, D // 2:D], x_in[0:127, D // 2:D], writes=[xsb])
        else:
            k.dma(SP, xs[:, D // 2:D], x_in[r0 - 1:r0 + 127, D // 2:D],
                  writes=[xsb])
        emit_ln_tm(k, sc, xs[:], xsb, xn[:], xnb)
        k.op(DVE, lambda e: e.tensor_tensor(xn[:], xn[:], gB[:], ALU.mult),
             reads=[xnb, gBb], writes=[xnb])
        k.op(DVE, lambda e: e.tensor_tensor(xn[:], xn[:], bB[:], ALU.add),
             reads=[xnb, bBb], writes=[xnb])
        k.op(DVE, lambda e: e.tensor_copy(xnbf[:], xn[:]), reads=[xnb],
             writes=[xnbfb])
        if DBG is not None and it == 0:
            DBG[0].dma(SP, DBG[1], xn[:], reads=[xnb])
            DBG[0].dma(SP, DBG[2], xs[:], reads=[xsb])
        for kc in range(KC):
            emit_transpose_128(k, sc, xnbf[:, kc * 128:(kc + 1) * 128],
                               [xnbfb], xncm[:, kc * 128:(kc + 1) * 128],
                               [xncmb])
        for half, base in ((0, 0), (1, DI)):
            for i in range(NIC):
                m0 = base + i * 128
                mw = min(128, base + DI - m0)
                def mm(e, m0=m0, mw=mw):
                    last = None
                    for kc in range(KC):
                        last = e.matmul(hg[0:mw, 0:128],
                                        wslice(w1t, DH2, kc, m0, mw),
                                        xncm[:, kc * 128:(kc + 1) * 128],
                                        start=(kc == 0), stop=(kc == KC - 1))
                    return last
                k.op(PE, mm, reads=[w1b, xncmb], writes=[hgb])
                colb = half * NIC + i
                if half == 0:
                    k.op(DVE, lambda e, i=i, mw=mw, colb=colb: e.tensor_scalar(
                        acm[0:mw, i * 128:i * 128 + 128], hg[0:mw, 0:128],
                        b1t[0:mw, colb:colb + 1], None, ALU.add),
                        reads=[hgb], scalar_reads=[b1bb], writes=[acmb])
                else:
                    k.op(ACT, lambda e, i=i, mw=mw, colb=colb: e.activation(
                        gcm[0:mw, i * 128:i * 128 + 128], hg[0:mw, 0:128],
                        AF.Gelu, bias=b1t[0:mw, colb:colb + 1]),
                        reads=[hgb], scalar_reads=[b1bb], writes=[gcmb])
        k.op(DVE, lambda e: e.tensor_tensor(gcm[:], acm[:], gcm[:], ALU.mult),
             reads=[acmb, gcmb], writes=[gcmb])
        def mm2(e):
            last = None
            for no in (0, 512):
                for i in range(NIC):
                    mw = min(128, DI - i * 128)
                    last = e.matmul(po[:, no:no + 512],
                                    gcm[0:mw, i * 128:i * 128 + 128],
                                    wslice(w2t, 1024, i, no, 512, rows=mw),
                                    start=(i == 0), stop=False)
                last = e.matmul(po[:, no:no + 512], ones1[0:1, 0:128],
                                b2t[0:1, no:no + 512], start=False, stop=True)
            return last
        k.op(PE, mm2, reads=[gcmb, w2b, b2bb, ones1b], writes=[pob])
        k.op(DVE, lambda e: e.tensor_add(xt[:], xt[:], po[:, 0:1024]),
             reads=[pob], writes=[xtb])
        k.dma(SP, x_out[r0:r0 + 128, :], xt[:], reads=[xtb])


def load_lat_ffn_weights(k, sc, layer_tag, w1_d, b1_d, w2_d, b2_d, g_d, b_d):
    W = {}
    W["w1"] = load_weight_cm(k, f"w1_{layer_tag}", w1_d, D, DH2)
    W["w2"] = load_weight_cm(k, f"w2_{layer_tag}", w2_d, DI, 1024)
    b1t, b1b = k.sb(f"b1c_{layer_tag}", [128, 2 * NIC], F32)
    k.op(DVE, lambda e: e.memset(b1t[:], 0.0), writes=[b1b])
    for half in range(2):
        for i in range(NIC):
            lo = half * DI + i * 128
            hi = min(half * DI + DI, lo + 128)
            c = half * NIC + i
            k.dma(POOL, b1t[0:hi - lo, c:c + 1],
                  b1_d[lo:hi].rearrange("(p o) -> p o", o=1), writes=[b1b],
                  allow_slow_non_contiguous=True)
    W["b1c"] = (b1t, b1b)
    b2t, b2b = k.sb(f"b2r_{layer_tag}", [1, 1024], BF16)
    k.dma(POOL, b2t[:], b2_d.rearrange("(o n) -> o n", o=1), writes=[b2b])
    W["b2r"] = (b2t, b2b)
    W["gammaB"] = bcast_row_tile(k, sc, f"gB_{layer_tag}", g_d, D)
    W["betaB"] = bcast_row_tile(k, sc, f"bB_{layer_tag}", b_d, D)
    return W


NH = 16
HD = 64
KEYS = 384


def alloc_lat_attn(k, sc, ctx=None):
    sc.a_xs = k.sb("la_xs", [128, D], F32)
    sc.a_xsbf = k.sb("la_xsbf", [128, D], BF16)
    sc.a_xscm = k.sb("la_xscm", [128, KC * 128], BF16)
    sc.a_pqkv = k.ps("la_pqkv", [128, 512], F32)
    sc.a_qt = k.sb("la_qt", [128, D], F32)
    sc.a_kt = k.sb("la_kt", [128, D], F32)
    sc.a_sq = k.sb("la_sq", [128, D], F32)
    sc.a_ss = k.sb("la_ss", [128, NH], F32)
    sc.a_t1 = k.sb("la_t1", [128, D], F32)
    sc.a_t2 = k.sb("la_t2", [128, D], F32)
    sc.a_qbf = k.sb("la_qbf", [128, D], BF16)
    sc.a_kbf = k.sb("la_kbf", [128, D], BF16)
    sc.a_qcm = k.sb("la_qcm", [128, KC * 128], BF16)
    sc.a_kring = [k.sb(f"la_kring{j}", [128, KC * 128], BF16) for j in range(3)]
    sc.a_vring = [k.sb(f"la_vring{j}", [128, D], BF16) for j in range(3)]
    sc.a_psim = k.ps("la_psim", [128, KEYS], F32)
    sc.a_smask = k.sb("la_smask", [128, KEYS], F32)
    sc.a_nmax = k.sb("la_nmax", [128, 2], F32)
    sc.a_ebf = k.sb("la_ebf", [128, KEYS], BF16)
    sc.a_et = k.sb("la_et", [128, 3 * 128], BF16)
    sc.a_pav = k.ps("la_pav", [64, 128], F32)
    sc.a_acm = k.sb("la_acm", [128, KC * 128], BF16)
    sc.a_po = k.ps("la_po", [128, 512], F32)
    sc.a_xt = k.sb("la_xt", [128, D], F32)
    sc.a_tabs = k.sb("la_tabs", [128, 4 * HD], F32)


def emit_lat_attn(k, sc, x_in, x_out, W, masks):
    xs, xsb = sc.a_xs
    xsbf, xsbfb = sc.a_xsbf
    xscm, xscmb = sc.a_xscm
    pqkv, pqkvb = sc.a_pqkv
    qt, qtb = sc.a_qt
    kt, ktb = sc.a_kt
    sq, sqb = sc.a_sq
    ss, ssb = sc.a_ss
    t1, t1b = sc.a_t1
    t2, t2b = sc.a_t2
    qbf, qbfb = sc.a_qbf
    kbf, kbfb = sc.a_kbf
    qcm, qcmb = sc.a_qcm
    psim, psimb = sc.a_psim
    smask, smaskb = sc.a_smask
    nmax, nmaxb = sc.a_nmax
    ebf, ebfb = sc.a_ebf
    et, etb = sc.a_et
    pav, pavb = sc.a_pav
    acm, acmb = sc.a_acm
    po, pob = sc.a_po
    xt, xtb = sc.a_xt
    tabs, tabsb = sc.a_tabs
    wqkv, wqkvb = W["wqkv"]
    wo, wob = W["wo"]
    qsb_t, qsb_b = W["qsb"]
    ksb_t, ksb_b = W["ksb"]
    m_t, m_b = masks
    tabs_d = W["tabs_d"]

    for j in range(3):
        k.op(DVE, lambda e, j=j: e.memset(sc.a_kring[j][0][:], 0.0),
             writes=[sc.a_kring[j][1]])
        k.op(DVE, lambda e, j=j: e.memset(sc.a_vring[j][0][:], 0.0),
             writes=[sc.a_vring[j][1]])

    def bh(ap_):
        return ap_.rearrange("p (h d) -> p h d", h=NH)

    for it in range(NT):
        r0 = it * 128
        vt, vtb = sc.a_vring[it % 3]
        kcm, kcmb = sc.a_kring[it % 3]
        k.dma(SP, xt[:], x_in[r0:r0 + 128, :], writes=[xtb])
        k.dma(SP, xs[:, 0:D // 2], x_in[r0:r0 + 128, 0:D // 2], writes=[xsb])
        if it == 0:
            k.op(DVE, lambda e: e.memset(xs[0:1, D // 2:D], 0.0), writes=[xsb])
            k.dma(SP, xs[1:128, D // 2:D], x_in[0:127, D // 2:D], writes=[xsb])
        else:
            k.dma(SP, xs[:, D // 2:D], x_in[r0 - 1:r0 + 127, D // 2:D],
                  writes=[xsb])
        k.dma(SP, tabs[:], tabs_d[r0:r0 + 128, :], writes=[tabsb])
        k.op(DVE, lambda e: e.tensor_copy(xsbf[:], xs[:]), reads=[xsb],
             writes=[xsbfb])
        for kc in range(KC):
            emit_transpose_128(k, sc, xsbf[:, kc * 128:(kc + 1) * 128],
                               [xsbfb], xscm[:, kc * 128:(kc + 1) * 128],
                               [xscmb])
        for nc6 in range(6):
            def mmq(e, nc6=nc6):
                last = None
                for kc in range(KC):
                    last = e.matmul(pqkv[:, 0:512],
                                    xscm[:, kc * 128:(kc + 1) * 128],
                                    wslice(wqkv, 3072, kc, nc6 * 512, 512),
                                    start=(kc == 0), stop=(kc == KC - 1))
                return last
            k.op(PE, mmq, reads=[xscmb, wqkvb], writes=[pqkvb])
            if nc6 < 2:
                k.op(DVE, lambda e, nc6=nc6: e.tensor_copy(
                    qt[:, nc6 * 512:(nc6 + 1) * 512], pqkv[:, 0:512]),
                    reads=[pqkvb], writes=[qtb])
            elif nc6 < 4:
                k.op(DVE, lambda e, nc6=nc6: e.tensor_copy(
                    kt[:, (nc6 - 2) * 512:(nc6 - 1) * 512], pqkv[:, 0:512]),
                    reads=[pqkvb], writes=[ktb])
            else:
                k.op(DVE, lambda e, nc6=nc6: e.tensor_copy(
                    vt[:, (nc6 - 4) * 512:(nc6 - 3) * 512], pqkv[:, 0:512]),
                    reads=[pqkvb], writes=[vtb])
        for (src_t, src_b, dst_t, dst_b, scl, sclb, atab, btab) in (
                (qt, qtb, qbf, qbfb, qsb_t, qsb_b, 0, 1),
                (kt, ktb, kbf, kbfb, ksb_t, ksb_b, 2, 3)):
            k.op(ACT, lambda e, s=src_t: e.activation(sq[:], s[:], AF.Square),
                 reads=[src_b], writes=[sqb])
            k.op(DVE, lambda e: e.reduce_sum(ss[:], bh(sq[:]), axis=AX),
                 reads=[sqb], writes=[ssb])
            k.op(ACT, lambda e: e.activation(ss[:], ss[:], AF.Sqrt),
                 reads=[ssb], writes=[ssb])
            c12, c12b = k.c(1e-12)
            k.op(DVE, lambda e: e.tensor_scalar(ss[:], ss[:], c12, None,
                                                ALU.max),
                 reads=[ssb], scalar_reads=[c12b], writes=[ssb])
            k.op(DVE, lambda e: e.reciprocal(ss[:], ss[:]), reads=[ssb],
                 writes=[ssb])
            k.op(DVE, lambda e, s=src_t: e.tensor_tensor(
                bh(t1[:]), bh(s[:]),
                ss[:].rearrange("p (h o) -> p h o", o=1).broadcast_to((128, NH, HD)),
                ALU.mult), reads=[src_b, ssb], writes=[t1b])
            k.op(DVE, lambda e, sc_=scl: e.tensor_tensor(
                bh(t1[:]), bh(t1[:]),
                sc_[:].rearrange("p (o d) -> p o d", o=1).broadcast_to((128, NH, HD)),
                ALU.mult), reads=[t1b, sclb], writes=[t1b])
            atap = tabs[:, atab * HD:(atab + 1) * HD]
            btap = tabs[:, btab * HD:(btab + 1) * HD]
            k.op(DVE, lambda e, at=atap: e.tensor_tensor(
                bh(sq[:]), bh(t1[:]),
                at.rearrange("p (o d) -> p o d", o=1).broadcast_to((128, NH, HD)),
                ALU.mult), reads=[t1b, tabsb], writes=[sqb])
            swap = bass.AP(tensor=t1[:].tensor, offset=HD // 2,
                           ap=[list(t1[:].ap[0]), [HD, NH], [-(HD // 2), 2],
                               [1, HD // 2]])
            k.op(DVE, lambda e, bt=btap, sw=swap: e.tensor_tensor(
                t2[:].rearrange("p (h s j) -> p h s j", h=NH, s=2), sw,
                bt.rearrange("p (o s j) -> p o s j", o=1, s=2)
                  .broadcast_to((128, NH, 2, HD // 2)),
                ALU.mult), reads=[t1b, tabsb], writes=[t2b])
            k.op(DVE, lambda e, d=dst_t: e.tensor_tensor(d[:], sq[:], t2[:],
                                                         ALU.add),
                 reads=[sqb, t2b], writes=[dst_b])
        for h in range(NH):
            pr = (h % 2) * 64
            cc = (h // 2) * 128
            emit_transpose_128(k, sc, qbf[:, h * HD:(h + 1) * HD], [qbfb],
                               qcm[pr:pr + 64, cc:cc + 128], [qcmb])
            emit_transpose_128(k, sc, kbf[:, h * HD:(h + 1) * HD], [kbfb],
                               kcm[pr:pr + 64, cc:cc + 128], [kcmb])
        mvar = 2 if it >= 2 else it
        for h in range(NH):
            pr = (h % 2) * 64
            cc = (h // 2) * 128
            def mms(e, pr=pr, cc=cc):
                last = None
                for j in range(3):
                    jt = it - 2 + j
                    kr, krb = sc.a_kring[jt % 3]
                    last = e.matmul(psim[:, j * 128:(j + 1) * 128],
                                    qcm[pr:pr + 64, cc:cc + 128],
                                    kr[pr:pr + 64, cc:cc + 128],
                                    start=True, stop=True)
                return last
            k.op(PE, mms, reads=[qcmb] + [sc.a_kring[j][1] for j in range(3)],
                 writes=[psimb])
            k.op(DVE, lambda e: e.scalar_tensor_tensor(
                smask[:], psim[:, 0:KEYS], 8.0,
                m_t[:, mvar * KEYS:(mvar + 1) * KEYS], ALU.mult, ALU.add),
                reads=[psimb, m_b], writes=[smaskb])
            k.op(DVE, lambda e: e.reduce_max(nmax[:, 0:1], smask[:], axis=AX,
                                             negate=True),
                 reads=[smaskb], writes=[nmaxb])
            k.op(ACT, lambda e: e.activation(ebf[:], smask[:], AF.Exp,
                                             bias=nmax[:, 0:1],
                                             accum_out=nmax[:, 1:2]),
                 reads=[smaskb], scalar_reads=[nmaxb], writes=[ebfb, nmaxb])
            k.op(DVE, lambda e: e.reciprocal(nmax[:, 1:2], nmax[:, 1:2]),
                 reads=[nmaxb], writes=[nmaxb])
            k.op(DVE, lambda e: e.tensor_scalar(ebf[:], ebf[:], nmax[:, 1:2],
                                                None, ALU.mult),
                 reads=[ebfb], scalar_reads=[nmaxb], writes=[ebfb])
            for j in range(3):
                emit_transpose_128(k, sc, ebf[:, j * 128:(j + 1) * 128],
                                   [ebfb], et[:, j * 128:(j + 1) * 128], [etb])
            def mmv(e, h=h):
                last = None
                for j in range(3):
                    jt = it - 2 + j
                    vr, vrb = sc.a_vring[jt % 3]
                    last = e.matmul(pav[:, 0:128], vr[:, h * HD:(h + 1) * HD],
                                    et[:, j * 128:(j + 1) * 128],
                                    start=(j == 0), stop=(j == 2))
                return last
            k.op(PE, mmv, reads=[etb] + [sc.a_vring[j][1] for j in range(3)],
                 writes=[pavb])
            k.op(DVE, lambda e, pr=pr, cc=cc: e.tensor_copy(
                acm[pr:pr + 64, cc:cc + 128], pav[:, 0:128]),
                reads=[pavb], writes=[acmb])
        for no in range(2):
            def mmo(e, no=no):
                last = None
                for kc in range(KC):
                    last = e.matmul(po[:, 0:512],
                                    acm[:, kc * 128:(kc + 1) * 128],
                                    wslice(wo, 1024, kc, no * 512, 512),
                                    start=(kc == 0), stop=(kc == KC - 1))
                return last
            k.op(PE, mmo, reads=[acmb, wob], writes=[pob])
            k.op(DVE, lambda e, no=no: e.tensor_add(
                xt[:, no * 512:(no + 1) * 512], xt[:, no * 512:(no + 1) * 512],
                po[:, 0:512]), reads=[pob], writes=[xtb])
        k.dma(SP, x_out[r0:r0 + 128, :], xt[:], reads=[xtb])


def load_lat_attn_weights(k, sc, tag, wqkv_d, wo_d, qs_rep_d, ks_rep_d, tabs_d):
    W = {}
    W["wqkv"] = load_weight_cm(k, f"wqkv_{tag}", wqkv_d, D, 3072)
    W["wo"] = load_weight_cm(k, f"wo_{tag}", wo_d, D, 1024)
    W["qsb"] = bcast_row_tile(k, sc, f"qsb_{tag}", qs_rep_d, HD)
    W["ksb"] = bcast_row_tile(k, sc, f"ksb_{tag}", ks_rep_d, HD)
    W["tabs_d"] = tabs_d
    return W


P7 = 256
NP7 = T // P7


def alloc_rwkv7(k, sc):
    sc.r_xnp = k.sb("r7_xnp", [128, KC * P7], BF16)
    sc.r_xnsh = k.sb("r7_xnsh", [128, KC * P7], BF16)
    sc.r_stage = k.sb("r7_stage", [128, KC * 128], BF16)
    sc.r_xt = k.sb("r7_xt", [128, D], F32)
    sc.r_xn = k.sb("r7_xn", [128, D], F32)
    sc.r_xnbf = k.sb("r7_xnbf", [128, D], BF16)
    sc.r_mix = k.sb("r7_mix", [128, KC * P7], BF16)
    sc.r_sx = k.sb("r7_sx", [128, KC * P7], F32)
    sc.r_pw = k.ps("r7_pw", [128, P7], F32)
    sc.r_pw2 = k.ps("r7_pw2", [128, P7], F32)
    sc.r_plr = k.ps("r7_plr", [64, P7], F32)
    sc.r_ph = k.ps("r7_ph", [16, P7], F32)
    sc.r_r = k.sb("r7_r", [128, KC * P7], F32)
    sc.r_k = k.sb("r7_k", [128, KC * P7], F32)
    sc.r_v = k.sb("r7_v", [128, KC * P7], F32)
    sc.r_g = k.sb("r7_g", [128, KC * P7], BF16)
    sc.r_kv = k.sb("r7_kv", [128, P7], F32)
    sc.r_st = k.sb("r7_st", [128, KC * P7], F32)
    sc.r_carry = k.sb("r7_carry", [128, KC], F32)
    sc.r_lr = k.sb("r7_lr", [64, P7], BF16)
    sc.r_hh = k.sb("r7_hh", [16, 8 * P7], F32)
    sc.r_wf = k.sb("r7_wf", [128, P7], F32)
    sc.r_xgn = k.sb("r7_xgn", [128, KC * P7], BF16)
    sc.r_po = k.ps("r7_po", [128, 512], F32)
    sc.r_sqp = k.sb("r7_sqp", [128, P7], F32)


def emit_rwkv7(k, sc, x_in, x_out, W, xn_hbm, xn_hbm_buf):
    xnp, xnpb = sc.r_xnp
    xnsh, xnshb = sc.r_xnsh
    stage, stageb = sc.r_stage
    xt, xtb = sc.r_xt
    xn, xnb = sc.r_xn
    xnbf, xnbfb = sc.r_xnbf
    mix, mixb = sc.r_mix
    sx, sxb = sc.r_sx
    pw, pwb = sc.r_pw
    pw2, pw2b = sc.r_pw2
    plr, plrb = sc.r_plr
    ph, phb = sc.r_ph
    r_, rb = sc.r_r
    kx, kxb = sc.r_k
    vx, vxb = sc.r_v
    gx, gxb = sc.r_g
    kv, kvb = sc.r_kv
    stt, sttb = sc.r_st
    carry, carryb = sc.r_carry
    lr, lrb = sc.r_lr
    hh, hhb = sc.r_hh
    wf, wfb = sc.r_wf
    xgn, xgnb = sc.r_xgn
    po, pob = sc.r_po
    sqp, sqpb = sc.r_sqp
    Wr, Wrb = W["Wr"]; Wk, Wkb = W["Wk"]; Wv, Wvb = W["Wv"]; Wo, Wob = W["Wo"]
    w1, w1b = W["w1"]; a1, a1b = W["a1"]; g1, g1b = W["g1"]
    w2, w2b = W["w2"]; a2, a2b = W["a2"]; g2, g2b = W["g2"]
    w0a0, w0a0b = W["w0a0"]
    mixc, mixcb = W["mixcols"]
    cols, colsb = W["cols"]
    bm, bmb = W["bm"]
    rkm, rkmb = W["rkm"]

    def HS(i):
        return hh[:, i * P7:(i + 1) * P7]
    HW, HA, HRS, HRA, HBC, HM, HR2, HT = range(8)

    for it in range(NT):
        r0 = it * 128
        k.dma(SP, xt[:], x_in[r0:r0 + 128, :], writes=[xtb])
        emit_ln_tm(k, sc, xt[:], xtb, xn[:], xnb)
        k.op(DVE, lambda e: e.tensor_copy(xnbf[:], xn[:]), reads=[xnb],
             writes=[xnbfb])
        for kc in range(KC):
            emit_transpose_128(k, sc, xnbf[:, kc * 128:(kc + 1) * 128],
                               [xnbfb], stage[:, kc * 128:(kc + 1) * 128],
                               [stageb])
        for kc in range(KC):
            k.dma(SP, xn_hbm[kc * 128:(kc + 1) * 128, r0:r0 + 128],
                  stage[:, kc * 128:(kc + 1) * 128], reads=[stageb],
                  writes=[xn_hbm_buf])
    for rr in range(KC):
        k.op(DVE, lambda e, rr=rr: e.tensor_scalar(
            rkm[:, rr * 16:(rr + 1) * 16], bm[:, rr * 16:(rr + 1) * 16],
            cols[:, 8 + rr:8 + rr + 1], None, ALU.mult),
            reads=[bmb], scalar_reads=[colsb], writes=[rkmb])
    k.op(DVE, lambda e: e.memset(carry[:], 0.0), writes=[carryb])

    def mixpanel(p0, mcol):
        for rr in range(KC):
            k.op(DVE, lambda e, rr=rr, mcol=mcol: e.scalar_tensor_tensor(
                mix[:, rr * P7:(rr + 1) * P7], sx[:, rr * P7:(rr + 1) * P7],
                mixc[:, mcol * 8 + rr:mcol * 8 + rr + 1],
                xnp[:, rr * P7:(rr + 1) * P7], ALU.mult, ALU.add),
                reads=[sxb, xnpb], scalar_reads=[mixcb], writes=[mixb])

    def lowrank1(wt, wtb):
        def mm(e):
            last = None
            for kc in range(KC):
                last = e.matmul(plr[:, 0:P7], wslice(wt, 64, kc, 0, 64),
                                mix[:, kc * P7:(kc + 1) * P7],
                                start=(kc == 0), stop=(kc == KC - 1))
            return last
        k.op(PE, mm, reads=[wtb, mixb], writes=[plrb])
        k.op(ACT, lambda e: e.activation(lr[:], plr[:, 0:P7], AF.Tanh),
             reads=[plrb], writes=[lrb])

    def bcast16(dst_psum, dst_psum_b, src_row_ap, rr):
        bmT, bmTb = W["bmT"]
        k.op(PE, lambda e: e.matmul(dst_psum[:, 0:P7],
                                    bmT[0:16, rr * 128:(rr + 1) * 128],
                                    src_row_ap, start=True, stop=True),
             reads=[bmTb, hhb], writes=[dst_psum_b])

    for ip in range(NP7):
        p0 = ip * P7
        for rr in range(KC):
            k.dma(SP, xnp[:, rr * P7:(rr + 1) * P7],
                  xn_hbm[rr * 128:(rr + 1) * 128, p0:p0 + P7],
                  reads=[xn_hbm_buf], writes=[xnpb])
            if ip == 0:
                k.op(DVE, lambda e, rr=rr: e.memset(
                    xnsh[:, rr * P7:rr * P7 + 1], 0.0), writes=[xnshb])
                k.dma(SP, xnsh[:, rr * P7 + 1:(rr + 1) * P7],
                      xn_hbm[rr * 128:(rr + 1) * 128, 0:P7 - 1],
                      reads=[xn_hbm_buf], writes=[xnshb])
            else:
                k.dma(SP, xnsh[:, rr * P7:(rr + 1) * P7],
                      xn_hbm[rr * 128:(rr + 1) * 128, p0 - 1:p0 + P7 - 1],
                      reads=[xn_hbm_buf], writes=[xnshb])
            k.op(DVE, lambda e, rr=rr: e.tensor_sub(
                sx[:, rr * P7:(rr + 1) * P7], xnp[:, rr * P7:(rr + 1) * P7],
                xnsh[:, rr * P7:(rr + 1) * P7]), reads=[xnpb, xnshb],
                writes=[sxb])
        for (mcol, w_t, w_b, dst, dstb) in (
                (0, Wr, Wrb, r_, rb), (2, Wk, Wkb, kx, kxb),
                (3, Wv, Wvb, vx, vxb)):
            mixpanel(p0, mcol)
            for rr in range(KC):
                def mm(e, rr=rr, w_t=w_t):
                    last = None
                    for kc in range(KC):
                        last = e.matmul(pw[:, 0:P7],
                                        wslice(w_t, 1024, kc, rr * 128, 128),
                                        mix[:, kc * P7:(kc + 1) * P7],
                                        start=(kc == 0), stop=(kc == KC - 1))
                    return last
                k.op(PE, mm, reads=[w_b, mixb], writes=[pwb])
                k.op(DVE, lambda e, rr=rr, dst=dst: e.tensor_copy(
                    dst[:, rr * P7:(rr + 1) * P7], pw[:, 0:P7]),
                    reads=[pwb], writes=[dstb])
        mixpanel(p0, 3)
        v1t, v1tb = W["v1"]
        v2t, v2tb = W["v2"]
        lowrank1(v1t, v1tb)
        for rr in range(KC):
            k.op(PE, lambda e, rr=rr: e.matmul(
                pw[:, 0:P7], wslice(v2t, 1024, 0, rr * 128, 128, rows=64),
                lr[:], start=True, stop=True), reads=[v2tb, lrb], writes=[pwb])
            k.op(ACT, lambda e, rr=rr: e.activation(
                sqp[:, 0:P7], pw[:, 0:P7], AF.Sigmoid,
                bias=cols[:, 32 + rr:32 + rr + 1]),
                reads=[pwb], scalar_reads=[colsb], writes=[sqpb])
            k.op(DVE, lambda e, rr=rr: e.tensor_sub(
                kv[:, 0:P7], vx[:, rr * P7:(rr + 1) * P7],
                vx[:, rr * P7:(rr + 1) * P7]), reads=[vxb], writes=[kvb])
            k.op(DVE, lambda e: e.tensor_tensor(kv[:, 0:P7], kv[:, 0:P7],
                                                sqp[:, 0:P7], ALU.mult),
                 reads=[kvb, sqpb], writes=[kvb])
            k.op(DVE, lambda e, rr=rr: e.tensor_add(
                vx[:, rr * P7:(rr + 1) * P7], vx[:, rr * P7:(rr + 1) * P7],
                kv[:, 0:P7]), reads=[vxb, kvb], writes=[vxb])
        mixpanel(p0, 1)
        lowrank1(w1, w1b)
        k.op(PE, lambda e: e.matmul(ph[:, 0:P7], w2[0:64, 0:16], lr[:],
                                    start=True, stop=True),
             reads=[w2b, lrb], writes=[phb])
        k.op(DVE, lambda e: e.tensor_scalar(HS(HW), ph[:, 0:P7],
                                            w0a0[0:16, 0:1], None, ALU.add),
             reads=[phb], scalar_reads=[w0a0b], writes=[hhb])
        zs = HS(HW)
        ts_ = HS(HT)
        k.op(DVE, lambda e: e.tensor_scalar(ts_, zs, -1.0, None, ALU.mult),
             reads=[hhb], writes=[hhb])
        k.op(DVE, lambda e: e.tensor_max(ts_, ts_, zs), reads=[hhb],
             writes=[hhb])
        k.op(ACT, lambda e: e.activation(ts_, ts_, AF.Exp, scale=-1.0),
             reads=[hhb], writes=[hhb])
        c1, c1b = k.c_rows(1.0, 16)
        k.op(ACT, lambda e: e.activation(ts_, ts_, AF.Ln, bias=c1),
             reads=[hhb, c1b], writes=[hhb])
        k.op(DVE, lambda e: e.tensor_scalar(zs, zs, -1.0, 0.0, ALU.mult,
                                            ALU.max), reads=[hhb], writes=[hhb])
        k.op(DVE, lambda e: e.tensor_add(zs, zs, ts_), reads=[hhb], writes=[hhb])
        cm05, cm05b = k.c_rows(-0.5, 16)
        k.op(ACT, lambda e: e.activation(zs, zs, AF.Exp, scale=-1.0, bias=cm05),
             reads=[hhb, cm05b], writes=[hhb])
        mixpanel(p0, 4)
        lowrank1(a1, a1b)
        k.op(PE, lambda e: e.matmul(ph[:, 0:P7], a2[0:64, 0:16], lr[:],
                                    start=True, stop=True),
             reads=[a2b, lrb], writes=[phb])
        k.op(ACT, lambda e: e.activation(HS(HA), ph[:, 0:P7], AF.Sigmoid,
                                         bias=w0a0[0:16, 1:2]),
             reads=[phb], scalar_reads=[w0a0b], writes=[hhb])
        mixpanel(p0, 5)
        lowrank1(g1, g1b)
        for rr in range(KC):
            k.op(PE, lambda e, rr=rr: e.matmul(
                pw[:, 0:P7], wslice(g2, 1024, 0, rr * 128, 128, rows=64),
                lr[:], start=True, stop=True), reads=[g2b, lrb], writes=[pwb])
            k.op(ACT, lambda e, rr=rr: e.activation(
                gx[:, rr * P7:(rr + 1) * P7], pw[:, 0:P7], AF.Sigmoid),
                reads=[pwb], writes=[gxb])
        kkx, kkxb = kx, kxb
        for rr in range(KC):
            k.op(DVE, lambda e, rr=rr: e.tensor_scalar(
                kkx[:, rr * P7:(rr + 1) * P7], kx[:, rr * P7:(rr + 1) * P7],
                cols[:, rr:rr + 1], None, ALU.mult),
                reads=[kxb], scalar_reads=[colsb], writes=[kkxb])
        for rr in range(KC):
            k.op(ACT, lambda e, rr=rr: e.activation(
                sqp[:, 0:P7], kkx[:, rr * P7:(rr + 1) * P7], AF.Square),
                reads=[kkxb], writes=[sqpb])
            k.op(PE, lambda e, rr=rr: e.matmul(
                ph[:, 0:P7], bm[:, rr * 16:(rr + 1) * 16], sqp[:, 0:P7],
                start=(rr == 0), stop=(rr == KC - 1)),
                reads=[bmb, sqpb], writes=[phb])
        rs = HS(HRS)
        k.op(ACT, lambda e: e.activation(rs, ph[:, 0:P7], AF.Sqrt),
             reads=[phb], writes=[hhb])
        c12, c12b = k.c_rows(1e-12, 16)
        k.op(DVE, lambda e: e.tensor_scalar(rs, rs, c12, None, ALU.max),
             reads=[hhb], scalar_reads=[c12b], writes=[hhb])
        k.op(DVE, lambda e: e.reciprocal(rs, rs), reads=[hhb], writes=[hhb])
        k.op(DVE, lambda e: e.tensor_tensor(HS(HRA), rs, HS(HA), ALU.mult),
             reads=[hhb], writes=[hhb])
        for rr in range(KC):
            k.op(DVE, lambda e, rr=rr: e.tensor_tensor(
                sqp[:, 0:P7], r_[:, rr * P7:(rr + 1) * P7],
                kkx[:, rr * P7:(rr + 1) * P7], ALU.mult),
                reads=[rb, kkxb], writes=[sqpb])
            k.op(PE, lambda e, rr=rr: e.matmul(
                ph[:, 0:P7], rkm[:, rr * 16:(rr + 1) * 16], sqp[:, 0:P7],
                start=(rr == 0), stop=(rr == KC - 1)),
                reads=[rkmb, sqpb], writes=[phb])
        k.op(DVE, lambda e: e.tensor_tensor(HS(HBC), ph[:, 0:P7], rs, ALU.mult),
             reads=[phb, hhb], writes=[hhb])
        for rr in range(KC):
            bcast16(pw, pwb, HS(HRA), rr)
            k.op(DVE, lambda e, rr=rr: e.tensor_tensor(
                kv[:, 0:P7], kkx[:, rr * P7:(rr + 1) * P7],
                vx[:, rr * P7:(rr + 1) * P7], ALU.mult),
                reads=[kkxb, vxb], writes=[kvb])
            k.op(DVE, lambda e: e.tensor_tensor(kv[:, 0:P7], kv[:, 0:P7],
                                                pw[:, 0:P7], ALU.mult),
                 reads=[kvb, pwb], writes=[kvb])
            bcast16(pw2, pw2b, HS(HW), rr)
            k.op(DVE, lambda e: e.tensor_copy(wf[:, 0:P7], pw2[:, 0:P7]),
                 reads=[pw2b], writes=[wfb])
            k.op(DVE, lambda e, rr=rr: e.tensor_tensor_scan(
                stt[:, rr * P7:(rr + 1) * P7], wf[:, 0:P7], kv[:, 0:P7],
                carry[:, rr:rr + 1], ALU.mult, ALU.add),
                reads=[wfb, kvb], scalar_reads=[carryb], writes=[sttb])
            k.op(DVE, lambda e, rr=rr: e.tensor_copy(
                carry[:, rr:rr + 1], stt[:, (rr + 1) * P7 - 1:(rr + 1) * P7]),
                reads=[sttb], writes=[carryb])
        for rr in range(KC):
            bcast16(pw, pwb, HS(HBC), rr)
            k.op(DVE, lambda e, rr=rr: e.tensor_tensor(
                stt[:, rr * P7:(rr + 1) * P7], stt[:, rr * P7:(rr + 1) * P7],
                r_[:, rr * P7:(rr + 1) * P7], ALU.mult),
                reads=[sttb, rb], writes=[sttb])
            k.op(DVE, lambda e, rr=rr: e.tensor_tensor(
                sqp[:, 0:P7], vx[:, rr * P7:(rr + 1) * P7], pw[:, 0:P7],
                ALU.mult), reads=[vxb, pwb], writes=[sqpb])
            k.op(DVE, lambda e, rr=rr: e.tensor_add(
                stt[:, rr * P7:(rr + 1) * P7], stt[:, rr * P7:(rr + 1) * P7],
                sqp[:, 0:P7]), reads=[sttb, sqpb], writes=[sttb])
        for rr in range(KC):
            k.op(PE, lambda e, rr=rr: e.matmul(
                ph[:, 0:P7], bm[:, rr * 16:(rr + 1) * 16],
                stt[:, rr * P7:(rr + 1) * P7],
                start=(rr == 0), stop=(rr == KC - 1)),
                reads=[bmb, sttb], writes=[phb])
        k.op(DVE, lambda e: e.tensor_scalar(HS(HM), ph[:, 0:P7], 1.0 / 64, None,
                                            ALU.mult), reads=[phb], writes=[hhb])
        for rr in range(KC):
            k.op(ACT, lambda e, rr=rr: e.activation(
                sqp[:, 0:P7], stt[:, rr * P7:(rr + 1) * P7], AF.Square),
                reads=[sttb], writes=[sqpb])
            k.op(PE, lambda e, rr=rr: e.matmul(
                ph[:, 0:P7], bm[:, rr * 16:(rr + 1) * 16], sqp[:, 0:P7],
                start=(rr == 0), stop=(rr == KC - 1)),
                reads=[bmb, sqpb], writes=[phb])
        k.op(DVE, lambda e: e.scalar_tensor_tensor(HS(HT), HS(HM), -1.0,
                                                   HS(HM), ALU.mult, ALU.mult),
             reads=[hhb], writes=[hhb])
        k.op(DVE, lambda e: e.tensor_scalar(HS(HR2), ph[:, 0:P7], 1.0 / 64,
                                            None, ALU.mult),
             reads=[phb], writes=[hhb])
        k.op(DVE, lambda e: e.tensor_add(HS(HR2), HS(HR2), HS(HT)),
             reads=[hhb], writes=[hhb])
        ceps, cepsb = k.c_rows(1e-5, 16)
        k.op(ACT, lambda e: e.activation(HS(HR2), HS(HR2), AF.Ln, bias=ceps),
             reads=[hhb, cepsb], writes=[hhb])
        k.op(ACT, lambda e: e.activation(HS(HR2), HS(HR2), AF.Exp, scale=-0.5),
             reads=[hhb], writes=[hhb])
        k.op(DVE, lambda e: e.scalar_tensor_tensor(HS(HT), HS(HM), -1.0,
                                                   HS(HR2), ALU.mult, ALU.mult),
             reads=[hhb], writes=[hhb])
        for rr in range(KC):
            bcast16(pw, pwb, HS(HR2), rr)
            bcast16(pw2, pw2b, HS(HT), rr)
            k.op(DVE, lambda e, rr=rr: e.tensor_tensor(
                sqp[:, 0:P7], stt[:, rr * P7:(rr + 1) * P7], pw[:, 0:P7],
                ALU.mult), reads=[sttb, pwb], writes=[sqpb])
            k.op(DVE, lambda e, rr=rr: e.tensor_add(
                sqp[:, 0:P7], sqp[:, 0:P7], pw2[:, 0:P7]),
                reads=[sqpb, pw2b], writes=[sqpb])
            k.op(DVE, lambda e, rr=rr: e.tensor_scalar(
                sqp[:, 0:P7], sqp[:, 0:P7], cols[:, 16 + rr:16 + rr + 1],
                cols[:, 24 + rr:24 + rr + 1], ALU.mult, ALU.add),
                reads=[sqpb], scalar_reads=[colsb], writes=[sqpb])
            k.op(DVE, lambda e, rr=rr: e.tensor_tensor(
                xgn[:, rr * P7:(rr + 1) * P7], sqp[:, 0:P7],
                gx[:, rr * P7:(rr + 1) * P7], ALU.mult),
                reads=[sqpb, gxb], writes=[xgnb])
        for tt in range(P7 // 128):
            t0 = p0 + tt * 128
            k.dma(SP, xt[:], x_in[t0:t0 + 128, :], writes=[xtb])
            for no in range(2):
                def mmo(e, tt=tt, no=no):
                    last = None
                    for kc in range(KC):
                        last = e.matmul(po[:, 0:512],
                                        xgn[:, kc * P7 + tt * 128:kc * P7 + tt * 128 + 128],
                                        wslice(Wo, 1024, kc, no * 512, 512),
                                        start=(kc == 0), stop=(kc == KC - 1))
                    return last
                k.op(PE, mmo, reads=[xgnb, Wob], writes=[pob])
                k.op(DVE, lambda e, no=no: e.tensor_add(
                    xt[:, no * 512:(no + 1) * 512],
                    xt[:, no * 512:(no + 1) * 512], po[:, 0:512]),
                    reads=[pob], writes=[xtb])
            k.dma(SP, x_out[t0:t0 + 128, :], xt[:], reads=[xtb])


def load_rwkv7_weights(k, sc, tag, d):
    W = {}
    for nm in ("Wr", "Wk", "Wv", "Wo"):
        W[nm] = load_weight_cm(k, f"{nm}_{tag}", d[nm], D, 1024)
    for nm in ("w1", "a1", "g1", "v1"):
        W[nm] = load_weight_cm(k, f"{nm}_{tag}", d[nm], D, 64)
    for nm, (ri, ci) in (("w2", (64, 16)), ("a2", (64, 16)), ("g2", (64, 1024)),
                         ("v2", (64, 1024))):
        t, b = k.sb(f"{nm}_{tag}", [ri, ci], BF16)
        k.dma(POOL, t[:], d[nm], writes=[b])
        W[nm] = (t, b)
    for nm, (ri, ci) in (("w0a0", (16, 2)), ("mixcols", (128, 48)),
                         ("cols", (128, 40)), ("bm", (128, 128)),
                         ("bmT", (16, 1024))):
        t, b = k.sb(f"{nm}_{tag}", [ri, ci], F32)
        k.dma(SP, t[:], d[nm], writes=[b])
        W[nm] = (t, b)
    t, b = k.sb(f"rkm_{tag}", [128, 128], F32)
    W["rkm"] = (t, b)
    return W


P4 = 256
NP4 = T // P4


def alloc_rwkv4(k, sc):
    sc.q_xnp = k.sb("r4_xnp", [128, KC * P4], BF16)
    sc.q_xnsh = k.sb("r4_xnsh", [128, KC * P4], BF16)
    sc.q_stage = k.sb("r4_stage", [128, KC * 128], BF16)
    sc.q_xt = k.sb("r4_xt", [128, D], F32)
    sc.q_xn = k.sb("r4_xn", [128, D], F32)
    sc.q_xnbf = k.sb("r4_xnbf", [128, D], BF16)
    sc.q_mix = k.sb("r4_mix", [128, KC * P4], BF16)
    sc.q_pw = k.ps("r4_pw", [128, P4], F32)
    sc.q_r = k.sb("r4_r", [128, KC * P4], F32)
    sc.q_eK = k.sb("r4_eK", [128, KC * P4], F32)
    sc.q_v = k.sb("r4_v", [128, KC * P4], F32)
    sc.q_bb = k.sb("r4_bb", [128, KC * P4], F32)
    sc.q_bbp = k.sb("r4_bbp", [128, P4], F32)
    sc.q_aa = k.sb("r4_aa", [128, KC * P4], F32)
    sc.q_aap = k.sb("r4_aap", [128, P4], F32)
    sc.q_t1 = k.sb("r4_t1", [128, P4], F32)
    sc.q_t2 = k.sb("r4_t2", [128, P4], F32)
    sc.q_t3 = k.sb("r4_t3", [128, P4], F32)
    sc.q_ewf = k.sb("r4_ewf", [128, KC * P4], F32)
    sc.q_carB = k.sb("r4_carB", [128, KC], F32)
    sc.q_carA = k.sb("r4_carA", [128, KC], F32)
    sc.q_carB2 = k.sb("r4_carB2", [128, KC], F32)
    sc.q_carA2 = k.sb("r4_carA2", [128, KC], F32)
    sc.q_ph = k.ps("r4_ph", [16, P4], F32)
    sc.q_acc = k.sb("r4_acc", [16, 4], F32)
    sc.q_dcol = k.sb("r4_dcol", [128, 16], F32)
    sc.q_xgn = k.sb("r4_xgn", [128, KC * P4], BF16)
    sc.q_po = k.ps("r4_po", [128, 512], F32)
    sc.q_colAB = k.sb("r4_colAB", [128, 2], F32)
    sc.q_pc = k.ps("r4_pc", [128, 1], F32)


def emit_rwkv4(k, sc, x_in, x_out, W, xn_hbm, xn_hbm_buf, rw_hbm, rw_hbm_buf):
    xnp, xnpb = sc.q_xnp
    xnsh, xnshb = sc.q_xnsh
    stage, stageb = sc.q_stage
    xt, xtb = sc.q_xt
    xn, xnb = sc.q_xn
    xnbf, xnbfb = sc.q_xnbf
    mix, mixb = sc.q_mix
    pw, pwb = sc.q_pw
    r_, rb = sc.q_r
    eK, eKb = sc.q_eK
    vx, vxb = sc.q_v
    bb, bbb = sc.q_bb
    bbp, bbpb = sc.q_bbp
    aa, aab = sc.q_aa
    aap, aapb = sc.q_aap
    t1, t1b = sc.q_t1
    t2, t2b = sc.q_t2
    t3, t3b = sc.q_t3
    ewf, ewfb = sc.q_ewf
    carB, carBb = sc.q_carB
    carA, carAb = sc.q_carA
    carB2, carB2b = sc.q_carB2
    carA2, carA2b = sc.q_carA2
    ph, phb = sc.q_ph
    acc, accb = sc.q_acc
    dcol, dcolb = sc.q_dcol
    xgn, xgnb = sc.q_xgn
    po, pob = sc.q_po
    colAB, colABb = sc.q_colAB
    pc, pcb = sc.q_pc
    WrT, WrTb = W["WrT"]; WkT, WkTb = W["WkT"]
    WvT, WvTb = W["WvT"]; WoT, WoTb = W["WoT"]
    cols, colsb = W["cols4"]
    bm, bmb = W["bm"]
    bmT, bmTb = W["bmT"]

    CT_TMR, CT_TMK, CT_TMV, CT_TD, CT_TF, CT_G, CT_B = range(7)

    def ccol(which, rr):
        return cols[:, which * 8 + rr:which * 8 + rr + 1]

    for it in range(NT):
        r0 = it * 128
        k.dma(SP, xt[:], x_in[r0:r0 + 128, :], writes=[xtb])
        emit_ln_tm(k, sc, xt[:], xtb, xn[:], xnb)
        k.op(DVE, lambda e: e.tensor_copy(xnbf[:], xn[:]), reads=[xnb],
             writes=[xnbfb])
        for kc in range(KC):
            emit_transpose_128(k, sc, xnbf[:, kc * 128:(kc + 1) * 128],
                               [xnbfb], stage[:, kc * 128:(kc + 1) * 128],
                               [stageb])
        for kc in range(KC):
            k.dma(SP, xn_hbm[kc * 128:(kc + 1) * 128, r0:r0 + 128],
                  stage[:, kc * 128:(kc + 1) * 128], reads=[stageb],
                  writes=[xn_hbm_buf])
    for rr in range(KC):
        k.op(ACT, lambda e, rr=rr: e.activation(dcol[:, rr:rr + 1],
                                                ccol(CT_TD, rr), AF.Exp),
             reads=[colsb], writes=[dcolb])
        k.op(ACT, lambda e, rr=rr: e.activation(dcol[:, rr:rr + 1],
                                                dcol[:, rr:rr + 1], AF.Exp,
                                                scale=-1.0),
             reads=[dcolb], writes=[dcolb])
        k.op(ACT, lambda e, rr=rr: e.activation(dcol[:, 8 + rr:8 + rr + 1],
                                                ccol(CT_TF, rr), AF.Exp),
             reads=[colsb], writes=[dcolb])
        k.op(DVE, lambda e, rr=rr: e.memset(ewf[:, rr * P4:(rr + 1) * P4], 1.0),
             writes=[ewfb])
        k.op(DVE, lambda e, rr=rr: e.tensor_scalar(
            ewf[:, rr * P4:(rr + 1) * P4], ewf[:, rr * P4:(rr + 1) * P4],
            dcol[:, rr:rr + 1], None, ALU.mult),
            reads=[ewfb], scalar_reads=[dcolb], writes=[ewfb])
    k.op(DVE, lambda e: e.memset(carB[:], 0.0), writes=[carBb])
    k.op(DVE, lambda e: e.memset(carA[:], 0.0), writes=[carAb])
    k.op(DVE, lambda e: e.memset(carB2[:], 0.0), writes=[carB2b])
    k.op(DVE, lambda e: e.memset(carA2[:], 0.0), writes=[carA2b])
    k.op(DVE, lambda e: e.memset(acc[:], 0.0), writes=[accb])

    def mixpanel(which):
        for rr in range(KC):
            k.op(DVE, lambda e, rr=rr: e.tensor_scalar(
                mix[:, rr * P4:(rr + 1) * P4], xnsh[:, rr * P4:(rr + 1) * P4],
                ccol(which, rr), None, ALU.mult),
                reads=[xnshb], scalar_reads=[colsb], writes=[mixb])
            k.op(DVE, lambda e, rr=rr: e.tensor_sub(
                mix[:, rr * P4:(rr + 1) * P4], xnsh[:, rr * P4:(rr + 1) * P4],
                mix[:, rr * P4:(rr + 1) * P4]), reads=[xnshb, mixb],
                writes=[mixb])
            k.op(DVE, lambda e, rr=rr: e.scalar_tensor_tensor(
                mix[:, rr * P4:(rr + 1) * P4], xnp[:, rr * P4:(rr + 1) * P4],
                ccol(which, rr), mix[:, rr * P4:(rr + 1) * P4],
                ALU.mult, ALU.add),
                reads=[xnpb, mixb], scalar_reads=[colsb], writes=[mixb])

    for ip in range(NP4):
        p0 = ip * P4
        for rr in range(KC):
            k.dma(SP, xnp[:, rr * P4:(rr + 1) * P4],
                  xn_hbm[rr * 128:(rr + 1) * 128, p0:p0 + P4],
                  reads=[xn_hbm_buf], writes=[xnpb])
            if ip == 0:
                k.op(DVE, lambda e, rr=rr: e.memset(
                    xnsh[:, rr * P4:rr * P4 + 1], 0.0), writes=[xnshb])
                k.dma(SP, xnsh[:, rr * P4 + 1:(rr + 1) * P4],
                      xn_hbm[rr * 128:(rr + 1) * 128, 0:P4 - 1],
                      reads=[xn_hbm_buf], writes=[xnshb])
            else:
                k.dma(SP, xnsh[:, rr * P4:(rr + 1) * P4],
                      xn_hbm[rr * 128:(rr + 1) * 128, p0 - 1:p0 + P4 - 1],
                      reads=[xn_hbm_buf], writes=[xnshb])
        for (which, w_t, w_b, dst, dstb, act) in (
                (CT_TMR, WrT, WrTb, r_, rb, None),
                (CT_TMK, WkT, WkTb, eK, eKb, AF.Exp),
                (CT_TMV, WvT, WvTb, vx, vxb, None)):
            mixpanel(which)
            for rr in range(KC):
                def mm(e, rr=rr, w_t=w_t):
                    last = None
                    for kc in range(KC):
                        last = e.matmul(pw[:, 0:P4],
                                        wslice(w_t, 1024, kc, rr * 128, 128),
                                        mix[:, kc * P4:(kc + 1) * P4],
                                        start=(kc == 0), stop=(kc == KC - 1))
                    return last
                k.op(PE, mm, reads=[w_b, mixb], writes=[pwb])
                if act is None:
                    k.op(DVE, lambda e, rr=rr, dst=dst: e.tensor_copy(
                        dst[:, rr * P4:(rr + 1) * P4], pw[:, 0:P4]),
                        reads=[pwb], writes=[dstb])
                else:
                    k.op(ACT, lambda e, rr=rr, dst=dst, act=act: e.activation(
                        dst[:, rr * P4:(rr + 1) * P4], pw[:, 0:P4], act),
                        reads=[pwb], writes=[dstb])
        for rr in range(KC):
            sl = slice(rr * P4, (rr + 1) * P4)
            ew_sl = ewf[:, sl]
            k.op(DVE, lambda e: e.tensor_tensor_scan(
                bb[:, sl], ew_sl, eK[:, sl], carB[:, rr:rr + 1],
                ALU.mult, ALU.add),
                reads=[ewfb, eKb], scalar_reads=[carBb], writes=[bbb])
            k.op(DVE, lambda e: e.tensor_copy(bbp[:, 1:P4], bb[:, rr * P4:(rr + 1) * P4 - 1]),
                 reads=[bbb], writes=[bbpb])
            k.op(DVE, lambda e: e.tensor_copy(bbp[:, 0:1], carB[:, rr:rr + 1]),
                 reads=[carBb], writes=[bbpb])
            k.op(DVE, lambda e: e.tensor_copy(carB[:, rr:rr + 1],
                                              bb[:, (rr + 1) * P4 - 1:(rr + 1) * P4]),
                 reads=[bbb], writes=[carBb])
            k.op(DVE, lambda e: e.tensor_tensor(t1[:], ew_sl, bbp[:], ALU.mult),
                 reads=[ewfb, bbpb], writes=[t1b])
            k.op(DVE, lambda e: e.tensor_max(t2[:], t1[:], eK[:, sl]),
                 reads=[t1b, eKb], writes=[t2b])
            k.op(DVE, lambda e: e.reciprocal(t2[:], t2[:]), reads=[t2b],
                 writes=[t2b])
            k.op(DVE, lambda e: e.tensor_tensor(t1[:], t1[:], t2[:], ALU.mult),
                 reads=[t1b, t2b], writes=[t1b])
            k.op(DVE, lambda e: e.tensor_tensor(t3[:], eK[:, sl], t2[:],
                                                ALU.mult),
                 reads=[eKb, t2b], writes=[t3b])
            k.op(DVE, lambda e: e.tensor_tensor(t3[:], t3[:], vx[:, sl],
                                                ALU.mult),
                 reads=[t3b, vxb], writes=[t3b])
            k.op(DVE, lambda e: e.tensor_tensor_scan(
                aa[:, sl], t1[:], t3[:], carA[:, rr:rr + 1], ALU.mult, ALU.add),
                reads=[t1b, t3b], scalar_reads=[carAb], writes=[aab])
            k.op(DVE, lambda e: e.tensor_copy(aap[:, 1:P4], aa[:, rr * P4:(rr + 1) * P4 - 1]),
                 reads=[aab], writes=[aapb])
            k.op(DVE, lambda e: e.tensor_copy(aap[:, 0:1], carA[:, rr:rr + 1]),
                 reads=[carAb], writes=[aapb])
            k.op(DVE, lambda e: e.tensor_copy(carA[:, rr:rr + 1],
                                              aa[:, (rr + 1) * P4 - 1:(rr + 1) * P4]),
                 reads=[aab], writes=[carAb])
            k.op(DVE, lambda e, rr=rr: e.tensor_scalar(
                t1[:], eK[:, sl], dcol[:, 8 + rr:8 + rr + 1], None, ALU.mult),
                reads=[eKb], scalar_reads=[dcolb], writes=[t1b])
            k.op(DVE, lambda e: e.tensor_tensor(t2[:], aap[:], bbp[:], ALU.mult),
                 reads=[aapb, bbpb], writes=[t2b])
            k.op(DVE, lambda e: e.tensor_tensor(t3[:], t1[:], vx[:, sl],
                                                ALU.mult),
                 reads=[t1b, vxb], writes=[t3b])
            k.op(DVE, lambda e: e.tensor_add(t2[:], t2[:], t3[:]),
                 reads=[t2b, t3b], writes=[t2b])
            k.op(DVE, lambda e: e.tensor_max(t3[:], bbp[:], t1[:]),
                 reads=[bbpb, t1b], writes=[t3b])
            k.op(DVE, lambda e: e.tensor_add(t1[:], t1[:], bbp[:]),
                 reads=[t1b, bbpb], writes=[t1b])
            k.op(DVE, lambda e: e.scalar_tensor_tensor(t1[:], t3[:], 1e-8,
                                                       t1[:], ALU.mult, ALU.add),
                 reads=[t3b, t1b], writes=[t1b])
            k.op(DVE, lambda e: e.reciprocal(t1[:], t1[:]), reads=[t1b],
                 writes=[t1b])
            k.op(DVE, lambda e: e.tensor_tensor(t2[:], t2[:], t1[:], ALU.mult),
                 reads=[t2b, t1b], writes=[t2b])
            k.op(ACT, lambda e: e.activation(t1[:], r_[:, sl], AF.Sigmoid),
                 reads=[rb], writes=[t1b])
            k.op(DVE, lambda e: e.tensor_tensor(t3[:], t1[:], t2[:], ALU.mult),
                 reads=[t1b, t2b], writes=[t3b])
            k.op(DVE, lambda e, rr=rr: e.tensor_copy(xgn[:, sl], t3[:]),
                 reads=[t3b], writes=[xgnb])
            k.dma(SP, rw_hbm[rr * 128:(rr + 1) * 128, p0:p0 + P4], xgn[:, sl],
                  reads=[xgnb], writes=[rw_hbm_buf])
            k.op(PE, lambda e, rr=rr: e.matmul(
                ph[:, 0:P4], bm[:, rr * 16:(rr + 1) * 16], t3[:],
                start=True, stop=True), reads=[bmb, t3b], writes=[phb])
            k.op(DVE, lambda e: e.reduce_sum(acc[:, 2:3], ph[:, 0:P4], axis=AX),
                 reads=[phb], writes=[accb])
            k.op(DVE, lambda e: e.tensor_add(acc[:, 0:1], acc[:, 0:1],
                                             acc[:, 2:3]),
                 reads=[accb], writes=[accb])
            k.op(ACT, lambda e: e.activation(t1[:], t3[:], AF.Square),
                 reads=[t3b], writes=[t1b])
            k.op(PE, lambda e, rr=rr: e.matmul(
                ph[:, 0:P4], bm[:, rr * 16:(rr + 1) * 16], t1[:],
                start=True, stop=True), reads=[bmb, t1b], writes=[phb])
            k.op(DVE, lambda e: e.reduce_sum(acc[:, 2:3], ph[:, 0:P4], axis=AX),
                 reads=[phb], writes=[accb])
            k.op(DVE, lambda e: e.tensor_add(acc[:, 1:2], acc[:, 1:2],
                                             acc[:, 2:3]),
                 reads=[accb], writes=[accb])
    NEL = float(T * 64)
    k.op(DVE, lambda e: e.tensor_scalar(acc[:, 0:1], acc[:, 0:1], 1.0 / NEL,
                                        None, ALU.mult), reads=[accb],
         writes=[accb])
    k.op(DVE, lambda e: e.tensor_scalar(acc[:, 1:2], acc[:, 1:2], 1.0 / NEL,
                                        None, ALU.mult), reads=[accb],
         writes=[accb])
    k.op(DVE, lambda e: e.scalar_tensor_tensor(acc[:, 2:3], acc[:, 0:1], -1.0,
                                               acc[:, 0:1], ALU.mult, ALU.mult),
         reads=[accb], writes=[accb])
    k.op(DVE, lambda e: e.tensor_add(acc[:, 1:2], acc[:, 1:2], acc[:, 2:3]),
         reads=[accb], writes=[accb])
    ce, ceb = k.c_rows(1e-5, 16)
    k.op(ACT, lambda e: e.activation(acc[:, 1:2], acc[:, 1:2], AF.Ln, bias=ce),
         reads=[accb, ceb], writes=[accb])
    k.op(ACT, lambda e: e.activation(acc[:, 1:2], acc[:, 1:2], AF.Exp,
                                     scale=-0.5), reads=[accb], writes=[accb])
    k.op(DVE, lambda e: e.scalar_tensor_tensor(acc[:, 2:3], acc[:, 0:1], -1.0,
                                               acc[:, 1:2], ALU.mult, ALU.mult),
         reads=[accb], writes=[accb])
    for ip in range(NP4):
        p0 = ip * P4
        for rr in range(KC):
            sl = slice(rr * P4, (rr + 1) * P4)
            k.op(PE, lambda e, rr=rr: e.matmul(
                pc[:, 0:1], bmT[0:16, rr * 128:(rr + 1) * 128], acc[0:16, 1:2],
                start=True, stop=True), reads=[bmTb, accb], writes=[pcb])
            k.op(DVE, lambda e: e.tensor_copy(colAB[:, 0:1], pc[:, 0:1]),
                 reads=[pcb], writes=[colABb])
            k.op(PE, lambda e, rr=rr: e.matmul(
                pc[:, 0:1], bmT[0:16, rr * 128:(rr + 1) * 128], acc[0:16, 2:3],
                start=True, stop=True), reads=[bmTb, accb], writes=[pcb])
            k.op(DVE, lambda e: e.tensor_copy(colAB[:, 1:2], pc[:, 0:1]),
                 reads=[pcb], writes=[colABb])
            k.dma(SP, mix[:, sl], rw_hbm[rr * 128:(rr + 1) * 128, p0:p0 + P4],
                  reads=[rw_hbm_buf], writes=[mixb])
            k.op(DVE, lambda e, rr=rr: e.tensor_scalar(
                t1[:], mix[:, sl], colAB[:, 0:1], colAB[:, 1:2],
                ALU.mult, ALU.add),
                reads=[mixb], scalar_reads=[colABb], writes=[t1b])
            k.op(DVE, lambda e, rr=rr: e.tensor_scalar(
                xgn[:, sl], t1[:], ccol(CT_G, rr), ccol(CT_B, rr),
                ALU.mult, ALU.add),
                reads=[t1b], scalar_reads=[colsb], writes=[xgnb])
        for tt in range(P4 // 128):
            t0 = p0 + tt * 128
            k.dma(SP, xt[:], x_in[t0:t0 + 128, :], writes=[xtb])
            for no in range(2):
                def mmo(e, tt=tt, no=no):
                    last = None
                    for kc in range(KC):
                        last = e.matmul(po[:, 0:512],
                                        xgn[:, kc * P4 + tt * 128:kc * P4 + tt * 128 + 128],
                                        wslice(WoT, 1024, kc, no * 512, 512),
                                        start=(kc == 0), stop=(kc == KC - 1))
                    return last
                k.op(PE, mmo, reads=[xgnb, WoTb], writes=[pob])
                k.op(DVE, lambda e, no=no: e.tensor_add(
                    xt[:, no * 512:(no + 1) * 512],
                    xt[:, no * 512:(no + 1) * 512], po[:, 0:512]),
                    reads=[pob], writes=[xtb])
            k.dma(SP, x_out[t0:t0 + 128, :], xt[:], reads=[xtb])


def load_rwkv4_weights(k, sc, tag, d):
    W = {}
    for nm in ("WrT", "WkT", "WvT", "WoT"):
        W[nm] = load_weight_cm(k, f"{nm}_{tag}", d[nm], D, 1024)
    for nm, (ri, ci) in (("cols4", (128, 56)), ("bm", (128, 128)),
                         ("bmT", (16, 1024))):
        t, b = k.sb(f"{nm}_{tag}", [ri, ci], F32)
        k.dma(SP, t[:], d[nm], writes=[b])
        W[nm] = (t, b)
    return W


NE = 8
CAPG = 10240
GT = 48
W1COEF = 512


def alloc_moe_route(k, sc):
    sc.m_xt = k.sb("mr_xt", [128, D], F32)
    sc.m_xn = k.sb("mr_xn", [128, D], F32)
    sc.m_xnbf = k.sb("mr_xnbf", [128, D], BF16)
    sc.m_xncm = k.sb("mr_xncm", [128, KC * 128], BF16)
    sc.m_pl = k.ps("mr_pl", [128, 8], F32)
    sc.m_probs = k.sb("mr_probs", [128, 8], F32)
    sc.m_row = k.sb("mr_row", [128, 16], F32)
    sc.m_top = k.sb("mr_top", [128, 8], F32)
    sc.m_eq = k.sb("mr_eq", [128, 8], F32)
    sc.m_eq2 = k.sb("mr_eq2", [128, 8], F32)
    sc.m_iota = k.sb("mr_iota", [128, 8], F32)
    sc.m_ohtm = k.sb("mr_ohtm", [128, 16], BF16)
    sc.m_poh = k.ps("mr_poh", [8, 128], BF16)
    sc.m_oh0 = k.sb("mr_oh0", [8, T], F32)
    sc.m_incl = k.sb("mr_incl", [8, T], F32)
    sc.m_oh1 = k.sb("mr_oh1", [8, T], F32)
    sc.m_ones16 = k.sb("mr_ones16", [8, T // 8], F32)
    sc.m_pos = k.ps("mr_pos", [1, T // 8], F32)
    sc.m_post0 = k.sb("mr_post0", [1, T], F32)
    sc.m_post1 = k.sb("mr_post1", [1, T], F32)
    sc.m_ppos = k.ps("mr_ppos", [128, 1], F32)
    sc.m_route = k.sb("mr_route", [128, 8], F32)
    sc.m_car16 = k.sb("mr_car16", [8, 1], F32)
    sc.m_cnt16 = k.sb("mr_cnt16", [8, 2], F32)


def emit_moe_route(k, sc, x_in, counts_out, route_out):
    xt, xtb = sc.m_xt
    xn, xnb = sc.m_xn
    xnbf, xnbfb = sc.m_xnbf
    xncm, xncmb = sc.m_xncm
    pl, plb = sc.m_pl
    probs, probsb = sc.m_probs
    row, rowb = sc.m_row
    top, topb = sc.m_top
    eq, eqb = sc.m_eq
    eq2, eq2b = sc.m_eq2
    iota, iotab = sc.m_iota
    ohtm, ohtmb = sc.m_ohtm
    poh, pohb = sc.m_poh
    oh0, oh0b = sc.m_oh0
    oh1, oh1b = sc.m_oh1
    incl, inclb = sc.m_incl
    ones16, ones16b = sc.m_ones16
    pos, posb = sc.m_pos
    post0, post0b = sc.m_post0
    post1, post1b = sc.m_post1
    ppos, pposb = sc.m_ppos
    route, routeb = sc.m_route
    car16, car16b = sc.m_car16
    cnt16, cnt16b = sc.m_cnt16
    ones8c, ones8cb = k.sb("mr_ones8c", [8, 1], F32)
    k.op(DVE, lambda e: e.memset(ones8c[:], 1.0), writes=[ones8cb])
    rT, rTb = sc.m_routerT

    BIG = 1e6
    k.op(POOL, lambda e: e.iota(iota[:], [[1, 8]], base=0, channel_multiplier=0,
                                allow_small_or_imprecise_dtypes=True),
         writes=[iotab])
    k.op(DVE, lambda e: e.memset(ones16[:], 1.0), writes=[ones16b])

    for it in range(NT):
        r0 = it * 128
        k.dma(SP, xt[:], x_in[r0:r0 + 128, :], writes=[xtb])
        emit_ln_tm(k, sc, xt[:], xtb, xn[:], xnb)
        k.op(DVE, lambda e: e.tensor_copy(xnbf[:], xn[:]), reads=[xnb],
             writes=[xnbfb])
        for kc in range(KC):
            emit_transpose_128(k, sc, xnbf[:, kc * 128:(kc + 1) * 128],
                               [xnbfb], xncm[:, kc * 128:(kc + 1) * 128],
                               [xncmb])
        def mml(e):
            last = None
            for kc in range(KC):
                last = e.matmul(pl[:, 0:8], xncm[:, kc * 128:(kc + 1) * 128],
                                rT[:, kc * 8:(kc + 1) * 8],
                                start=(kc == 0), stop=(kc == KC - 1))
            return last
        k.op(PE, mml, reads=[xncmb, rTb], writes=[plb])
        k.op(DVE, lambda e: e.reduce_max(row[:, 7:8], pl[:, 0:8], axis=AX,
                                         negate=True), reads=[plb],
             writes=[rowb])
        k.op(ACT, lambda e: e.activation(probs[:], pl[:, 0:8], AF.Exp,
                                         bias=row[:, 7:8],
                                         accum_out=row[:, 6:7]),
             reads=[plb], scalar_reads=[rowb], writes=[probsb, rowb])
        k.op(DVE, lambda e: e.reciprocal(row[:, 6:7], row[:, 6:7]),
             reads=[rowb], writes=[rowb])
        k.op(DVE, lambda e: e.tensor_scalar(probs[:], probs[:], row[:, 6:7],
                                            None, ALU.mult),
             reads=[probsb], scalar_reads=[rowb], writes=[probsb])
        k.op(DVE, lambda e: e.max(top[:], probs[:]), reads=[probsb],
             writes=[topb])
        k.op(DVE, lambda e: e.tensor_scalar(eq[:], probs[:], top[:, 0:1], None,
                                            ALU.is_equal),
             reads=[probsb], scalar_reads=[topb], writes=[eqb])
        k.op(DVE, lambda e: e.scalar_tensor_tensor(eq2[:], eq[:], -BIG,
                                                   iota[:], ALU.mult, ALU.add),
             reads=[eqb, iotab], writes=[eq2b])
        k.op(DVE, lambda e: e.tensor_reduce(row[:, 2:3], eq2[:], axis=AX,
                                            op=ALU.min),
             reads=[eq2b], writes=[rowb])
        k.op(DVE, lambda e: e.tensor_scalar(row[:, 2:3], row[:, 2:3], BIG,
                                            None, ALU.add),
             reads=[rowb], writes=[rowb])
        k.op(DVE, lambda e: e.tensor_scalar(eq[:], iota[:], row[:, 2:3], None,
                                            ALU.is_equal),
             reads=[iotab], scalar_reads=[rowb], writes=[eqb])
        k.op(DVE, lambda e: e.tensor_scalar(eq2[:], probs[:], top[:, 1:2], None,
                                            ALU.is_equal),
             reads=[probsb], scalar_reads=[topb], writes=[eq2b])
        k.op(DVE, lambda e: e.tensor_sub(eq2[:], eq2[:], eq[:]),
             reads=[eq2b, eqb], writes=[eq2b])
        k.op(DVE, lambda e: e.tensor_scalar(eq2[:], eq2[:], 0.0, None,
                                            ALU.max),
             reads=[eq2b], writes=[eq2b])
        k.op(DVE, lambda e: e.scalar_tensor_tensor(eq2[:], eq2[:], -BIG,
                                                   iota[:], ALU.mult, ALU.add),
             reads=[eq2b, iotab], writes=[eq2b])
        k.op(DVE, lambda e: e.tensor_reduce(row[:, 3:4], eq2[:], axis=AX,
                                            op=ALU.min),
             reads=[eq2b], writes=[rowb])
        k.op(DVE, lambda e: e.tensor_scalar(row[:, 3:4], row[:, 3:4], BIG,
                                            None, ALU.add),
             reads=[rowb], writes=[rowb])
        k.op(DVE, lambda e: e.tensor_scalar(eq2[:], iota[:], row[:, 3:4], None,
                                            ALU.is_equal),
             reads=[iotab], scalar_reads=[rowb], writes=[eq2b])
        k.op(DVE, lambda e: e.tensor_add(row[:, 6:7], top[:, 0:1], top[:, 1:2]),
             reads=[topb], writes=[rowb])
        k.op(DVE, lambda e: e.reciprocal(row[:, 6:7], row[:, 6:7]),
             reads=[rowb], writes=[rowb])
        k.op(DVE, lambda e: e.tensor_scalar(row[:, 4:5], top[:, 0:1],
                                            row[:, 6:7], None, ALU.mult),
             reads=[topb], scalar_reads=[rowb], writes=[rowb])
        k.op(DVE, lambda e: e.tensor_scalar(row[:, 5:6], top[:, 1:2],
                                            row[:, 6:7], None, ALU.mult),
             reads=[topb], scalar_reads=[rowb], writes=[rowb])
        k.op(DVE, lambda e: e.tensor_copy(ohtm[:, 0:8], eq[:]), reads=[eqb],
             writes=[ohtmb])
        k.op(DVE, lambda e: e.tensor_copy(ohtm[:, 8:16], eq2[:]), reads=[eq2b],
             writes=[ohtmb])
        k.transpose(poh[0:8, 0:128], ohtm[:, 0:8], reads=[ohtmb], writes=[pohb])
        k.op(DVE, lambda e: e.tensor_copy(oh0[:, r0:r0 + 128], poh[0:8, 0:128]),
             reads=[pohb], writes=[oh0b])
        k.transpose(poh[0:8, 0:128], ohtm[:, 8:16], reads=[ohtmb], writes=[pohb])
        k.op(DVE, lambda e: e.tensor_copy(oh1[:, r0:r0 + 128], poh[0:8, 0:128]),
             reads=[pohb], writes=[oh1b])
        k.op(DVE, lambda e: e.tensor_copy(route[:, 0:1], row[:, 2:3]),
             reads=[rowb], writes=[routeb])
        k.op(DVE, lambda e: e.tensor_copy(route[:, 1:2], row[:, 3:4]),
             reads=[rowb], writes=[routeb])
        k.op(DVE, lambda e: e.tensor_copy(route[:, 2:3], row[:, 4:5]),
             reads=[rowb], writes=[routeb])
        k.op(DVE, lambda e: e.tensor_copy(route[:, 3:4], row[:, 5:6]),
             reads=[rowb], writes=[routeb])
        k.dma(SP, route_out[r0:r0 + 128, 0:4], route[:, 0:4], reads=[routeb])
    CH = T // 8
    for (oh, ohb, postt, posttb, cidx) in ((oh0, oh0b, post0, post0b, 0),
                                           (oh1, oh1b, post1, post1b, 1)):
        for c in range(8):
            lo = c * CH
            if c == 0:
                k.op(DVE, lambda e, oh=oh: e.tensor_tensor_scan(
                    incl[:, 0:CH], ones16[:], oh[:, 0:CH], 0.0, ALU.mult,
                    ALU.add), reads=[ones16b, ohb], writes=[inclb])
            else:
                k.op(DVE, lambda e, lo=lo: e.tensor_copy(car16[:],
                                                         incl[:, lo - 1:lo]),
                     reads=[inclb], writes=[car16b])
                k.op(DVE, lambda e, lo=lo, oh=oh: e.tensor_tensor_scan(
                    incl[:, lo:lo + CH], ones16[:], oh[:, lo:lo + CH],
                    car16[:], ALU.mult, ALU.add),
                    reads=[ones16b, ohb], scalar_reads=[car16b], writes=[inclb])
        k.op(DVE, lambda e, cidx=cidx: e.tensor_copy(cnt16[:, cidx:cidx + 1],
                                                     incl[:, T - 1:T]),
             reads=[inclb], writes=[cnt16b])
        for c in range(8):
            lo = c * CH
            k.op(DVE, lambda e, lo=lo, oh=oh: e.tensor_tensor(
                incl[:, lo:lo + CH], incl[:, lo:lo + CH], oh[:, lo:lo + CH],
                ALU.mult), reads=[inclb, ohb], writes=[inclb])
            k.op(PE, lambda e, lo=lo: e.matmul(pos[0:1, 0:CH],
                                               ones8c[0:8, 0:1],
                                               incl[:, lo:lo + CH],
                                               start=True, stop=True),
                 reads=[ones8cb, inclb], writes=[posb])
            k.op(DVE, lambda e, lo=lo, postt=postt: e.tensor_scalar(
                postt[:, lo:lo + CH], pos[0:1, 0:CH], -1.0, None, ALU.add),
                reads=[posb], writes=[posttb])
    k.dma(SP, counts_out[0:8, :], cnt16[:, 0:1], reads=[cnt16b])
    k.dma(SP, counts_out[8:16, :], cnt16[:, 1:2], reads=[cnt16b])
    for it in range(NT):
        r0 = it * 128
        k.transpose(ppos[:, 0:1], post0[0:1, r0:r0 + 128], reads=[post0b],
                    writes=[pposb])
        k.op(DVE, lambda e: e.tensor_copy(route[:, 4:5], ppos[:, 0:1]),
             reads=[pposb], writes=[routeb])
        k.transpose(ppos[:, 0:1], post1[0:1, r0:r0 + 128], reads=[post1b],
                    writes=[pposb])
        k.op(DVE, lambda e: e.tensor_copy(route[:, 5:6], ppos[:, 0:1]),
             reads=[pposb], writes=[routeb])
        k.dma(SP, route_out[r0:r0 + 128, 4:6], route[:, 4:6], reads=[routeb])


def load_moe_route_weights(k, sc, routerT_d):
    t, b = k.sb("mr_rT", [128, KC * 8], BF16)
    for kc in range(KC):
        k.dma(POOL, t[:, kc * 8:(kc + 1) * 8],
              routerT_d[kc * 128:(kc + 1) * 128, :], writes=[b])
    sc.m_routerT = (t, b)


def alloc_moe_expert(k, sc):
    sc.e_xt = k.sb("me_xt", [128, D], F32)
    sc.e_xn = k.sb("me_xn", [128, D], F32)
    sc.e_xnbf = k.sb("me_xnbf", [128, D], BF16)
    sc.e_route = k.sb("me_route", [128, 8], F32)
    sc.e_rows = k.sb("me_rows", [128, 16], F32)
    sc.e_slot = k.sb("me_slot", [128, 4], U32)
    sc.e_gk = k.sb("me_gk", [128, NT * 2], F32)
    sc.e_c1r = k.sb("me_c1r", [1, 80], F32)
    sc.e_p1 = k.ps("me_p1", [128, 512], F32)
    sc.e_p2 = k.ps("me_p2", [1, 16], F32)
    sc.e_bc8 = k.sb("me_bc8", [128, 48], F32)
    sc.e_iotaw = k.sb("me_iotaw", [128, 64], U32)
    sc.e_iotaw2 = k.sb("me_iotaw2", [128, 64], U32)
    sc.e_widx = k.sb("me_widx", [128, 64], U32)
    sc.e_w2idx = k.sb("me_w2idx", [128, 64], U32)
    sc.e_w1sb = k.sb("me_w1sb", [128, 8 * 4096], BF16)
    sc.e_w2sb = k.sb("me_w2sb", [128, 32 * 1024], BF16)
    sc.e_gin = k.sb("me_gin", [128, D], BF16)
    sc.e_gcm = k.sb("me_gcm", [128, KC * 128], BF16)
    sc.e_hcm = k.sb("me_hcm", [128, 32 * 128], BF16)
    sc.e_y = k.sb("me_y", [128, D], BF16)
    sc.e_eprev = k.sb("me_eprev", [128, 2], F32)
    sc.e_ecur = k.sb("me_ecur", [128, 2], F32)
    sc.e_py = k.ps("me_py", [128, 512], F32)
    sc.e_y0 = k.sb("me_y0", [128, D], BF16)
    sc.e_y1 = k.sb("me_y1", [128, D], BF16)


def emit_moe_expert(k, sc, x_in, x_out, route_in, countsall_in, coremask_in,
                    w1f, w2f, g0_hbm, g1_hbm, y0_hbm, y1_hbm):
    xt, xtb = sc.e_xt
    xn, xnb = sc.e_xn
    xnbf, xnbfb = sc.e_xnbf
    route, routeb = sc.e_route
    rows, rowsb = sc.e_rows
    slot, slotb = sc.e_slot
    gk, gkb = sc.e_gk
    c1r, c1rb = sc.e_c1r
    p1, p1b = sc.e_p1
    p2, p2b = sc.e_p2
    bc8, bc8b = sc.e_bc8
    iotaw, iotawb = sc.e_iotaw
    iotaw2, iotaw2b = sc.e_iotaw2
    widx, widxb = sc.e_widx
    w2idx, w2idxb = sc.e_w2idx
    w1sb, w1sbb = sc.e_w1sb
    w2sb, w2sbb = sc.e_w2sb
    gin, ginb = sc.e_gin
    gcm, gcmb = sc.e_gcm
    hcm, hcmb = sc.e_hcm
    yy, yyb = sc.e_y
    eprev, eprevb = sc.e_eprev
    ecur, ecurb = sc.e_ecur
    py, pyb = sc.e_py
    y0, y0b = sc.e_y0
    y1, y1b = sc.e_y1
    ones1f, ones1fb = sc.ones1f
    g_hbm = [g0_hbm, g1_hbm]
    y_hbm = [y0_hbm, y1_hbm]
    g_buf = [Buf("g0h"), Buf("g1h")]
    y_buf = [Buf("y0h"), Buf("y1h")]
    w1f_buf = Buf("w1f")
    w2f_buf = Buf("w2f")

    cmsb, cmsbb = k.sb("me_cmsb", [8, 2], F32)
    k.dma(SP, cmsb[:], coremask_in, writes=[cmsbb])
    casb, casbb = k.sb("me_casb", [8, 16], F32)
    k.dma(SP, casb[:], countsall_in, writes=[casbb])
    k.op(PE, lambda e: e.matmul(p2[0:1, 0:16], cmsb[:, 0:1], casb[:],
                                start=True, stop=True),
         reads=[cmsbb, casbb], writes=[p2b])
    k.op(DVE, lambda e: e.tensor_copy(c1r[0:1, 0:16], p2[0:1, 0:16]),
         reads=[p2b], writes=[c1rb])
    k.op(PE, lambda e: e.matmul(p2[0:1, 0:16], cmsb[:, 1:2], casb[:],
                                start=True, stop=True),
         reads=[cmsbb, casbb], writes=[p2b])
    k.op(DVE, lambda e: e.tensor_copy(c1r[0:1, 16:32], p2[0:1, 0:16]),
         reads=[p2b], writes=[c1rb])
    ceilu, ceilub = k.sb("me_ceilu", [1, 16], U32)
    k.op(DVE, lambda e: e.tensor_scalar(ceilu[0:1, 0:16], c1r[0:1, 16:32],
                                        1.0 / 256, 0.498046875, ALU.mult,
                                        ALU.add),
         reads=[c1rb], writes=[ceilub])
    k.op(DVE, lambda e: e.tensor_scalar(c1r[0:1, 32:48], ceilu[0:1, 0:16],
                                        256.0, None, ALU.mult),
         reads=[ceilub], writes=[c1rb])
    k.op(DVE, lambda e: e.memset(c1r[0:1, 64:80], 1.0), writes=[c1rb])
    for kk2 in range(2):
        k.op(DVE, lambda e, kk2=kk2: e.tensor_tensor_scan(
            c1r[0:1, 48 + kk2 * 8:56 + kk2 * 8], c1r[0:1, 64:72],
            c1r[0:1, 32 + kk2 * 8:40 + kk2 * 8], 0.0, ALU.mult, ALU.add),
            reads=[c1rb], writes=[c1rb])
    k.op(DVE, lambda e: e.tensor_sub(c1r[0:1, 48:64], c1r[0:1, 48:64],
                                     c1r[0:1, 32:48]),
         reads=[c1rb], writes=[c1rb])
    for i, (lo_, hi_) in enumerate(((48, 56), (56, 64), (0, 8), (8, 16))):
        k.op(PE, lambda e, lo_=lo_, hi_=hi_: e.matmul(
            p1[:, 0:8], ones1f[0:1, 0:128], c1r[0:1, lo_:hi_],
            start=True, stop=True),
            reads=[c1rb, ones1fb], writes=[p1b])
        k.op(DVE, lambda e, i=i: e.tensor_copy(bc8[:, i * 8:(i + 1) * 8],
                                               p1[:, 0:8]),
             reads=[p1b], writes=[bc8b])
    k.op(POOL, lambda e: e.iota(bc8[:, 32:40], [[1, 8]], base=0,
                                channel_multiplier=0,
                                allow_small_or_imprecise_dtypes=True),
         writes=[bc8b])
    for kc in range(8):
        for ncx in range(8):
            k.op(POOL, lambda e, kc=kc, ncx=ncx: e.iota(
                iotaw[:, kc * 8 + ncx:kc * 8 + ncx + 1], [[1, 1]],
                base=kc * 1024 + ncx, channel_multiplier=8), writes=[iotawb])
    for kc2 in range(32):
        for ncx in range(2):
            k.op(POOL, lambda e, kc2=kc2, ncx=ncx: e.iota(
                iotaw2[:, kc2 * 2 + ncx:kc2 * 2 + ncx + 1], [[1, 1]],
                base=kc2 * 256 + ncx, channel_multiplier=2), writes=[iotaw2b])

    for it in range(NT):
        r0 = it * 128
        k.dma(SP, xt[:], x_in[r0:r0 + 128, :], writes=[xtb])
        k.dma(SP, route[:], route_in[r0:r0 + 128, :], writes=[routeb])
        emit_ln_tm(k, sc, xt[:], xtb, xn[:], xnb)
        k.op(DVE, lambda e: e.tensor_copy(xnbf[:], xn[:]), reads=[xnb],
             writes=[xnbfb])
        for kk2 in range(2):
            ecol = route[:, kk2:kk2 + 1]
            gcol = route[:, 2 + kk2:3 + kk2]
            pcol = route[:, 4 + kk2:5 + kk2]
            k.op(DVE, lambda e, kk2=kk2, ecol=ecol: e.tensor_scalar(
                rows[:, 0:8], bc8[:, 32:40], ecol, None, ALU.is_equal),
                reads=[bc8b], scalar_reads=[routeb], writes=[rowsb])
            k.op(DVE, lambda e, kk2=kk2: e.tensor_tensor(
                rows[:, 8:16], rows[:, 0:8], bc8[:, kk2 * 8:(kk2 + 1) * 8],
                ALU.mult), reads=[rowsb, bc8b], writes=[rowsb])
            k.op(DVE, lambda e: e.reduce_sum(rows[:, 8:9], rows[:, 8:16],
                                             axis=AX),
                 reads=[rowsb], writes=[rowsb])
            k.op(DVE, lambda e, kk2=kk2: e.tensor_tensor(
                rows[:, 0:8], rows[:, 0:8],
                bc8[:, 16 + kk2 * 8:24 + kk2 * 8], ALU.mult),
                reads=[rowsb, bc8b], writes=[rowsb])
            k.op(DVE, lambda e: e.reduce_sum(rows[:, 9:10], rows[:, 0:8],
                                             axis=AX),
                 reads=[rowsb], writes=[rowsb])
            k.op(DVE, lambda e, kk2=kk2, pcol=pcol: e.tensor_scalar(
                slot[:, kk2:kk2 + 1], rows[:, 8:9], pcol, None, ALU.add),
                reads=[rowsb], scalar_reads=[routeb], writes=[slotb])
            k.op(DVE, lambda e, pcol=pcol: e.tensor_scalar(
                rows[:, 10:11], rows[:, 9:10], pcol, None, ALU.add),
                reads=[rowsb], scalar_reads=[routeb], writes=[rowsb])
            k.op(DVE, lambda e: e.tensor_scalar(rows[:, 10:11], rows[:, 10:11],
                                                float(CAPG), None, ALU.is_lt),
                 reads=[rowsb], writes=[rowsb])
            k.op(DVE, lambda e, kk2=kk2, it=it, gcol=gcol: e.tensor_scalar(
                gk[:, it * 2 + kk2:it * 2 + kk2 + 1], rows[:, 10:11], gcol,
                0.5, ALU.mult, ALU.mult),
                reads=[rowsb], scalar_reads=[routeb], writes=[gkb])
            k.indirect(g_hbm[kk2], IndirectOffsetOnAxis(slot[:, kk2:kk2 + 1], 0),
                       xnbf[:], None, reads=[slotb, xnbfb], writes=[g_buf[kk2]])
    k.fence_indirect()
    for kk2 in range(2):
        k.op(DVE, lambda e: e.memset(eprev[:, 0:1], -1.0), writes=[eprevb])
        for gt in range(GT):
            k.op(DVE, lambda e, gt=gt, kk2=kk2: e.tensor_scalar(
                rows[:, 0:8], bc8[:, kk2 * 8:(kk2 + 1) * 8], float(gt * 128),
                None, ALU.is_le), reads=[bc8b], writes=[rowsb])
            k.op(DVE, lambda e: e.reduce_sum(rows[:, 8:9], rows[:, 0:8],
                                             axis=AX),
                 reads=[rowsb], writes=[rowsb])
            k.op(DVE, lambda e: e.tensor_scalar(ecur[:, 0:1], rows[:, 8:9],
                                                -1.0, None, ALU.add),
                 reads=[rowsb], writes=[ecurb])
            k.op(DVE, lambda e: e.tensor_scalar(rows[:, 12:13], ecur[:, 0:1],
                                                8192.0, None, ALU.mult),
                 reads=[ecurb], writes=[rowsb])
            k.op(DVE, lambda e: e.tensor_scalar(widx[:], iotaw[:],
                                                rows[:, 12:13], None, ALU.add),
                 reads=[iotawb], scalar_reads=[rowsb], writes=[widxb])
            k.op(DVE, lambda e: e.tensor_scalar(rows[:, 13:14], ecur[:, 0:1],
                                                8192.0, None, ALU.mult),
                 reads=[ecurb], writes=[rowsb])
            k.op(DVE, lambda e: e.tensor_scalar(w2idx[:], iotaw2[:],
                                                rows[:, 13:14], None, ALU.add),
                 reads=[iotaw2b], scalar_reads=[rowsb], writes=[w2idxb])
            for kc in range(8):
                for ncx in range(8):
                    cix = kc * 8 + ncx
                    k.indirect(w1sb[:, cix * 512:(cix + 1) * 512], None, w1f,
                               IndirectOffsetOnAxis(widx[:, cix:cix + 1], 0),
                               reads=[widxb, w1f_buf], writes=[w1sbb])
            for kc2 in range(32):
                for ncx in range(2):
                    cix = kc2 * 2 + ncx
                    k.indirect(w2sb[:, cix * 512:(cix + 1) * 512], None, w2f,
                               IndirectOffsetOnAxis(w2idx[:, cix:cix + 1], 0),
                               reads=[w2idxb, w2f_buf], writes=[w2sbb])
            k.fence_indirect()
            k.dma(SP, gin[:], g_hbm[kk2][gt * 128:(gt + 1) * 128, :],
                  reads=[g_buf[kk2]], writes=[ginb])
            for kc in range(KC):
                emit_transpose_128(k, sc, gin[:, kc * 128:(kc + 1) * 128],
                                   [ginb], gcm[:, kc * 128:(kc + 1) * 128],
                                   [gcmb])
            for mo in range(32):
                def mmh(e, mo=mo):
                    last = None
                    for kc in range(KC):
                        last = e.matmul(p1[:, 0:128],
                                        w1sb[:, kc * 4096 + mo * 128:kc * 4096 + mo * 128 + 128],
                                        gcm[:, kc * 128:(kc + 1) * 128],
                                        start=(kc == 0), stop=(kc == KC - 1))
                    return last
                k.op(PE, mmh, reads=[w1sbb, gcmb], writes=[p1b])
                k.op(ACT, lambda e, mo=mo: e.activation(
                    hcm[:, mo * 128:(mo + 1) * 128], p1[:, 0:128], AF.Gelu),
                    reads=[p1b], writes=[hcmb])
            for no in range(2):
                def mmy(e, no=no):
                    last = None
                    for kc2 in range(32):
                        last = e.matmul(py[:, 0:512],
                                        hcm[:, kc2 * 128:(kc2 + 1) * 128],
                                        w2sb[:, kc2 * 1024 + no * 512:kc2 * 1024 + no * 512 + 512],
                                        start=(kc2 == 0), stop=(kc2 == 31))
                    return last
                k.op(PE, mmy, reads=[hcmb, w2sbb], writes=[pyb])
                k.op(DVE, lambda e, no=no: e.tensor_copy(
                    yy[:, no * 512:(no + 1) * 512], py[:, 0:512]),
                    reads=[pyb], writes=[yyb])
            k.dma(SP, y_hbm[kk2][gt * 128:(gt + 1) * 128, :], yy[:],
                  reads=[yyb], writes=[y_buf[kk2]])
    for it in range(NT):
        r0 = it * 128
        k.dma(SP, xt[:], x_in[r0:r0 + 128, :], writes=[xtb])
        k.dma(SP, route[:], route_in[r0:r0 + 128, :], writes=[routeb])
        for kk2 in range(2):
            ecol = route[:, kk2:kk2 + 1]
            pcol = route[:, 4 + kk2:5 + kk2]
            k.op(DVE, lambda e, ecol=ecol: e.tensor_scalar(
                rows[:, 0:8], bc8[:, 32:40], ecol, None, ALU.is_equal),
                reads=[bc8b], scalar_reads=[routeb], writes=[rowsb])
            k.op(DVE, lambda e, kk2=kk2: e.tensor_tensor(
                rows[:, 8:16], rows[:, 0:8], bc8[:, kk2 * 8:(kk2 + 1) * 8],
                ALU.mult), reads=[rowsb, bc8b], writes=[rowsb])
            k.op(DVE, lambda e: e.reduce_sum(rows[:, 8:9], rows[:, 8:16],
                                             axis=AX),
                 reads=[rowsb], writes=[rowsb])
            k.op(DVE, lambda e, kk2=kk2, pcol=pcol: e.tensor_scalar(
                slot[:, kk2:kk2 + 1], rows[:, 8:9], pcol, None, ALU.add),
                reads=[rowsb], scalar_reads=[routeb], writes=[slotb])
        k.indirect(y0[:], None, y_hbm[0],
                   IndirectOffsetOnAxis(slot[:, 0:1], 0),
                   reads=[slotb, y_buf[0]], writes=[y0b])
        k.indirect(y1[:], None, y_hbm[1],
                   IndirectOffsetOnAxis(slot[:, 1:2], 0),
                   reads=[slotb, y_buf[1]], writes=[y1b])
        k.fence_indirect()
        k.op(DVE, lambda e, it=it: e.tensor_scalar(
            xn[:], y0[:], gk[:, it * 2:it * 2 + 1], None, ALU.mult),
            reads=[y0b], scalar_reads=[gkb], writes=[xnb])
        k.op(DVE, lambda e: e.tensor_add(xt[:], xt[:], xn[:]),
             reads=[xnb], writes=[xtb])
        k.op(DVE, lambda e, it=it: e.tensor_scalar(
            xn[:], y1[:], gk[:, it * 2 + 1:it * 2 + 2], None, ALU.mult),
            reads=[y1b], scalar_reads=[gkb], writes=[xnb])
        k.op(DVE, lambda e: e.tensor_add(xt[:], xt[:], xn[:]),
             reads=[xnb], writes=[xtb])
        k.dma(SP, x_out[r0:r0 + 128, :], xt[:], reads=[xtb])



def alloc_tiny(k, sc):
    sc.t_a = k.sb("tp_a", [128, D], F32)
    sc.t_b = k.sb("tp_b", [128, D], F32)


def emit_lnpass(k, sc, x_in, y_out):
    a, ab = sc.t_a
    b, bb = sc.t_b
    for it in range(NT):
        r0 = it * 128
        k.dma(SP, a[:], x_in[r0:r0 + 128, :], writes=[ab])
        emit_ln_tm(k, sc, a[:], ab, b[:], bb)
        k.dma(SP, y_out[r0:r0 + 128, :], b[:], reads=[bb])


def emit_axpy(k, sc, x_in, y_in, out, scale=0.5):
    a, ab = sc.t_a
    b, bb = sc.t_b
    for it in range(NT):
        r0 = it * 128
        k.dma(SP, a[:], x_in[r0:r0 + 128, :], writes=[ab])
        k.dma(SP, b[:], y_in[r0:r0 + 128, :], writes=[bb])
        k.op(DVE, lambda e: e.scalar_tensor_tensor(a[:], b[:], scale, a[:],
                                                   ALU.mult, ALU.add),
             reads=[ab, bb], writes=[ab])
        k.dma(SP, out[r0:r0 + 128, :], a[:], reads=[ab])


"""Driver: builds the 8 per-sublayer programs once, runs the unfused launch
chain. kernel.py is generated by concatenating bk.py + kern.py + this file
(with sibling imports stripped)."""

_PROGS = {}


def _mk(name, build_fn):
    if name not in _PROGS:
        _PROGS[name] = build_fn()
    return _PROGS[name]


def _new_nc():
    return bass.Bass("TRN2", target_bir_lowering=False)


def _build_route():
    nc = _new_nc()
    x_d = nc.dram_tensor("x", [T, D], F32, kind="ExternalInput").ap()
    rT_d = nc.dram_tensor("rT", [D, 8], F32, kind="ExternalInput").ap()
    cnt_d = nc.dram_tensor("counts", [16, 1], F32, kind="ExternalOutput").ap()
    rt_d = nc.dram_tensor("route", [T, 8], F32, kind="ExternalOutput").ap()
    ctx = contextlib.ExitStack()
    with ctx:
        k = Kit(nc, ctx)
        sc = Scratch()
        alloc_common(k, sc)
        alloc_moe_route(k, sc)
        load_moe_route_weights(k, sc, rT_d)
        emit_moe_route(k, sc, x_d, cnt_d, rt_d)
        k.barrier()
    return nc


def _build_moeB():
    nc = _new_nc()
    x_d = nc.dram_tensor("x", [T, D], F32, kind="ExternalInput").ap()
    rt_d = nc.dram_tensor("route", [T, 8], F32, kind="ExternalInput").ap()
    ca_d = nc.dram_tensor("countsall", [8, 16], F32, kind="ExternalInput").ap()
    cm_d = nc.dram_tensor("coremask", [8, 2], F32, kind="ExternalInput").ap()
    w1_d = nc.dram_tensor("w1f", [65536, 512], F32, kind="ExternalInput").ap()
    w2_d = nc.dram_tensor("w2f", [65536, 512], F32, kind="ExternalInput").ap()
    y_d = nc.dram_tensor("y", [T, D], F32, kind="ExternalOutput").ap()
    g0 = nc.dram_tensor("g0", [GT * 128, D], BF16).ap()
    g1 = nc.dram_tensor("g1", [GT * 128, D], BF16).ap()
    yg0 = nc.dram_tensor("yg0", [GT * 128, D], BF16).ap()
    yg1 = nc.dram_tensor("yg1", [GT * 128, D], BF16).ap()
    ctx = contextlib.ExitStack()
    with ctx:
        k = Kit(nc, ctx)
        sc = Scratch()
        alloc_common(k, sc)
        alloc_moe_expert(k, sc)
        emit_moe_expert(k, sc, x_d, y_d, rt_d, ca_d, cm_d, w1_d, w2_d,
                        g0, g1, yg0, yg1)
        k.barrier()
    return nc


def _build_rwkv7():
    nc = _new_nc()
    shapes = {"Wr": (D, D), "Wk": (D, D), "Wv": (D, D), "Wo": (D, D),
              "w1": (D, 64), "a1": (D, 64), "g1": (D, 64), "v1": (D, 64),
              "w2": (64, 16), "a2": (64, 16), "g2": (64, D), "v2": (64, D),
              "w0a0": (16, 2), "mixcols": (128, 48), "cols": (128, 40),
              "bm": (128, 128), "bmT": (16, 1024)}
    dd = {nm: nc.dram_tensor(nm, list(sh), F32, kind="ExternalInput").ap()
          for nm, sh in shapes.items()}
    x_d = nc.dram_tensor("x", [T, D], F32, kind="ExternalInput").ap()
    y_d = nc.dram_tensor("y", [T, D], F32, kind="ExternalOutput").ap()
    xn_hbm = nc.dram_tensor("xn_scr", [D, T], BF16).ap()
    ctx = contextlib.ExitStack()
    with ctx:
        k = Kit(nc, ctx)
        sc = Scratch()
        alloc_common(k, sc)
        alloc_rwkv7(k, sc)
        W = load_rwkv7_weights(k, sc, "l0", dd)
        emit_rwkv7(k, sc, x_d, y_d, W, xn_hbm, Buf("xnh"))
        k.barrier()
    return nc


def _build_rwkv4():
    nc = _new_nc()
    shapes = {"WrT": (D, D), "WkT": (D, D), "WvT": (D, D), "WoT": (D, D),
              "cols4": (128, 56), "bm": (128, 128), "bmT": (16, 1024)}
    dd = {nm: nc.dram_tensor(nm, list(sh), F32, kind="ExternalInput").ap()
          for nm, sh in shapes.items()}
    x_d = nc.dram_tensor("x", [T, D], F32, kind="ExternalInput").ap()
    y_d = nc.dram_tensor("y", [T, D], F32, kind="ExternalOutput").ap()
    xn_hbm = nc.dram_tensor("xn_scr", [D, T], BF16).ap()
    rw_hbm = nc.dram_tensor("rw_scr", [D, T], BF16).ap()
    ctx = contextlib.ExitStack()
    with ctx:
        k = Kit(nc, ctx)
        sc = Scratch()
        alloc_common(k, sc)
        alloc_rwkv4(k, sc)
        W = load_rwkv4_weights(k, sc, "l1", dd)
        emit_rwkv4(k, sc, x_d, y_d, W, xn_hbm, Buf("xnh"), rw_hbm, Buf("rwh"))
        k.barrier()
    return nc


def _build_latattn():
    nc = _new_nc()
    x_d = nc.dram_tensor("x", [T, D], F32, kind="ExternalInput").ap()
    wqkv_d = nc.dram_tensor("wqkv", [D, 3072], F32, kind="ExternalInput").ap()
    qs_d = nc.dram_tensor("qs", [64], F32, kind="ExternalInput").ap()
    ks_d = nc.dram_tensor("ks", [64], F32, kind="ExternalInput").ap()
    wo_d = nc.dram_tensor("wo", [D, 1024], F32, kind="ExternalInput").ap()
    tabs_d = nc.dram_tensor("tabs", [T, 256], F32, kind="ExternalInput").ap()
    mask_d = nc.dram_tensor("masks", [128, 3 * 384], F32,
                            kind="ExternalInput").ap()
    y_d = nc.dram_tensor("y", [T, D], F32, kind="ExternalOutput").ap()
    ctx = contextlib.ExitStack()
    with ctx:
        k = Kit(nc, ctx)
        sc = Scratch()
        alloc_common(k, sc)
        alloc_lat_attn(k, sc)
        mt, mb = k.sb("masks_sb", [128, 3 * 384], F32)
        k.dma(SP, mt[:], mask_d, writes=[mb])
        W = load_lat_attn_weights(k, sc, "l", wqkv_d, wo_d, qs_d, ks_d, tabs_d)
        emit_lat_attn(k, sc, x_d, y_d, W, (mt, mb))
        k.barrier()
    return nc


def _build_latffn():
    nc = _new_nc()
    x_d = nc.dram_tensor("x", [T, D], F32, kind="ExternalInput").ap()
    w1_d = nc.dram_tensor("w1", [D, DH2], F32, kind="ExternalInput").ap()
    b1_d = nc.dram_tensor("b1", [DH2], F32, kind="ExternalInput").ap()
    w2_d = nc.dram_tensor("w2", [DI, 1024], F32, kind="ExternalInput").ap()
    b2_d = nc.dram_tensor("b2", [1024], F32, kind="ExternalInput").ap()
    g_d = nc.dram_tensor("g", [D], F32, kind="ExternalInput").ap()
    b_d = nc.dram_tensor("b", [D], F32, kind="ExternalInput").ap()
    y_d = nc.dram_tensor("y", [T, D], F32, kind="ExternalOutput").ap()
    ctx = contextlib.ExitStack()
    with ctx:
        k = Kit(nc, ctx)
        sc = Scratch()
        alloc_common(k, sc)
        alloc_lat_ffn(k, sc)
        W = load_lat_ffn_weights(k, sc, "l", w1_d, b1_d, w2_d, b2_d, g_d, b_d)
        emit_lat_ffn(k, sc, x_d, y_d, W)
        k.barrier()
    return nc


def _build_ln():
    nc = _new_nc()
    x_d = nc.dram_tensor("x", [T, D], F32, kind="ExternalInput").ap()
    y_d = nc.dram_tensor("y", [T, D], F32, kind="ExternalOutput").ap()
    ctx = contextlib.ExitStack()
    with ctx:
        k = Kit(nc, ctx)
        sc = Scratch()
        alloc_common(k, sc)
        alloc_tiny(k, sc)
        emit_lnpass(k, sc, x_d, y_d)
        k.barrier()
    return nc


def _build_axpy():
    nc = _new_nc()
    x_d = nc.dram_tensor("x", [T, D], F32, kind="ExternalInput").ap()
    yy_d = nc.dram_tensor("yy", [T, D], F32, kind="ExternalInput").ap()
    o_d = nc.dram_tensor("o", [T, D], F32, kind="ExternalOutput").ap()
    ctx = contextlib.ExitStack()
    with ctx:
        k = Kit(nc, ctx)
        sc = Scratch()
        alloc_common(k, sc)
        alloc_tiny(k, sc)
        emit_axpy(k, sc, x_d, yy_d, o_d)
        k.barrier()
    return nc




def _build_mid():
    nc = _new_nc()
    x_d = nc.dram_tensor("x", [T, D], F32, kind="ExternalInput").ap()
    rt_d = nc.dram_tensor("route", [T, 8], F32, kind="ExternalInput").ap()
    ca_d = nc.dram_tensor("countsall", [8, 16], F32, kind="ExternalInput").ap()
    cm_d = nc.dram_tensor("coremask", [8, 2], F32, kind="ExternalInput").ap()
    w1_d = nc.dram_tensor("w1f", [65536, 512], F32, kind="ExternalInput").ap()
    w2_d = nc.dram_tensor("w2f", [65536, 512], F32, kind="ExternalInput").ap()
    rT_d = nc.dram_tensor("rT", [D, 8], F32, kind="ExternalInput").ap()
    shapes = {"Wr": (D, D), "Wk": (D, D), "Wv": (D, D), "Wo": (D, D),
              "w1": (D, 64), "a1": (D, 64), "g1": (D, 64), "v1": (D, 64),
              "w2": (64, 16), "a2": (64, 16), "g2": (64, D), "v2": (64, D),
              "w0a0": (16, 2), "mixcols": (128, 48), "cols": (128, 40),
              "bm": (128, 128), "bmT": (16, 1024)}
    dd = {nm: nc.dram_tensor("r7" + nm, list(sh), F32,
                             kind="ExternalInput").ap()
          for nm, sh in shapes.items()}
    x2_d = nc.dram_tensor("x2", [T, D], F32, kind="ExternalOutput").ap()
    cnt_d = nc.dram_tensor("counts2", [16, 1], F32, kind="ExternalOutput").ap()
    rt2_d = nc.dram_tensor("route2", [T, 8], F32, kind="ExternalOutput").ap()
    x1_h = nc.dram_tensor("x1_scr", [T, D], F32).ap()
    g0 = nc.dram_tensor("g0", [GT * 128, D], BF16).ap()
    g1 = nc.dram_tensor("g1", [GT * 128, D], BF16).ap()
    yg0 = nc.dram_tensor("yg0", [GT * 128, D], BF16).ap()
    yg1 = nc.dram_tensor("yg1", [GT * 128, D], BF16).ap()
    xn_hbm = nc.dram_tensor("xn_scr", [D, T], BF16).ap()
    ctx = contextlib.ExitStack()
    with ctx:
        k = Kit(nc, ctx)
        sc = Scratch()
        alloc_common(k, sc)
        k.push_scope()
        alloc_moe_expert(k, sc)
        emit_moe_expert(k, sc, x_d, x1_h, rt_d, ca_d, cm_d, w1_d, w2_d,
                        g0, g1, yg0, yg1)
        k.pop_scope()
        k.push_scope()
        alloc_rwkv7(k, sc)
        W = load_rwkv7_weights(k, sc, "l0", dd)
        emit_rwkv7(k, sc, x1_h, x2_d, W, xn_hbm, Buf("xnh"))
        k.pop_scope()
        k.push_scope()
        alloc_moe_route(k, sc)
        load_moe_route_weights(k, sc, rT_d)
        emit_moe_route(k, sc, x2_d, cnt_d, rt2_d)
        k.pop_scope()
        k.barrier()
    return nc


def _build_lat6():
    nc = _new_nc()
    x_d = nc.dram_tensor("x", [T, D], F32, kind="ExternalInput").ap()
    tabs_d = nc.dram_tensor("tabs", [T, 256], F32, kind="ExternalInput").ap()
    mask_d = nc.dram_tensor("masks", [128, 3 * 384], F32,
                            kind="ExternalInput").ap()
    lay = []
    for i in range(6):
        lay.append({
            "wqkv": nc.dram_tensor(f"wqkv{i}", [D, 3072], F32,
                                   kind="ExternalInput").ap(),
            "qs": nc.dram_tensor(f"qs{i}", [64], F32, kind="ExternalInput").ap(),
            "ks": nc.dram_tensor(f"ks{i}", [64], F32, kind="ExternalInput").ap(),
            "wo": nc.dram_tensor(f"wo{i}", [D, 1024], F32,
                                 kind="ExternalInput").ap(),
            "w1": nc.dram_tensor(f"w1_{i}", [D, DH2], F32,
                                 kind="ExternalInput").ap(),
            "b1": nc.dram_tensor(f"b1_{i}", [DH2], F32,
                                 kind="ExternalInput").ap(),
            "w2": nc.dram_tensor(f"w2_{i}", [DI, 1024], F32,
                                 kind="ExternalInput").ap(),
            "b2": nc.dram_tensor(f"b2_{i}", [1024], F32,
                                 kind="ExternalInput").ap(),
            "g": nc.dram_tensor(f"g_{i}", [D], F32, kind="ExternalInput").ap(),
            "bb": nc.dram_tensor(f"bb_{i}", [D], F32,
                                 kind="ExternalInput").ap(),
        })
    o_d = nc.dram_tensor("o", [T, D], F32, kind="ExternalOutput").ap()
    ya = nc.dram_tensor("lat_ya", [T, D], F32).ap()
    yb = nc.dram_tensor("lat_yb", [T, D], F32).ap()
    ctx = contextlib.ExitStack()
    with ctx:
        k = Kit(nc, ctx)
        sc = Scratch()
        alloc_common(k, sc)
        k.push_scope()
        alloc_tiny(k, sc)
        emit_lnpass(k, sc, x_d, ya)
        k.pop_scope()
        cur, nxt = ya, yb
        for i in range(6):
            k.push_scope()
            alloc_lat_attn(k, sc)
            mt, mb = k.sb("masks_sb", [128, 3 * 384], F32)
            k.dma(SP, mt[:], mask_d, writes=[mb])
            W = load_lat_attn_weights(k, sc, f"a{i}", lay[i]["wqkv"],
                                      lay[i]["wo"], lay[i]["qs"], lay[i]["ks"],
                                      tabs_d)
            emit_lat_attn(k, sc, cur, nxt, W, (mt, mb))
            k.pop_scope()
            cur, nxt = nxt, cur
            k.push_scope()
            alloc_lat_ffn(k, sc)
            W = load_lat_ffn_weights(k, sc, f"f{i}", lay[i]["w1"],
                                     lay[i]["b1"], lay[i]["w2"], lay[i]["b2"],
                                     lay[i]["g"], lay[i]["bb"])
            emit_lat_ffn(k, sc, cur, nxt, W)
            k.pop_scope()
            cur, nxt = nxt, cur
        k.push_scope()
        alloc_tiny(k, sc)
        emit_axpy(k, sc, x_d, cur, o_d)
        k.pop_scope()
        k.barrier()
    return nc



def _host_rotary_tables():
    HDL = 64
    half = np.arange(0, HDL, 2, dtype=np.float64)
    inv_freq = 1.0 / 10000.0 ** (half / HDL)
    t = np.arange(T, dtype=np.float64)
    freqs = np.concatenate([t[:, None] * inv_freq[None, :]] * 2, -1)
    base_scale = (half + 0.4 * HDL) / (1.4 * HDL)
    power = (t - T // 2) / 128.0
    scale = base_scale[None, :] ** power[:, None]
    scale = np.concatenate([scale, scale], -1)
    cos = np.cos(freqs)
    sin = np.sin(freqs)
    sgn = np.concatenate([-np.ones(HDL // 2), np.ones(HDL // 2)])
    return np.concatenate([cos * scale, sgn * sin * scale,
                           cos / scale, sgn * sin / scale], -1).astype(np.float32)


def _host_masks():
    KEYS_ = 384
    p = np.arange(128)[:, None]
    jl = np.arange(KEYS_)[None, :]
    base = (p <= jl) & (jl <= p + 256)
    out = np.zeros((128, 3 * KEYS_), np.float32)
    for v, m in enumerate((base & (jl >= 256), base & (jl >= 128), base)):
        out[:, v * KEYS_:(v + 1) * KEYS_] = np.where(m, 0.0, -1e30)
    return out


def _host_bm():
    bm = np.zeros((128, 128), np.float32)
    bmT = np.zeros((16, 1024), np.float32)
    for rr in range(8):
        for p in range(128):
            h = 2 * rr + (1 if p >= 64 else 0)
            bm[p, rr * 16 + h] = 1.0
            bmT[h, rr * 128 + p] = 1.0
    return bm, bmT


def _cols8(vec1024):
    out = np.zeros((128, 8), np.float32)
    v = np.asarray(vec1024).reshape(-1)
    for rr in range(8):
        out[:, rr] = v[rr * 128:(rr + 1) * 128]
    return out


def _run(nc, in_maps):
    res = run_bass_kernel_spmd(nc, in_maps, core_ids=list(range(8)))
    return res.results


def kernel(**inp):
    xs = [np.ascontiguousarray(inp["L0_attn_x"][c]).astype(np.float32,
                                                           copy=False)
          for c in range(8)]
    bm, bmT = _host_bm()
    tabs = _host_rotary_tables()
    masks = _host_masks()

    rT = np.ascontiguousarray(inp["L0_ffn_router_w"].T)
    w1f = np.ascontiguousarray(inp["L0_ffn_w1"].reshape(65536, 512))
    w2f = np.ascontiguousarray(inp["L0_ffn_w2"].reshape(65536, 512))
    coremasks = []
    for c in range(8):
        cm = np.zeros((8, 2), np.float32)
        cm[:c, 0] = 1.0
        cm[c, 1] = 1.0
        coremasks.append(cm)

    mixcols = np.zeros((128, 48), np.float32)
    for m, nm in enumerate(("x_r", "x_w", "x_k", "x_v", "x_a", "x_g")):
        mixcols[:, m * 8:(m + 1) * 8] = _cols8(inp["L0_attn_" + nm])
    cols7 = np.zeros((128, 40), np.float32)
    cols7[:, 0:8] = _cols8(inp["L0_attn_k_k"])
    cols7[:, 8:16] = _cols8(inp["L0_attn_r_k"])
    cols7[:, 16:24] = _cols8(inp["L0_attn_ln_w"])
    cols7[:, 24:32] = _cols8(inp["L0_attn_ln_b"])
    cols7[:, 32:40] = _cols8(inp["L0_attn_v0"])
    r7w = {"Wr": inp["L0_attn_Wr"], "Wk": inp["L0_attn_Wk"],
           "Wv": inp["L0_attn_Wv"], "Wo": inp["L0_attn_Wo"],
           "w1": inp["L0_attn_w1"], "a1": inp["L0_attn_a1"],
           "g1": inp["L0_attn_g1"], "w2": inp["L0_attn_w2"],
           "a2": inp["L0_attn_a2"], "g2": inp["L0_attn_g2"],
           "v1": inp["L0_attn_v1"], "v2": inp["L0_attn_v2"],
           "w0a0": np.stack([inp["L0_attn_w0"].reshape(16),
                             inp["L0_attn_a0"].reshape(16)], 1),
           "mixcols": mixcols, "cols": cols7, "bm": bm, "bmT": bmT}
    r7w = {kk: np.ascontiguousarray(vv, dtype=np.float32)
           for kk, vv in r7w.items()}

    cols4 = np.zeros((128, 56), np.float32)
    cols4[:, 0:8] = _cols8(inp["L1_attn_time_mix_r"])
    cols4[:, 8:16] = _cols8(inp["L1_attn_time_mix_k"])
    cols4[:, 16:24] = _cols8(inp["L1_attn_time_mix_v"])
    cols4[:, 24:32] = _cols8(inp["L1_attn_time_decay"])
    cols4[:, 32:40] = _cols8(inp["L1_attn_time_first"])
    cols4[:, 40:48] = _cols8(inp["L1_attn_ln_gamma"])
    cols4[:, 48:56] = _cols8(inp["L1_attn_ln_beta"])
    r4w = {"WrT": np.ascontiguousarray(inp["L1_attn_W_r"].T),
           "WkT": np.ascontiguousarray(inp["L1_attn_W_k"].T),
           "WvT": np.ascontiguousarray(inp["L1_attn_W_v"].T),
           "WoT": np.ascontiguousarray(inp["L1_attn_W_o"].T),
           "cols4": cols4, "bm": bm, "bmT": bmT}
    r4w = {kk: np.ascontiguousarray(vv, dtype=np.float32)
           for kk, vv in r4w.items()}

    lat_ins = {"tabs": tabs, "masks": masks}
    for i in range(6):
        lat_ins[f"wqkv{i}"] = np.ascontiguousarray(inp["L1_ffn_Wqkv"][i])
        lat_ins[f"qs{i}"] = np.ascontiguousarray(inp["L1_ffn_q_scale"][i])
        lat_ins[f"ks{i}"] = np.ascontiguousarray(inp["L1_ffn_k_scale"][i])
        lat_ins[f"wo{i}"] = np.ascontiguousarray(inp["L1_ffn_Wo"][i])
        lat_ins[f"w1_{i}"] = np.ascontiguousarray(inp["L1_ffn_W1"][i])
        lat_ins[f"b1_{i}"] = np.ascontiguousarray(inp["L1_ffn_b1"][i])
        lat_ins[f"w2_{i}"] = np.ascontiguousarray(inp["L1_ffn_W2"][i])
        lat_ins[f"b2_{i}"] = np.ascontiguousarray(inp["L1_ffn_b2"][i])
        lat_ins[f"g_{i}"] = np.ascontiguousarray(inp["L1_ffn_ln_g"][i])
        lat_ins[f"bb_{i}"] = np.ascontiguousarray(inp["L1_ffn_ln_b"][i])

    def moe_call(xs):
        nc = _mk("route", _build_route)
        resA = _run(nc, [{"x": xs[c], "rT": rT} for c in range(8)])
        countsall = np.concatenate([r["counts"].reshape(1, 16) for r in resA],
                                   0).astype(np.float32)
        nc = _mk("moeB", _build_moeB)
        resB = _run(nc, [{"x": xs[c], "route": resA[c]["route"],
                          "countsall": countsall, "coremask": coremasks[c],
                          "w1f": w1f, "w2f": w2f} for c in range(8)])
        return [r["y"] for r in resB]

    def rwkv4_call(xs):
        nc = _mk("rwkv4", _build_rwkv4)
        res = _run(nc, [dict(r4w, x=xs[c]) for c in range(8)])
        return [r["y"] for r in res]

    def lat6_call(xs):
        nc = _mk("lat6", _build_lat6)
        res = _run(nc, [dict(lat_ins, x=xs[c]) for c in range(8)])
        return [r["o"] for r in res]

    nc = _mk("route", _build_route)
    resA = _run(nc, [{"x": xs[c], "rT": rT} for c in range(8)])
    countsall = np.concatenate([r["counts"].reshape(1, 16) for r in resA],
                               0).astype(np.float32)
    nc = _mk("mid", _build_mid)
    mid_ins = {("r7" + kk): vv for kk, vv in r7w.items()}
    res = _run(nc, [dict(mid_ins, x=xs[c], route=resA[c]["route"],
                         countsall=countsall, coremask=coremasks[c],
                         w1f=w1f, w2f=w2f, rT=rT) for c in range(8)])
    xs = [r["x2"] for r in res]
    countsall2 = np.concatenate([r["counts2"].reshape(1, 16) for r in res],
                                0).astype(np.float32)
    routes2 = [r["route2"] for r in res]
    nc = _mk("moeB", _build_moeB)
    res = _run(nc, [{"x": xs[c], "route": routes2[c], "countsall": countsall2,
                     "coremask": coremasks[c], "w1f": w1f, "w2f": w2f}
                    for c in range(8)])
    xs = [r["y"] for r in res]
    xs = lat6_call(xs)
    xs = rwkv4_call(xs)
    xs = lat6_call(xs)
    return np.stack(xs, 0)

